# version 01
# baseline (speedup 1.0000x reference)
import numpy as np
from contextlib import ExitStack

import concourse.bass as bass
import concourse.mybir as mybir
from concourse.bass_utils import run_bass_kernel_spmd

F32 = mybir.dt.float32
AF = mybir.ActivationFunctionType
ALU = mybir.AluOpType
AX = mybir.AxisListType

D, H, E = 1024, 16, 64
T, NCH, KC = 512, 4, 8
EPS = 1e-5

_AR = {}
_off = 0
def _reg(name, cols):
    global _off
    _AR[name] = (_off, cols)
    _off += (cols + 15) // 16 * 16
for _n, _c in [
    ("R0", 4096), ("R1", 4096), ("R2", 4096), ("R3", 4096), ("R4", 4096),
    ("R5", 5200), ("R9", 4096), ("R10", 4096), ("R11", 4096),
    ("ring0", 4096), ("ring1", 4096),
    ("kprev", 1024), ("pT", 512), ("ATm", 256), ("kT", 128),
    ("SZ", 520), ("sqdump", 1024), ("stats", 32),
    ("dens", 16),
    ("brows", 2048), ("allones", 128), ("cid", 192), ("triu", 128),
    ("mb", 256), ("mf", 256),
    ("bqk_a", 16), ("bqk_s", 16), ("bp_a", 8), ("bo_s", 8), ("b1mt", 32),
    ("g1t", 8), ("b1t", 8), ("gmt", 8), ("bmt", 8),
]:
    _reg(_n, _c)
ARENA_COLS = _off
assert ARENA_COLS <= 53200, ARENA_COLS


def rkeys(region, n=8):
    return tuple(f"{region}:{i}" for i in range(n))


class KB:
    ENGS = ("pe", "dve", "act", "sp", "gp")
    DMAQ = ("sp", "gp")

    def __init__(self):
        self.ops = {n: [] for n in self.ENGS}
        self.cnt = {n: 0 for n in self.ENGS}
        self.waited = {c: {p: 0 for p in self.ENGS} for c in self.ENGS}
        self.reg = {}
        self.pend = {n: [] for n in self.ENGS}

    def _need(self, eng, waits, selfwaits, p, v):
        if v is None:
            raise RuntimeError("dependency on an unsignalled op")
        if p == eng:
            if eng == "pe":
                return
            if v > self.waited[eng][eng]:
                selfwaits.append(v)
                self.waited[eng][eng] = v
            return
        if v > self.waited[eng][p]:
            waits.append((p, v))
            self.waited[eng][p] = v

    def op(self, eng, fn, reads=(), writes=(), sreads=(), pre=(), sig=True):
        waits, selfwaits = [], []
        for k in tuple(reads) + tuple(sreads):
            r = self.reg.get(k)
            if r and r["w"] is not None:
                self._need(eng, waits, selfwaits, *r["w"])
        for k in tuple(writes) + tuple(pre):
            r = self.reg.get(k)
            if r:
                if r["w"] is not None:
                    self._need(eng, waits, selfwaits, *r["w"])
                for p, v in r["r"].items():
                    self._need(eng, waits, selfwaits, p, v)
        if sig:
            self.cnt[eng] += 1
            val = self.cnt[eng]
            for k in self.pend[eng]:
                rr = self.reg.setdefault(k, {"w": None, "r": {}})
                rr["r"][eng] = val
            self.pend[eng].clear()
            for k in tuple(reads) + tuple(sreads):
                rr = self.reg.setdefault(k, {"w": None, "r": {}})
                rr["r"][eng] = val
        else:
            val = None
            for k in tuple(reads) + tuple(sreads):
                self.pend[eng].append(k)
        for k in writes:
            if not sig:
                raise RuntimeError("writes require sig=True")
            self.reg[k] = {"w": (eng, val), "r": {}}
        self.ops[eng].append((fn, waits, selfwaits, sig))


def build_nc(L, taps=()):
    NG = L // T
    nc = bass.Bass("TRN2", target_bir_lowering=False, debug=False)

    dr = {}
    dr["x"] = nc.dram_tensor("x", [L, D], F32, kind="ExternalInput").ap()
    for n, shp in [
        ("Wqkv_a", [D, 3 * D]), ("Wp_a", [D, D]), ("Wqkv_s", [D, 3 * D]),
        ("Wo_s", [D, D]), ("Wo", [D, D]), ("W1", [D, 4 * D]), ("W2", [4 * D, D]),
        ("th", [D, D]), ("thn", [D, D]), ("ph", [D, D]), ("phn", [D, D]),
        ("cid", [128, 192]), ("triu", [128, 128]), ("mb", [128, 256]),
        ("mf", [128, 256]), ("bqk_a", [128, 16]), ("bqk_s", [128, 16]),
        ("bp_a", [128, 8]), ("bo_s", [128, 8]), ("b1mt", [128, 32]),
        ("g1t", [128, 8]), ("b1t", [128, 8]), ("gmt", [128, 8]), ("bmt", [128, 8]),
        ("brows", [4, 1024]),
    ]:
        dr[n] = nc.dram_tensor(n, shp, F32, kind="ExternalInput").ap()
    dr["out"] = nc.dram_tensor("out", [L, D], F32, kind="ExternalOutput").ap()
    dr["cscr"] = nc.dram_tensor("cscr", [D, D], F32, kind="Internal").ap()
    dr["sscr"] = nc.dram_tensor("sscr", [D, D], F32, kind="Internal").ap()
    tap_dr = {}
    for tname, tcols in taps:
        tap_dr[tname] = nc.dram_tensor("tap_" + tname, [128, tcols], F32,
                                       kind="ExternalOutput").ap()

    st = ExitStack()
    arena = st.enter_context(nc.sbuf_tensor("arena", [128, ARENA_COLS], F32)).ap()
    A = {n: arena[:, o:o + c] for n, (o, c) in _AR.items()}
    A["eluscr"] = A["sqdump"][:, 0:512]
    A["pexp"] = A["sqdump"][:, 0:512]
    SCRK = ("sqd", "elusc", "pexp0", "pexp1")
    def scr_pre(own):
        return tuple(k for k in SCRK if k != own)
    PS = [st.enter_context(nc.psum_tensor(f"pb{i}", [128, 512], F32)).ap()
          for i in range(8)]

    kb = KB()

    xr = dr["x"].rearrange("(g c p) d -> g p c d", c=NCH, p=128)
    outr = dr["out"].rearrange("(g c p) d -> g p c d", c=NCH, p=128)
    wv = {}
    for n in ("Wqkv_a", "Wqkv_s", "Wp_a", "Wo_s", "Wo", "W1"):
        wv[n] = dr[n].rearrange("(kc p) n -> p kc n", p=128)
    wv["W2"] = dr["W2"].rearrange("(kg kc p) n -> kg p kc n", kg=4, p=128)
    wv["cscr"] = dr["cscr"].rearrange("(kc p) n -> p kc n", p=128)
    wv["sscr"] = dr["sscr"].rearrange("(kc p) n -> p kc n", p=128)

    ring = [A["ring0"], A["ring1"]]
    ring_i = [0]

    def load_piece(src_ap, ncols=512, reads=()):
        i = ring_i[0] % 2
        ring_i[0] += 1
        buf = ring[i]
        key = f"ring{i}"
        dst = buf.rearrange("p (kc n) -> p kc n", n=ncols)
        kb.op("sp", lambda e, d=dst, s=src_ap: e.dma_start(out=d, in_=s),
              reads=reads, writes=(key,))
        return buf, key

    pj = [0]

    def next_bank():
        i = pj[0] % 2
        pj[0] += 1
        return PS[i], f"P{i}"

    for n in ("cid", "triu", "mb", "mf", "bqk_a", "bqk_s", "bp_a", "bo_s",
              "b1mt", "g1t", "b1t", "gmt", "bmt"):
        kb.op("sp", lambda e, d=A[n], s=dr[n]: e.dma_start(out=d, in_=s),
              writes=(n,))
    _BR = {0: (0, 0), 1: (32, 0), 2: (64, 0), 3: (0, 1024)}
    for r in range(4):
        bp, bc = _BR[r]
        kb.op("sp", lambda e, d=A["brows"][bp:bp + 1, bc:bc + 1024],
              s=dr["brows"][r:r + 1, :]: e.dma_start(out=d, in_=s),
              writes=(f"brows{r}",))
    kb.op("dve", lambda e: e.memset(A["allones"], 1.0), writes=("allones",))
    kb.op("dve", lambda e: e.memset(A["SZ"], 0.0),
          writes=tuple(f"SZ:{h}" for h in range(H)))
    kb.op("dve", lambda e: e.memset(A["kprev"], 0.0), writes=("kprev",))
    kb.op("dve", lambda e: e.memset(A["R5"], 0.0),
          writes=tuple(f"vex:{s}" for s in range(5)))
    vext_ones = A["R5"].rearrange("p (g e) -> p g e", e=65)[:, :, 64:65]
    kb.op("dve", lambda e: e.memset(vext_ones, 1.0), writes=("R5ones",))
    kb.op("dve", lambda e: e.tensor_scalar(A["bqk_a"][:, 0:8], A["bqk_a"][:, 0:8],
                                           0.125, None, ALU.mult),
          reads=("bqk_a",), writes=("bqk_a",))

    spanA = arena[:, _AR["R0"][0]:_AR["R0"][0] + 8192]
    spanB = arena[:, _AR["R2"][0]:_AR["R2"][0] + 8192]
    kA = ("R0",) + rkeys("R1")
    kB = rkeys("R2") + rkeys("R3")
    for src, nsrc, scr, iscos in (("th", "thn", "cscr", True),
                                  ("ph", "phn", "sscr", False)):
        srcA = dr[src].rearrange("(kc p) n -> p kc n", p=128)
        srcB = dr[nsrc].rearrange("(kc p) n -> p kc n", p=128)
        dstA = spanA.rearrange("p (kc n) -> p kc n", n=1024)
        dstB = spanB.rearrange("p (kc n) -> p kc n", n=1024)
        kb.op("sp", lambda e, d=dstA, s=srcA: e.dma_start(out=d, in_=s), writes=kA)
        kb.op("sp", lambda e, d=dstB, s=srcB: e.dma_start(out=d, in_=s), writes=kB)
        kb.op("dve", lambda e: e.tensor_add(spanA, spanA, spanB),
              reads=kA + kB, writes=kA)
        if iscos:
            kb.op("act", lambda e: e.activation(spanB, spanA, AF.Sin, scale=0.5),
                  reads=kA, writes=kB)
            kb.op("act", lambda e: e.activation(spanA, spanB, AF.Square),
                  reads=kB, writes=kA)
            kb.op("dve", lambda e: e.tensor_scalar(spanA, spanA, -2.0, 1.0,
                                                   ALU.mult, ALU.add),
                  reads=kA, writes=kA)
        else:
            kb.op("act", lambda e: e.activation(spanA, spanA, AF.Sin, scale=1.0),
                  reads=kA, writes=kA)
        dstv = dr[scr].rearrange("(kc p) n -> p kc n", p=128)
        srcv = spanA.rearrange("p (kc n) -> p kc n", n=1024)
        kb.op("gp", lambda e, d=dstv, s=srcv: e.dma_start(out=d, in_=s),
              reads=kA, writes=(scr,))

    def x_load(g):
        dst = A["R0"].rearrange("p (c d) -> p c d", d=D)
        kb.op("gp", lambda e, d=dst, s=xr[g]: e.dma_start(out=d, in_=s),
              writes=("R0",))

    stats = A["stats"]
    st_m, st_r = stats[:, 8:12], stats[:, 12:16]

    def ln_stats(src_keys, src_slab):
        sums, ssq = stats[:, 0:4], stats[:, 4:8]
        scr1, scr2 = stats[:, 16:20], stats[:, 20:24]
        for c in range(NCH):
            xc = src_slab[:, c * D:(c + 1) * D]
            kb.op("act", lambda e, xc=xc, c=c: e.activation(
                A["sqdump"], xc, AF.Square, accum_out=ssq[:, c:c + 1]),
                reads=src_keys,
                pre=(("st_ssq",) if c == 0 else ()) + ("sqd",) + scr_pre("sqd"),
                writes=(("st_ssq",) if c == 3 else ()) + ("sqd",), sig=True)
        kb.op("dve", lambda e: e.reduce_sum(
            sums, src_slab.rearrange("p (c d) -> p c d", d=D), axis=AX.X),
            reads=src_keys, pre=("st_sum",), writes=("st_sum",))
        kb.op("dve", lambda e: e.tensor_scalar(st_m, sums, 1.0 / D, None, ALU.mult),
              reads=("st_sum",), writes=("st_m",))
        kb.op("dve", lambda e: e.tensor_scalar(scr1, ssq, 1.0 / D, EPS,
                                               ALU.mult, ALU.add),
              reads=("st_ssq",), writes=("st_s1",))
        kb.op("act", lambda e: e.activation(scr2, st_m, AF.Square),
              reads=("st_m",), writes=("st_s2",))
        kb.op("dve", lambda e: e.tensor_sub(scr1, scr1, scr2),
              reads=("st_s1", "st_s2"), writes=("st_s1",))
        kb.op("act", lambda e: e.activation(scr2, scr1, AF.Ln),
              reads=("st_s1",), writes=("st_s2",))
        kb.op("act", lambda e: e.activation(st_r, scr2, AF.Exp, scale=-0.5),
              reads=("st_s2",), writes=("st_r",))

    def ln_norm(src_keys, src_slab, dst_region, dst_slab):
        for c in range(NCH):
            xc = src_slab[:, c * D:(c + 1) * D]
            oc = dst_slab[:, c * D:(c + 1) * D]
            wk = (f"{dst_region}:{2 * c}", f"{dst_region}:{2 * c + 1}") \
                if dst_region != "R0" else ("R0",)
            kb.op("dve", lambda e, xc=xc, oc=oc, c=c: e.tensor_scalar(
                oc, xc, st_m[:, c:c + 1], st_r[:, c:c + 1], ALU.subtract, ALU.mult),
                reads=src_keys, sreads=("st_m", "st_r"), writes=wk)

    def transpose_slab(src_keys, src_slab, dst_region, dst_slab, affine=None):
        for k in range(KC):
            bank = PS[6]
            for c in range(NCH):
                src = src_slab[:, c * D + k * 128: c * D + (k + 1) * 128]
                kb.op("pe", lambda e, s=src, c=c: e.transpose(
                    bank[:, c * 128:(c + 1) * 128], s, A["cid"][:, 0:128]),
                    reads=src_keys + ("cid",),
                    pre=("P6",) if c == 0 else (),
                    writes=("P6",) if c == 3 else (), sig=(c == 3))
            dst = dst_slab[:, k * T:(k + 1) * T]
            dk = (f"{dst_region}:{k}",)
            if affine is None:
                kb.op("act", lambda e, d=dst: e.activation(d, bank[:, 0:512],
                                                           AF.Copy),
                      reads=("P6",), writes=dk)
            else:
                gt, bt = affine
                kb.op("dve", lambda e, d=dst, k=k, gt=gt, bt=bt: e.tensor_scalar(
                    d, bank[:, 0:512], A[gt][:, k:k + 1], A[bt][:, k:k + 1],
                    ALU.mult, ALU.add),
                    reads=("P6", gt, bt), writes=dk)

    def modeA(w_piece_srcs, in_region, in_slab, n_chunks, evac):
        for j, src in enumerate(w_piece_srcs):
            buf, rk = load_piece(src)
            for nsub in range(4):
                nchunk = 4 * j + nsub
                if nchunk >= n_chunks:
                    break
                bank, bkey = next_bank()
                for k in range(KC):
                    stat = buf[:, k * T + nsub * 128: k * T + (nsub + 1) * 128]
                    mov = in_slab[:, k * T:(k + 1) * T]
                    kb.op("pe", lambda e, st_=stat, mv=mov, k=k, bank=bank:
                          e.matmul(bank, st_, mv, start=(k == 0),
                                   stop=(k == KC - 1)),
                          reads=(rk, f"{in_region}:{k}"),
                          pre=(bkey,) if k == 0 else (),
                          writes=(bkey,) if k == KC - 1 else (),
                          sig=(k == KC - 1))
                evac(nchunk, bank, bkey)

    def modeB(w_piece_srcs, in_region, in_slab, evac, bias_row=None,
              resid=(), wextra=()):
        for nt, src in enumerate(w_piece_srcs):
            buf, rk = load_piece(src, reads=wextra)
            banks = [(PS[i], f"P{i}") for i in range(4)]
            tail = ([("bias", bias_row)] if bias_row is not None else []) + \
                   [("resid", rv) for rv in resid]
            for k in range(KC):
                for ts in range(NCH):
                    bank, bkey = banks[ts]
                    stat = in_slab[:, k * T + ts * 128: k * T + (ts + 1) * 128]
                    mov = buf[:, k * 512:(k + 1) * 512]
                    last = (k == KC - 1) and not tail
                    kb.op("pe", lambda e, st_=stat, mv=mov, bank=bank, k=k,
                          last=last: e.matmul(bank, st_, mv, start=(k == 0),
                                              stop=last),
                          reads=(rk, f"{in_region}:{k}"),
                          pre=(bkey,) if k == 0 else (),
                          writes=(bkey,) if last else (), sig=last)
            for ts in range(NCH):
                bank, bkey = banks[ts]
                for i, (kind, val) in enumerate(tail):
                    last = (i == len(tail) - 1)
                    if kind == "bias":
                        r = val
                        bp, bc = _BR[r]
                        kb.op("pe", lambda e, bank=bank, bp=bp, bc=bc, nt=nt,
                              last=last: e.matmul(
                                  bank, A["allones"][bp:bp + 1, 0:128],
                                  A["brows"][bp:bp + 1,
                                             bc + nt * 512:bc + (nt + 1) * 512],
                                  start=False, stop=last),
                              reads=("allones", f"brows{r}"),
                              writes=(bkey,) if last else (), sig=last)
                    else:
                        slab, skeys = val
                        mv = slab[:, ts * D + nt * 512: ts * D + (nt + 1) * 512]
                        kb.op("pe", lambda e, bank=bank, mv=mv, last=last:
                              e.matmul(bank, A["cid"][:, 0:128], mv,
                                       start=False, stop=last),
                              reads=("cid",) + tuple(skeys),
                              writes=(bkey,) if last else (), sig=last)
                evac(ts, nt, bank, bkey)

    def tap(name, slab, cols, reads):
        if name in tap_dr:
            kb.op("gp", lambda e: e.dma_start(out=tap_dr[name],
                                              in_=slab[:, 0:cols]),
                  reads=reads)

    vexr = A["R5"].rearrange("p (s q) -> p s q", s=5)

    def emit_group(g):
        is_first = (g == 0)

        ln_stats(("R0",), A["R0"])
        ln_norm(("R0",), A["R0"], "R1", A["R1"])
        tap(f"xn{g}", A["R1"], 4096, rkeys("R1"))
        transpose_slab(rkeys("R1"), A["R1"], "R2", A["R2"], affine=("g1t", "b1t"))
        tap(f"xnT{g}", A["R2"], 4096, rkeys("R2"))

        def evac_qk(nchunk, bank, bkey):
            if nchunk < 8:
                dst, key, scale = (A["R3"][:, nchunk * T:(nchunk + 1) * T],
                                   f"R3:{nchunk}", 0.125)
            else:
                dst, key, scale = (A["R4"][:, (nchunk - 8) * T:(nchunk - 7) * T],
                                   f"R4:{nchunk - 8}", 1.0)
            kb.op("act", lambda e, d=dst, nchunk=nchunk, scale=scale:
                  e.activation(d, bank[:, 0:512], AF.Identity,
                               bias=A["bqk_a"][:, nchunk:nchunk + 1], scale=scale),
                  reads=(bkey, "bqk_a"), writes=(key,))
        modeA([wv["Wqkv_a"][:, :, j * 512:(j + 1) * 512] for j in range(4)],
              "R2", A["R2"], 16, evac_qk)

        def evac_v(ts, nt, bank, bkey):
            dst = A["R5"].rearrange("p (s h e) -> p s h e", s=5, e=65)[
                :, ts + 1, 8 * nt:8 * (nt + 1), 0:64]
            src = bank.rearrange("p (h e) -> p h e", e=64)[:, 0:8, :]
            kb.op("dve", lambda e, d=dst, s=src: e.tensor_copy(d, s),
                  reads=(bkey,), pre=(f"vex:{ts + 1}",),
                  writes=(f"vex:{ts + 1}",))
        modeB([wv["Wqkv_a"][:, :, 2048 + nt * 512: 2048 + (nt + 1) * 512]
               for nt in range(2)], "R2", A["R2"], evac_v, bias_row=0)

        for c in range(NCH):
            for h in range(H):
                j, base, hp = h // 2, (h % 2) * 64, h % 2
                sc, sck = PS[4 + hp], f"P{4 + hp}"
                q = A["R3"][base:base + 64, j * T + c * 128: j * T + (c + 1) * 128]
                if c == 0:
                    kp = A["kprev"][base:base + 64, j * 128:(j + 1) * 128]
                    kcur = A["R4"][base:base + 64, j * T: j * T + 128]
                    kb.op("pe", lambda e, q=q, kp=kp, sc=sc: e.matmul(
                        sc[:, 0:128], q, kp, start=True, stop=False),
                        reads=(f"R3:{j}", "kprev"), pre=(sck,), sig=False)
                    kb.op("pe", lambda e, q=q, kc_=kcur, sc=sc: e.matmul(
                        sc[:, 128:256], q, kc_, start=False, stop=False),
                        reads=(f"R3:{j}", f"R4:{j}"), sig=False)
                else:
                    kc_ = A["R4"][base:base + 64,
                                  j * T + (c - 1) * 128: j * T + (c + 1) * 128]
                    kb.op("pe", lambda e, q=q, kc_=kc_, sc=sc: e.matmul(
                        sc[:, 0:256], q, kc_, start=True, stop=False),
                        reads=(f"R3:{j}", f"R4:{j}"), pre=(sck,), sig=False)
                mkey = "mf" if (is_first and c == 0) else "mb"
                kb.op("pe", lambda e, m=A[mkey], sc=sc: e.matmul(
                    sc[:, 0:256], A["cid"][:, 0:128], m, start=False, stop=True),
                    reads=("cid", mkey), writes=(sck,))
                nm = A["dens"][:, 0:1]
                kb.op("dve", lambda e, sc=sc: e.reduce_max(
                    nm, sc[:, 0:256], axis=AX.X, negate=True),
                    reads=(sck,), writes=("nm",))
                px = A["pexp"][:, hp * 256:hp * 256 + 256]
                kb.op("act", lambda e, sc=sc, px=px: e.activation(
                    px, sc[:, 0:256], AF.Exp, bias=nm, scale=1.0),
                    reads=(sck, "nm"),
                    pre=(f"pexp{hp}",) + scr_pre(f"pexp{hp}"),
                    writes=(f"pexp{hp}",))
                tb = PS[6]
                kb.op("pe", lambda e, px=px, tb=tb: e.transpose(
                    tb[:, 0:128], px[:, 0:128], A["cid"][:, 0:128]),
                    reads=(f"pexp{hp}", "cid"), pre=("P6",), sig=False)
                kb.op("pe", lambda e, px=px, tb=tb: e.transpose(
                    tb[:, 128:256], px[:, 128:256], A["cid"][:, 0:128]),
                    reads=(f"pexp{hp}", "cid"), writes=("P6",))
                pt = A["pT"][:, hp * 256:hp * 256 + 256]
                kb.op("act", lambda e, tb=tb, pt=pt: e.activation(
                    pt, tb[:, 0:256], AF.Copy),
                    reads=("P6",), pre=(f"pT{hp}",), writes=(f"pT{hp}",))
                av = PS[7]
                vprev = vexr[:, c, h * 65:h * 65 + 65]
                vcur = vexr[:, c + 1, h * 65:h * 65 + 65]
                kb.op("pe", lambda e, pt=pt, v=vprev, av=av: e.matmul(
                    av[:, 0:65], pt[:, 0:128], v, start=True, stop=False),
                    reads=(f"pT{hp}", f"vex:{c}", "R5ones"), pre=("P7",),
                    sig=False)
                kb.op("pe", lambda e, pt=pt, v=vcur, av=av: e.matmul(
                    av[:, 0:65], pt[:, 128:256], v, start=False, stop=True),
                    reads=(f"pT{hp}", f"vex:{c + 1}", "R5ones"), writes=("P7",))
                dent, rden = A["dens"][:, 1:2], A["dens"][:, 2:3]
                kb.op("dve", lambda e, av=av: e.tensor_copy(dent, av[:, 64:65]),
                      reads=("P7",), writes=("dent",))
                kb.op("dve", lambda e: e.reciprocal(rden, dent),
                      reads=("dent",), writes=("rden",))
                dst = A["R1"][:, c * D + h * 64: c * D + (h + 1) * 64]
                rkey = f"R1:{2 * c + h // 8}"
                kb.op("dve", lambda e, av=av, dst=dst: e.tensor_scalar(
                    dst, av[:, 0:64], rden, None, ALU.mult),
                    reads=("P7",), sreads=("rden",),
                    pre=(rkey,) if h % 8 == 0 else (), writes=(rkey,))
        tap(f"swa{g}", A["R1"], 4096, rkeys("R1"))

        kpdst = A["kprev"].rearrange("p (j q) -> p j q", q=128)
        kpsrc = A["R4"].rearrange("p (j t) -> p j t", t=T)[:, :, 384:512]
        kb.op("dve", lambda e: e.tensor_copy(kpdst, kpsrc),
              reads=rkeys("R4"), writes=("kprev",))
        kb.op("dve", lambda e: e.tensor_copy(A["R5"][:, 0:1040],
                                             A["R5"][:, 4160:5200]),
              reads=("vex:4", "R5ones"), pre=("vex:0",), writes=("vex:0",))

        def evac_qk_s(nchunk, bank, bkey):
            isq = nchunk < 8
            jj = nchunk if isq else nchunk - 8
            slab = A["R3"] if isq else A["R4"]
            dst = slab[:, jj * T:(jj + 1) * T]
            key = f"R3:{jj}" if isq else f"R4:{jj}"
            scale = 0.125 if isq else 1.0
            scr = A["eluscr"]
            kb.op("dve", lambda e, d=dst, nchunk=nchunk, scale=scale:
                  e.tensor_scalar(d, bank[:, 0:512],
                                  A["bqk_s"][:, nchunk:nchunk + 1], scale,
                                  ALU.add, ALU.mult),
                  reads=(bkey, "bqk_s"), pre=(key,), writes=(key,))
            kb.op("act", lambda e, d=dst: e.activation(scr, d, AF.Exp),
                  reads=(key,), pre=("elusc",) + scr_pre("elusc"),
                  writes=("elusc",))
            kb.op("dve", lambda e: e.tensor_scalar(scr, scr, 1.0, 1e-4,
                                                   ALU.min, ALU.add),
                  reads=("elusc",), writes=("elusc",))
            kb.op("dve", lambda e, d=dst: e.scalar_tensor_tensor(
                d, d, 0.0, scr, ALU.max, ALU.add),
                reads=(key, "elusc"), writes=(key,))
        modeA([wv["Wqkv_s"][:, :, j * 512:(j + 1) * 512] for j in range(4)],
              "R2", A["R2"], 16, evac_qk_s)
        modeB([wv["Wqkv_s"][:, :, 2048 + nt * 512: 2048 + (nt + 1) * 512]
               for nt in range(2)], "R2", A["R2"], evac_v, bias_row=1)
        tap(f"qps{g}", A["R3"], 4096, rkeys("R3"))
        tap(f"kps{g}", A["R4"], 4096, rkeys("R4"))

        for c in range(NCH):
            for h in range(H):
                j, base, hp = h // 2, (h % 2) * 64, h % 2
                qs = A["R3"][base:base + 64, j * T + c * 128: j * T + (c + 1) * 128]
                ks = A["R4"][base:base + 64, j * T + c * 128: j * T + (c + 1) * 128]
                at, atk = PS[4 + hp], f"P{4 + hp}"
                kb.op("pe", lambda e, ks=ks, qs=qs, at=at: e.matmul(
                    at[:, 0:128], ks, qs, start=True, stop=True),
                    reads=(f"R3:{j}", f"R4:{j}"), pre=(atk,), writes=(atk,))
                atm = A["ATm"][:, hp * 128:hp * 128 + 128]
                kb.op("dve", lambda e, at=at, atm=atm: e.scalar_tensor_tensor(
                    atm, at[:, 0:128], 1.0, A["triu"], ALU.mult, ALU.mult),
                    reads=(atk, "triu"), pre=(f"ATm{hp}",), writes=(f"ATm{hp}",))
                idb = A["cid"][0:64, 0:64] if base == 0 else \
                    A["cid"][64:128, 128:192]
                tb = PS[6]
                kb.op("pe", lambda e, ks=ks, idb=idb, tb=tb: e.transpose(
                    tb[:, 0:64], ks, idb),
                    reads=(f"R4:{j}", "cid"), pre=("P6",), writes=("P6",))
                kt = A["kT"][:, hp * 64:hp * 64 + 64]
                kb.op("act", lambda e, kt=kt, tb=tb: e.activation(
                    kt, tb[:, 0:64], AF.Copy),
                    reads=("P6",), pre=(f"kT{hp}",), writes=(f"kT{hp}",))
                nd, ndk = PS[5 - hp], f"P{5 - hp}"
                sz = A["SZ"][base:base + 64, j * 65:(j + 1) * 65]
                vex = vexr[:, c + 1, h * 65:h * 65 + 65]
                kb.op("pe", lambda e, qs=qs, sz=sz, nd=nd: e.matmul(
                    nd[:, 0:65], qs, sz, start=True, stop=False),
                    reads=(f"R3:{j}", f"SZ:{h}"), pre=(ndk,), sig=False)
                kb.op("pe", lambda e, atm=atm, vex=vex, nd=nd: e.matmul(
                    nd[:, 0:65], atm, vex, start=False, stop=True),
                    reads=(f"ATm{hp}", f"vex:{c + 1}", "R5ones"), writes=(ndk,))
                dent, rden = A["dens"][:, 4:5], A["dens"][:, 5:6]
                kb.op("dve", lambda e, nd=nd: e.tensor_scalar(
                    dent, nd[:, 64:65], 1e-6, None, ALU.add),
                    reads=(ndk,), writes=("dent2",))
                kb.op("dve", lambda e: e.reciprocal(rden, dent),
                      reads=("dent2",), writes=("rden2",))
                dst = A["R10"][:, c * D + h * 64: c * D + (h + 1) * 64]
                rkey = f"R10:{2 * c + h // 8}"
                kb.op("dve", lambda e, nd=nd, dst=dst: e.tensor_scalar(
                    dst, nd[:, 0:64], rden, None, ALU.mult),
                    reads=(ndk,), sreads=("rden2",),
                    pre=(rkey,) if h % 8 == 0 else (), writes=(rkey,))
                szp = PS[7]
                tp = None if base == 0 else (0, 64)
                kb.op("pe", lambda e, kt=kt, vex=vex, szp=szp, base=base, tp=tp:
                      e.matmul(szp[base:base + 64, 0:65], kt, vex,
                               start=True, stop=True, tile_position=tp),
                      reads=(f"kT{hp}", f"vex:{c + 1}", "R5ones"),
                      pre=("P7",), writes=("P7",))
                kb.op("dve", lambda e, sz=sz, szp=szp, base=base: e.tensor_add(
                    sz, sz, szp[base:base + 64, 0:65]),
                    reads=("P7", f"SZ:{h}"), writes=(f"SZ:{h}",))
        tap(f"o{g}", A["R10"], 4096, rkeys("R10"))

        transpose_slab(rkeys("R1"), A["R1"], "R3", A["R3"])

        def evac_wp(nchunk, bank, bkey):
            dst = A["R9"][:, nchunk * T:(nchunk + 1) * T]
            kb.op("act", lambda e, d=dst, nchunk=nchunk: e.activation(
                d, bank[:, 0:512], AF.Identity,
                bias=A["bp_a"][:, nchunk:nchunk + 1]),
                reads=(bkey, "bp_a"), writes=(f"R9:{nchunk}",))
        modeA([wv["Wp_a"][:, :, j * 512:(j + 1) * 512] for j in range(2)],
              "R3", A["R3"], 8, evac_wp)

        transpose_slab(rkeys("R10"), A["R10"], "R4", A["R4"])

        def evac_wos(nchunk, bank, bkey):
            dst = A["R9"][:, nchunk * T:(nchunk + 1) * T]
            kb.op("dve", lambda e, d=dst, nchunk=nchunk: e.scalar_tensor_tensor(
                d, bank[:, 0:512], A["bo_s"][:, nchunk:nchunk + 1], d,
                ALU.add, ALU.add),
                reads=(bkey, "bo_s", f"R9:{nchunk}"), writes=(f"R9:{nchunk}",))
        modeA([wv["Wo_s"][:, :, j * 512:(j + 1) * 512] for j in range(2)],
              "R4", A["R4"], 8, evac_wos)

        def evac_wo(ts, nt, bank, bkey):
            dst = A["R10"][:, ts * D + nt * 512: ts * D + (nt + 1) * 512]
            kb.op("act", lambda e, d=dst, bank=bank: e.activation(
                d, bank, AF.Copy),
                reads=(bkey,), pre=(f"R10:{2 * ts + nt}",),
                writes=(f"R10:{2 * ts + nt}",))
        modeB([wv["Wo"][:, :, nt * 512:(nt + 1) * 512] for nt in range(2)],
              "R9", A["R9"], evac_wo, bias_row=2, resid=[(A["R0"], ("R0",))])
        tap(f"x1_{g}", A["R10"], 4096, rkeys("R10"))

        ln_stats(rkeys("R10"), A["R10"])
        ln_norm(rkeys("R10"), A["R10"], "R1", A["R1"])
        tap(f"h{g}", A["R1"], 4096, rkeys("R1"))
        transpose_slab(rkeys("R1"), A["R1"], "R2", A["R2"], affine=("gmt", "bmt"))
        tap(f"hT{g}", A["R2"], 4096, rkeys("R2"))

        gel_reg = ["R3", "R4", "R9", "R1"]
        gel_slab = [A["R3"], A["R4"], A["R9"], A["R1"]]

        def evac_w1(nchunk, bank, bkey):
            regn = gel_reg[nchunk // 8]
            dst = gel_slab[nchunk // 8][:, (nchunk % 8) * T:((nchunk % 8) + 1) * T]
            kb.op("act", lambda e, d=dst, nchunk=nchunk: e.activation(
                d, bank[:, 0:512], AF.Gelu,
                bias=A["b1mt"][:, nchunk:nchunk + 1]),
                reads=(bkey, "b1mt"), pre=(f"{regn}:{nchunk % 8}",),
                writes=(f"{regn}:{nchunk % 8}",))
        modeA([wv["W1"][:, :, j * 512:(j + 1) * 512] for j in range(8)],
              "R2", A["R2"], 32, evac_w1)
        tap(f"gela{g}", A["R3"], 4096, rkeys("R3"))
        tap(f"gelb{g}", A["R4"], 4096, rkeys("R4"))
        tap(f"gelc{g}", A["R9"], 4096, rkeys("R9"))
        tap(f"geld{g}", A["R1"], 4096, rkeys("R1"))

        for nt in range(2):
            banks = [(PS[i], f"P{i}") for i in range(4)]
            for kg in range(4):
                buf, rk = load_piece(wv["W2"][kg][:, :, nt * 512:(nt + 1) * 512])
                for k in range(KC):
                    kk = kg * 8 + k
                    regn = gel_reg[kk // 8]
                    slab = gel_slab[kk // 8]
                    for ts in range(NCH):
                        bank, bkey = banks[ts]
                        stat = slab[:, (kk % 8) * T + ts * 128:
                                    (kk % 8) * T + (ts + 1) * 128]
                        mov = buf[:, k * 512:(k + 1) * 512]
                        kb.op("pe", lambda e, st_=stat, mv=mov, bank=bank,
                              first=(kg == 0 and k == 0): e.matmul(
                                  bank, st_, mv, start=first, stop=False),
                              reads=(rk, f"{regn}:{kk % 8}"),
                              pre=(bkey,) if (kg == 0 and k == 0) else (),
                              sig=(k == KC - 1 and ts == NCH - 1))
            for ts in range(NCH):
                bank, bkey = banks[ts]
                kb.op("pe", lambda e, bank=bank, nt=nt: e.matmul(
                    bank, A["allones"][0:1, 0:128],
                    A["brows"][0:1, 1024 + nt * 512:1024 + (nt + 1) * 512],
                    start=False, stop=False),
                    reads=("allones", "brows3"), sig=False)
                mvx1 = A["R10"][:, ts * D + nt * 512: ts * D + (nt + 1) * 512]
                kb.op("pe", lambda e, bank=bank, mv=mvx1: e.matmul(
                    bank, A["cid"][:, 0:128], mv, start=False, stop=False),
                    reads=("cid", f"R10:{2 * ts + nt}"), sig=False)
                mvx = A["R0"][:, ts * D + nt * 512: ts * D + (nt + 1) * 512]
                kb.op("pe", lambda e, bank=bank, mv=mvx: e.matmul(
                    bank, A["cid"][:, 0:128], mv, start=False, stop=True),
                    reads=("cid", "R0"), writes=(bkey,))
                dst = A["R11"][:, ts * D + nt * 512: ts * D + (nt + 1) * 512]
                kb.op("act", lambda e, d=dst, bank=bank: e.activation(
                    d, bank, AF.Copy),
                    reads=(bkey,), pre=(f"R11:{2 * ts + nt}",),
                    writes=(f"R11:{2 * ts + nt}",))
        tap(f"t{g}", A["R11"], 4096, rkeys("R11"))

        ln_stats(rkeys("R11"), A["R11"])
        ln_norm(rkeys("R11"), A["R11"], "R1", A["R1"])
        transpose_slab(rkeys("R1"), A["R1"], "R2", A["R2"])

        def evac_re(ts, nt, bank, bkey):
            dst = A["R10"][:, ts * D + nt * 512: ts * D + (nt + 1) * 512]
            kb.op("act", lambda e, d=dst, bank=bank: e.activation(
                d, bank, AF.Square),
                reads=(bkey,), pre=(f"R10:{2 * ts + nt}",),
                writes=(f"R10:{2 * ts + nt}",))
        def evac_im(ts, nt, bank, bkey):
            dst = A["R9"][:, ts * D + nt * 512: ts * D + (nt + 1) * 512]
            kb.op("act", lambda e, d=dst, bank=bank: e.activation(
                d, bank, AF.Square),
                reads=(bkey,), pre=(f"R9:{2 * ts + nt}",),
                writes=(f"R9:{2 * ts + nt}",))
        modeB([wv["cscr"][:, :, nt * 512:(nt + 1) * 512] for nt in range(2)],
              "R2", A["R2"], evac_re, wextra=("cscr",))
        modeB([wv["sscr"][:, :, nt * 512:(nt + 1) * 512] for nt in range(2)],
              "R2", A["R2"], evac_im, wextra=("sscr",))
        kb.op("dve", lambda e: e.tensor_add(A["R3"][:, 0:4096], A["R10"],
                                            A["R9"]),
              reads=rkeys("R10") + rkeys("R9"), pre=rkeys("R3"),
              writes=rkeys("R3"))
        kb.op("act", lambda e: e.activation(A["R10"], A["R3"][:, 0:4096],
                                            AF.Sqrt),
              reads=rkeys("R3"), pre=rkeys("R10"), writes=rkeys("R10"))
        kb.op("dve", lambda e: e.tensor_add(A["R9"], A["R1"], A["R10"]),
              reads=rkeys("R10") + rkeys("R1"), pre=rkeys("R9"),
              writes=rkeys("R9"))
        ln_stats(rkeys("R9"), A["R9"])
        ln_norm(rkeys("R9"), A["R9"], "R0", A["R0"])
        src = A["R0"].rearrange("p (c d) -> p c d", d=D)
        kb.op("gp", lambda e, g=g: e.dma_start(out=outr[g], in_=src),
              reads=("R0",))

    x_load(0)
    for g in range(NG):
        emit_group(g)
        if g + 1 < NG:
            x_load(g + 1)

    total_gp = kb.cnt["gp"]

    with ExitStack() as sst:
        sems = {n: sst.enter_context(nc.semaphore(f"sem_{n}")) for n in KB.ENGS}

        def make_body(eng_name):
            oplist = kb.ops[eng_name]
            mysem = sems[eng_name]
            mymul = 16 if eng_name in KB.DMAQ else 1

            def body(e):
                for fn, waits, selfwaits, sig in oplist:
                    for p, v in waits:
                        e.wait_ge(sems[p], v * (16 if p in KB.DMAQ else 1))
                    for v in selfwaits:
                        e.wait_ge(mysem, v * mymul)
                    inst = fn(e)
                    if sig:
                        inst.then_inc(mysem, mymul)
                if eng_name == "gp":
                    e.wait_ge(mysem, total_gp * 16)
            return body

        with nc.Block() as blk:
            blk.tensor(make_body("pe"))
            blk.vector(make_body("dve"))
            blk.scalar(make_body("act"))
            blk.sync(make_body("sp"))
            blk.gpsimd(make_body("gp"))
    st.close()
    return nc


def make_const_inputs():
    cid = np.zeros((128, 192), np.float32)
    cid[:, 0:128] = np.eye(128, dtype=np.float32)
    cid[64:128, 128:192] = np.eye(64, dtype=np.float32)
    triu = np.triu(np.ones((128, 128), np.float32))
    r = np.arange(128)[:, None]
    s = np.arange(256)[None, :]
    band = (s >= r + 1) & (s <= r + 128)
    first = band & (s >= 128)
    mb = np.where(band, 0.0, -1e30).astype(np.float32)
    mf = np.where(first, 0.0, -1e30).astype(np.float32)
    return cid, triu, mb, mf


def shard_inputs(inputs, b, L):
    cid, triu, mb, mf = make_const_inputs()
    g = lambda n: np.ascontiguousarray(np.asarray(inputs[n], dtype=np.float32))
    bq_a, bq_s = g("L0_attn_bqkv_a"), g("L0_attn_bqkv_s")
    brows = np.zeros((4, 1024), np.float32)
    brows[0] = bq_a[2048:3072]
    brows[1] = bq_s[2048:3072]
    brows[2] = g("L0_attn_bo")
    brows[3] = g("L0_attn_b2m")
    return {
        "x": np.ascontiguousarray(np.asarray(inputs["L0_attn_x"],
                                             dtype=np.float32)[b, :L]),
        "Wqkv_a": g("L0_attn_Wqkv_a"), "Wp_a": g("L0_attn_Wp_a"),
        "Wqkv_s": g("L0_attn_Wqkv_s"), "Wo_s": g("L0_attn_Wo_s"),
        "Wo": g("L0_attn_Wo"), "W1": g("L0_attn_W1"), "W2": g("L0_attn_W2"),
        "th": g("L0_ffn_theta"), "thn": g("L0_ffn_theta_noise"),
        "ph": g("L0_ffn_phi"), "phn": g("L0_ffn_phi_noise"),
        "cid": cid, "triu": triu, "mb": mb, "mf": mf,
        "bqk_a": np.ascontiguousarray(bq_a[0:2048].reshape(16, 128).T),
        "bqk_s": np.ascontiguousarray(bq_s[0:2048].reshape(16, 128).T),
        "bp_a": np.ascontiguousarray(g("L0_attn_bp_a").reshape(8, 128).T),
        "bo_s": np.ascontiguousarray(g("L0_attn_bo_s").reshape(8, 128).T),
        "b1mt": np.ascontiguousarray(g("L0_attn_b1m").reshape(32, 128).T),
        "g1t": np.ascontiguousarray(g("L0_attn_g1").reshape(8, 128).T),
        "b1t": np.ascontiguousarray(g("L0_attn_b1").reshape(8, 128).T),
        "gmt": np.ascontiguousarray(g("L0_attn_gm").reshape(8, 128).T),
        "bmt": np.ascontiguousarray(g("L0_attn_bm").reshape(8, 128).T),
        "brows": brows,
    }


def kernel(**inputs):
    L = 4096
    nc = build_nc(L)
    in_maps = [shard_inputs(inputs, b, L) for b in range(8)]
    res = run_bass_kernel_spmd(nc, in_maps, core_ids=list(range(8)))
    out = np.stack([np.asarray(r["out"]).reshape(L, D) for r in res.results])
    return out.astype(np.float32)


# version 02
# speedup vs baseline: 1.0456x; 1.0456x over previous
import numpy as np
from contextlib import ExitStack

import concourse.bass as bass
import concourse.mybir as mybir
from concourse.bass_utils import run_bass_kernel_spmd

F32 = mybir.dt.float32
BF16 = mybir.dt.bfloat16
AF = mybir.ActivationFunctionType
ALU = mybir.AluOpType
AX = mybir.AxisListType

D, H, E = 1024, 16, 64
T, NCH, KC = 512, 4, 8
EPS = 1e-5

_AR = {}
_off = 0
def _reg(name, cols):
    global _off
    _AR[name] = (_off, cols)
    _off += (cols + 15) // 16 * 16
for _n, _c in [
    ("R0", 4096), ("R1", 4096), ("R2", 4096), ("R3", 4096), ("R4", 4096),
    ("R5", 5200), ("R9", 4096), ("R10", 4096), ("R11", 4096),
    ("ring0", 4096), ("ring1", 4096),
    ("kprev", 1024), ("pT", 512), ("ATm", 256), ("kT", 128),
    ("SZ", 520), ("sqdump", 1024), ("stats", 32),
    ("dens", 16),
    ("brows", 2048), ("allones", 128), ("cid", 192), ("triu", 128),
    ("mb", 256), ("mf", 256),
    ("bqk_a", 16), ("bqk_s", 16), ("bp_a", 8), ("bo_s", 8), ("b1mt", 32),
    ("g1t", 8), ("b1t", 8), ("gmt", 8), ("bmt", 8),
]:
    _reg(_n, _c)
ARENA_COLS = _off
assert ARENA_COLS <= 53200, ARENA_COLS


def rkeys(region, n=8):
    return tuple(f"{region}:{i}" for i in range(n))


class KB:
    ENGS = ("pe", "dve", "act", "sp", "gp")
    DMAQ = ("sp", "gp")

    def __init__(self):
        self.ops = {n: [] for n in self.ENGS}
        self.cnt = {n: 0 for n in self.ENGS}
        self.waited = {c: {p: 0 for p in self.ENGS} for c in self.ENGS}
        self.reg = {}
        self.pend = {n: [] for n in self.ENGS}

    def _need(self, eng, waits, selfwaits, p, v):
        if v is None:
            raise RuntimeError("dependency on an unsignalled op")
        if p == eng:
            if eng == "pe":
                return
            if v > self.waited[eng][eng]:
                selfwaits.append(v)
                self.waited[eng][eng] = v
            return
        if v > self.waited[eng][p]:
            waits.append((p, v))
            self.waited[eng][p] = v

    def op(self, eng, fn, reads=(), writes=(), sreads=(), pre=(), sig=True):
        waits, selfwaits = [], []
        for k in tuple(reads) + tuple(sreads):
            r = self.reg.get(k)
            if r and r["w"] is not None:
                self._need(eng, waits, selfwaits, *r["w"])
        for k in tuple(writes) + tuple(pre):
            r = self.reg.get(k)
            if r:
                if r["w"] is not None:
                    self._need(eng, waits, selfwaits, *r["w"])
                for p, v in r["r"].items():
                    self._need(eng, waits, selfwaits, p, v)
        if sig:
            self.cnt[eng] += 1
            val = self.cnt[eng]
            for k in self.pend[eng]:
                rr = self.reg.setdefault(k, {"w": None, "r": {}})
                rr["r"][eng] = val
            self.pend[eng].clear()
            for k in tuple(reads) + tuple(sreads):
                rr = self.reg.setdefault(k, {"w": None, "r": {}})
                rr["r"][eng] = val
        else:
            val = None
            for k in tuple(reads) + tuple(sreads):
                self.pend[eng].append(k)
        for k in writes:
            if not sig:
                raise RuntimeError("writes require sig=True")
            self.reg[k] = {"w": (eng, val), "r": {}}
        self.ops[eng].append((fn, waits, selfwaits, sig))


def build_nc(L, taps=()):
    NG = L // T
    nc = bass.Bass("TRN2", target_bir_lowering=False, debug=False)

    dr = {}
    dr["x"] = nc.dram_tensor("x", [L, D], F32, kind="ExternalInput").ap()
    for n, shp in [
        ("Wqkv_a", [D, 3 * D]), ("Wp_a", [D, D]), ("Wqkv_s", [D, 3 * D]),
        ("Wo_s", [D, D]), ("Wo", [D, D]), ("W1", [D, 4 * D]), ("W2", [4 * D, D]),
        ("th", [D, D]), ("thn", [D, D]), ("ph", [D, D]), ("phn", [D, D]),
        ("cid", [128, 192]), ("triu", [128, 128]), ("mb", [128, 256]),
        ("mf", [128, 256]), ("bqk_a", [128, 16]), ("bqk_s", [128, 16]),
        ("bp_a", [128, 8]), ("bo_s", [128, 8]), ("b1mt", [128, 32]),
        ("g1t", [128, 8]), ("b1t", [128, 8]), ("gmt", [128, 8]), ("bmt", [128, 8]),
        ("brows", [4, 1024]),
    ]:
        dr[n] = nc.dram_tensor(n, shp, F32, kind="ExternalInput").ap()
    dr["out"] = nc.dram_tensor("out", [L, D], F32, kind="ExternalOutput").ap()
    for w, shp in [("Wqkv_a", [D, 3 * D]), ("Wqkv_s", [D, 3 * D]),
                   ("Wp_a", [D, D]), ("Wo_s", [D, D]), ("Wo", [D, D]),
                   ("W1", [D, 4 * D]), ("W2", [4 * D, D]),
                   ("csc", [D, D]), ("ssc", [D, D])]:
        dr[w + "_hi"] = nc.dram_tensor(w + "_hi", shp, BF16, kind="Internal").ap()
        dr[w + "_lo"] = nc.dram_tensor(w + "_lo", shp, BF16, kind="Internal").ap()
    tap_dr = {}
    for tname, tcols in taps:
        tap_dr[tname] = nc.dram_tensor("tap_" + tname, [128, tcols], F32,
                                       kind="ExternalOutput").ap()

    st = ExitStack()
    arena = st.enter_context(nc.sbuf_tensor("arena", [128, ARENA_COLS], F32)).ap()
    A = {n: arena[:, o:o + c] for n, (o, c) in _AR.items()}
    A["eluscr"] = A["sqdump"][:, 0:512]
    A["pexp"] = A["sqdump"][:, 0:512]
    SCRK = ("sqd", "elusc", "pexp0", "pexp1", "aff", "g32a", "g32b")
    def scr_pre(own):
        return tuple(k for k in SCRK if k != own)
    PS = [st.enter_context(nc.psum_tensor(f"pb{i}", [128, 512], F32)).ap()
          for i in range(8)]

    kb = KB()

    xr = dr["x"].rearrange("(g c p) d -> g p c d", c=NCH, p=128)
    outr = dr["out"].rearrange("(g c p) d -> g p c d", c=NCH, p=128)
    wv = {}
    for n in ("Wqkv_a", "Wqkv_s", "Wp_a", "Wo_s", "Wo", "W1"):
        wv[n] = dr[n].rearrange("(kc p) n -> p kc n", p=128)
    wv["W2"] = dr["W2"].rearrange("(kg kc p) n -> kg p kc n", kg=4, p=128)
    for w in ("Wqkv_a", "Wqkv_s", "Wp_a", "Wo_s", "Wo", "W1", "csc", "ssc"):
        for h in ("_hi", "_lo"):
            wv[w + h] = dr[w + h].rearrange("(kc p) n -> p kc n", p=128)
    for h in ("_hi", "_lo"):
        wv["W2" + h] = dr["W2" + h].rearrange("(kg kc p) n -> kg p kc n",
                                              kg=4, p=128)

    def phi(region, k):
        bc = A[region].bitcast(BF16)
        return bc[:, k * 512:(k + 1) * 512]

    def plo(region, k):
        bc = A[region].bitcast(BF16)
        return bc[:, 4096 + k * 512: 4096 + (k + 1) * 512]

    ring = [A["ring0"], A["ring1"]]
    ring_i = [0]

    def load_piece(src_ap, ncols=512, reads=()):
        i = ring_i[0] % 2
        ring_i[0] += 1
        buf = ring[i]
        key = f"ring{i}"
        dst = buf.rearrange("p (kc n) -> p kc n", n=ncols)
        kb.op("sp", lambda e, d=dst, s=src_ap: e.dma_start(out=d, in_=s),
              reads=reads, writes=(key,))
        return buf, key

    def load_piece3(src_hi, src_lo, reads=()):
        i = ring_i[0] % 2
        ring_i[0] += 1
        bc = ring[i].bitcast(BF16)
        key = f"ring{i}"
        bh = bc[:, 0:4096].rearrange("p (kc n) -> p kc n", n=512)
        bl = bc[:, 4096:8192].rearrange("p (kc n) -> p kc n", n=512)
        kb.op("sp", lambda e, s=src_hi: e.dma_start(out=bh, in_=s),
              reads=reads, writes=(key,))
        kb.op("sp", lambda e, s=src_lo: e.dma_start(out=bl, in_=s),
              reads=reads, writes=(key,))
        return bh, bl, key

    pj = [0]

    def next_bank():
        i = pj[0] % 2
        pj[0] += 1
        return PS[i], f"P{i}"

    for n in ("cid", "triu", "mb", "mf", "bqk_a", "bqk_s", "bp_a", "bo_s",
              "b1mt", "g1t", "b1t", "gmt", "bmt"):
        kb.op("sp", lambda e, d=A[n], s=dr[n]: e.dma_start(out=d, in_=s),
              writes=(n,))
    _BR = {0: (0, 0), 1: (32, 0), 2: (64, 0), 3: (0, 1024)}
    for r in range(4):
        bp, bc = _BR[r]
        kb.op("sp", lambda e, d=A["brows"][bp:bp + 1, bc:bc + 1024],
              s=dr["brows"][r:r + 1, :]: e.dma_start(out=d, in_=s),
              writes=(f"brows{r}",))
    kb.op("dve", lambda e: e.memset(A["allones"], 1.0), writes=("allones",))
    kb.op("dve", lambda e: e.memset(A["SZ"], 0.0),
          writes=tuple(f"SZ:{h}" for h in range(H)))
    kb.op("dve", lambda e: e.memset(A["kprev"], 0.0), writes=("kprev",))
    kb.op("dve", lambda e: e.memset(A["R5"], 0.0),
          writes=tuple(f"vex:{s}" for s in range(5)))
    vext_ones = A["R5"].rearrange("p (g e) -> p g e", e=65)[:, :, 64:65]
    kb.op("dve", lambda e: e.memset(vext_ones, 1.0), writes=("R5ones",))
    kb.op("dve", lambda e: e.tensor_scalar(A["bqk_a"][:, 0:8], A["bqk_a"][:, 0:8],
                                           0.125, None, ALU.mult),
          reads=("bqk_a",), writes=("bqk_a",))

    spanA = arena[:, _AR["R0"][0]:_AR["R0"][0] + 8192]
    spanB = arena[:, _AR["R2"][0]:_AR["R2"][0] + 8192]
    kA = ("R0",) + rkeys("R1")
    kB = rkeys("R2") + rkeys("R3")
    for src, nsrc, scr, iscos in (("th", "thn", "csc", True),
                                  ("ph", "phn", "ssc", False)):
        srcA = dr[src].rearrange("(kc p) n -> p kc n", p=128)
        srcB = dr[nsrc].rearrange("(kc p) n -> p kc n", p=128)
        dstA = spanA.rearrange("p (kc n) -> p kc n", n=1024)
        dstB = spanB.rearrange("p (kc n) -> p kc n", n=1024)
        kb.op("sp", lambda e, d=dstA, s=srcA: e.dma_start(out=d, in_=s), writes=kA)
        kb.op("sp", lambda e, d=dstB, s=srcB: e.dma_start(out=d, in_=s), writes=kB)
        kb.op("dve", lambda e: e.tensor_add(spanA, spanA, spanB),
              reads=kA + kB, writes=kA)
        if iscos:
            kb.op("act", lambda e: e.activation(spanB, spanA, AF.Sin, scale=0.5),
                  reads=kA, writes=kB)
            kb.op("act", lambda e: e.activation(spanA, spanB, AF.Square),
                  reads=kB, writes=kA)
            kb.op("dve", lambda e: e.tensor_scalar(spanA, spanA, -2.0, 1.0,
                                                   ALU.mult, ALU.add),
                  reads=kA, writes=kA)
        else:
            kb.op("act", lambda e: e.activation(spanA, spanA, AF.Sin, scale=1.0),
                  reads=kA, writes=kA)
        bcB = spanB.bitcast(BF16)
        hiv, lov = bcB[:, 0:8192], bcB[:, 8192:16384]
        kb.op("act", lambda e: e.activation(hiv, spanA, AF.Copy),
              reads=kA, writes=kB)
        kb.op("dve", lambda e: e.scalar_tensor_tensor(
            lov, spanA, 1.0, hiv, ALU.mult, ALU.subtract),
            reads=kA + kB, writes=kB)
        for h, v in (("_hi", hiv), ("_lo", lov)):
            dstv = wv[scr + h]
            srcv = v.rearrange("p (kc n) -> p kc n", n=1024)
            kb.op("gp", lambda e, d=dstv, s=srcv: e.dma_start(out=d, in_=s),
                  reads=kB, writes=(scr + h,))

    _splits = []
    for w, npieces in (("Wqkv_a", 6), ("Wqkv_s", 6), ("Wp_a", 2), ("Wo_s", 2),
                       ("Wo", 2), ("W1", 8)):
        for j in range(npieces):
            _splits.append((wv[w][:, :, j * 512:(j + 1) * 512],
                            wv[w + "_hi"][:, :, j * 512:(j + 1) * 512],
                            wv[w + "_lo"][:, :, j * 512:(j + 1) * 512]))
    for kg in range(4):
        for nt in range(2):
            _splits.append((wv["W2"][kg][:, :, nt * 512:(nt + 1) * 512],
                            wv["W2_hi"][kg][:, :, nt * 512:(nt + 1) * 512],
                            wv["W2_lo"][kg][:, :, nt * 512:(nt + 1) * 512]))
    spl_bc = A["R9"].bitcast(BF16)
    spl_hi = spl_bc[:, 0:4096].rearrange("p (kc n) -> p kc n", n=512)
    spl_lo = spl_bc[:, 4096:8192].rearrange("p (kc n) -> p kc n", n=512)
    for src, dhi, dlo in _splits:
        buf, rk = load_piece(src)
        b3 = buf.rearrange("p (kc n) -> p kc n", n=512)
        kb.op("act", lambda e, b3=b3: e.activation(spl_hi, b3, AF.Copy),
              reads=(rk,), pre=rkeys("R9"), writes=rkeys("R9"))
        kb.op("dve", lambda e, b3=b3: e.scalar_tensor_tensor(
            spl_lo, b3, 1.0, spl_hi, ALU.mult, ALU.subtract),
            reads=(rk,) + rkeys("R9"), writes=rkeys("R9"))
        kb.op("gp", lambda e, d=dhi: e.dma_start(out=d, in_=spl_hi),
              reads=rkeys("R9"))
        kb.op("gp", lambda e, d=dlo: e.dma_start(out=d, in_=spl_lo),
              reads=rkeys("R9"))

    def x_load(g):
        dst = A["R0"].rearrange("p (c d) -> p c d", d=D)
        kb.op("gp", lambda e, d=dst, s=xr[g]: e.dma_start(out=d, in_=s),
              writes=("R0",))

    stats = A["stats"]
    st_m, st_r = stats[:, 8:12], stats[:, 12:16]

    def ln_stats(src_keys, src_slab):
        sums, ssq = stats[:, 0:4], stats[:, 4:8]
        scr1, scr2 = stats[:, 16:20], stats[:, 20:24]
        for c in range(NCH):
            xc = src_slab[:, c * D:(c + 1) * D]
            kb.op("act", lambda e, xc=xc, c=c: e.activation(
                A["sqdump"], xc, AF.Square, accum_out=ssq[:, c:c + 1]),
                reads=src_keys,
                pre=(("st_ssq",) if c == 0 else ()) + ("sqd",) + scr_pre("sqd"),
                writes=(("st_ssq",) if c == 3 else ()) + ("sqd",), sig=True)
        kb.op("dve", lambda e: e.reduce_sum(
            sums, src_slab.rearrange("p (c d) -> p c d", d=D), axis=AX.X),
            reads=src_keys, pre=("st_sum",), writes=("st_sum",))
        kb.op("dve", lambda e: e.tensor_scalar(st_m, sums, 1.0 / D, None, ALU.mult),
              reads=("st_sum",), writes=("st_m",))
        kb.op("dve", lambda e: e.tensor_scalar(scr1, ssq, 1.0 / D, EPS,
                                               ALU.mult, ALU.add),
              reads=("st_ssq",), writes=("st_s1",))
        kb.op("act", lambda e: e.activation(scr2, st_m, AF.Square),
              reads=("st_m",), writes=("st_s2",))
        kb.op("dve", lambda e: e.tensor_sub(scr1, scr1, scr2),
              reads=("st_s1", "st_s2"), writes=("st_s1",))
        kb.op("act", lambda e: e.activation(scr2, scr1, AF.Ln),
              reads=("st_s1",), writes=("st_s2",))
        kb.op("act", lambda e: e.activation(st_r, scr2, AF.Exp, scale=-0.5),
              reads=("st_s2",), writes=("st_r",))

    def ln_norm(src_keys, src_slab, dst_region, dst_slab):
        for c in range(NCH):
            xc = src_slab[:, c * D:(c + 1) * D]
            oc = dst_slab[:, c * D:(c + 1) * D]
            wk = (f"{dst_region}:{2 * c}", f"{dst_region}:{2 * c + 1}") \
                if dst_region != "R0" else ("R0",)
            kb.op("dve", lambda e, xc=xc, oc=oc, c=c: e.tensor_scalar(
                oc, xc, st_m[:, c:c + 1], st_r[:, c:c + 1], ALU.subtract, ALU.mult),
                reads=src_keys, sreads=("st_m", "st_r"), writes=wk)

    def transpose_slab(src_keys, src_slab, dst_region, affine=None):
        for k in range(KC):
            bank = PS[6]
            for c in range(NCH):
                src = src_slab[:, c * D + k * 128: c * D + (k + 1) * 128]
                kb.op("pe", lambda e, s=src, c=c: e.transpose(
                    bank[:, c * 128:(c + 1) * 128], s, A["cid"][:, 0:128]),
                    reads=src_keys + ("cid",),
                    pre=("P6",) if c == 0 else (),
                    writes=("P6",) if c == 3 else (), sig=(c == 3))
            dk = (f"{dst_region}:{k}",)
            hid, lod = phi(dst_region, k), plo(dst_region, k)
            if affine is None:
                kb.op("act", lambda e, hid=hid: e.activation(
                    hid, bank[:, 0:512], AF.Copy),
                    reads=("P6",), pre=dk, writes=dk)
                kb.op("dve", lambda e, hid=hid, lod=lod: e.scalar_tensor_tensor(
                    lod, bank[:, 0:512], 1.0, hid, ALU.mult, ALU.subtract),
                    reads=("P6",) + dk, writes=dk)
            else:
                gt, bt = affine
                aff = A["sqdump"][:, 0:512]
                kb.op("dve", lambda e, k=k, gt=gt, bt=bt: e.tensor_scalar(
                    aff, bank[:, 0:512], A[gt][:, k:k + 1], A[bt][:, k:k + 1],
                    ALU.mult, ALU.add),
                    reads=("P6", gt, bt), pre=("aff",) + scr_pre("aff"),
                    writes=("aff",))
                kb.op("act", lambda e, hid=hid: e.activation(
                    hid, aff, AF.Copy),
                    reads=("aff",), pre=dk, writes=dk)
                kb.op("dve", lambda e, hid=hid, lod=lod: e.scalar_tensor_tensor(
                    lod, aff, 1.0, hid, ALU.mult, ALU.subtract),
                    reads=("aff",) + dk, writes=dk)

    def modeA(w_piece_srcs, in_region, in_slab, n_chunks, evac):
        for j, src in enumerate(w_piece_srcs):
            buf, rk = load_piece(src)
            for nsub in range(4):
                nchunk = 4 * j + nsub
                if nchunk >= n_chunks:
                    break
                bank, bkey = next_bank()
                for k in range(KC):
                    stat = buf[:, k * T + nsub * 128: k * T + (nsub + 1) * 128]
                    mov = in_slab[:, k * T:(k + 1) * T]
                    kb.op("pe", lambda e, st_=stat, mv=mov, k=k, bank=bank:
                          e.matmul(bank, st_, mv, start=(k == 0),
                                   stop=(k == KC - 1)),
                          reads=(rk, f"{in_region}:{k}"),
                          pre=(bkey,) if k == 0 else (),
                          writes=(bkey,) if k == KC - 1 else (),
                          sig=(k == KC - 1))
                evac(nchunk, bank, bkey)

    def modeB(w_piece_srcs, in_region, in_slab, evac, bias_row=None,
              resid=(), wextra=()):
        for nt, src in enumerate(w_piece_srcs):
            buf, rk = load_piece(src, reads=wextra)
            banks = [(PS[i], f"P{i}") for i in range(4)]
            tail = ([("bias", bias_row)] if bias_row is not None else []) + \
                   [("resid", rv) for rv in resid]
            for k in range(KC):
                for ts in range(NCH):
                    bank, bkey = banks[ts]
                    stat = in_slab[:, k * T + ts * 128: k * T + (ts + 1) * 128]
                    mov = buf[:, k * 512:(k + 1) * 512]
                    last = (k == KC - 1) and not tail
                    kb.op("pe", lambda e, st_=stat, mv=mov, bank=bank, k=k,
                          last=last: e.matmul(bank, st_, mv, start=(k == 0),
                                              stop=last),
                          reads=(rk, f"{in_region}:{k}"),
                          pre=(bkey,) if k == 0 else (),
                          writes=(bkey,) if last else (), sig=last)
            for ts in range(NCH):
                bank, bkey = banks[ts]
                for i, (kind, val) in enumerate(tail):
                    last = (i == len(tail) - 1)
                    if kind == "bias":
                        r = val
                        bp, bc = _BR[r]
                        kb.op("pe", lambda e, bank=bank, bp=bp, bc=bc, nt=nt,
                              last=last: e.matmul(
                                  bank, A["allones"][bp:bp + 1, 0:128],
                                  A["brows"][bp:bp + 1,
                                             bc + nt * 512:bc + (nt + 1) * 512],
                                  start=False, stop=last),
                              reads=("allones", f"brows{r}"),
                              writes=(bkey,) if last else (), sig=last)
                    else:
                        slab, skeys = val
                        mv = slab[:, ts * D + nt * 512: ts * D + (nt + 1) * 512]
                        kb.op("pe", lambda e, bank=bank, mv=mv, last=last:
                              e.matmul(bank, A["cid"][:, 0:128], mv,
                                       start=False, stop=last),
                              reads=("cid",) + tuple(skeys),
                              writes=(bkey,) if last else (), sig=last)
                evac(ts, nt, bank, bkey)

    def modeA3(w_srcs, in_region, n_chunks, evac):
        for j, (shi, slo) in enumerate(w_srcs):
            bh, bl, rk = load_piece3(shi, slo)
            for nsub in range(4):
                nchunk = 4 * j + nsub
                if nchunk >= n_chunks:
                    break
                bank, bkey = next_bank()
                for k in range(KC):
                    wh = bh[:, k, nsub * 128:(nsub + 1) * 128]
                    wl = bl[:, k, nsub * 128:(nsub + 1) * 128]
                    xh, xl = phi(in_region, k), plo(in_region, k)
                    ik = (rk, f"{in_region}:{k}")
                    kb.op("pe", lambda e, wh=wh, xh=xh, bank=bank, k=k:
                          e.matmul(bank, wh, xh, start=(k == 0), stop=False),
                          reads=ik, pre=(bkey,) if k == 0 else (), sig=False)
                    kb.op("pe", lambda e, wh=wh, xl=xl, bank=bank:
                          e.matmul(bank, wh, xl, start=False, stop=False),
                          reads=ik, sig=False)
                    kb.op("pe", lambda e, wl=wl, xh=xh, bank=bank, k=k:
                          e.matmul(bank, wl, xh, start=False,
                                   stop=(k == KC - 1)),
                          reads=ik, writes=(bkey,) if k == KC - 1 else (),
                          sig=(k == KC - 1))
                evac(nchunk, bank, bkey)

    def modeB3(w_srcs, in_region, evac, bias_row=None, resid=(), wextra=()):
        for nt, (shi, slo) in enumerate(w_srcs):
            bh, bl, rk = load_piece3(shi, slo, reads=wextra)
            banks = [(PS[i], f"P{i}") for i in range(4)]
            tail = ([("bias", bias_row)] if bias_row is not None else []) + \
                   [("resid", rv) for rv in resid]
            for k in range(KC):
                for ts in range(NCH):
                    bank, bkey = banks[ts]
                    xh = phi(in_region, k)[:, ts * 128:(ts + 1) * 128]
                    xl = plo(in_region, k)[:, ts * 128:(ts + 1) * 128]
                    mh, ml = bh[:, k, :], bl[:, k, :]
                    ik = (rk, f"{in_region}:{k}")
                    kb.op("pe", lambda e, xh=xh, mh=mh, bank=bank, k=k:
                          e.matmul(bank, xh, mh, start=(k == 0), stop=False),
                          reads=ik, pre=(bkey,) if k == 0 else (), sig=False)
                    kb.op("pe", lambda e, xh=xh, ml=ml, bank=bank:
                          e.matmul(bank, xh, ml, start=False, stop=False),
                          reads=ik, sig=False)
                    bank_last = (k == KC - 1 and not tail)
                    kb.op("pe", lambda e, xl=xl, mh=mh, bank=bank,
                          bank_last=bank_last: e.matmul(
                              bank, xl, mh, start=False, stop=bank_last),
                          reads=ik,
                          writes=(bkey,) if bank_last else (),
                          sig=(bank_last or (k == KC - 1 and ts == NCH - 1)))
            for ts in range(NCH):
                bank, bkey = banks[ts]
                for i, (kind, val) in enumerate(tail):
                    lastt = (i == len(tail) - 1)
                    if kind == "bias":
                        r = val
                        bp, bc = _BR[r]
                        kb.op("pe", lambda e, bank=bank, bp=bp, bc=bc, nt=nt,
                              lastt=lastt: e.matmul(
                                  bank, A["allones"][bp:bp + 1, 0:128],
                                  A["brows"][bp:bp + 1,
                                             bc + nt * 512:bc + (nt + 1) * 512],
                                  start=False, stop=lastt),
                              reads=("allones", f"brows{r}"),
                              writes=(bkey,) if lastt else (), sig=lastt)
                    else:
                        slab, skeys = val
                        mv = slab[:, ts * D + nt * 512: ts * D + (nt + 1) * 512]
                        kb.op("pe", lambda e, bank=bank, mv=mv, lastt=lastt:
                              e.matmul(bank, A["cid"][:, 0:128], mv,
                                       start=False, stop=lastt),
                              reads=("cid",) + tuple(skeys),
                              writes=(bkey,) if lastt else (), sig=lastt)
                evac(ts, nt, bank, bkey)

    def tap(name, slab, cols, reads):
        if name in tap_dr:
            kb.op("gp", lambda e: e.dma_start(out=tap_dr[name],
                                              in_=slab[:, 0:cols]),
                  reads=reads)

    vexr = A["R5"].rearrange("p (s q) -> p s q", s=5)

    def emit_group(g):
        is_first = (g == 0)

        ln_stats(("R0",), A["R0"])
        ln_norm(("R0",), A["R0"], "R1", A["R1"])
        tap(f"xn{g}", A["R1"], 4096, rkeys("R1"))
        transpose_slab(rkeys("R1"), A["R1"], "R2", affine=("g1t", "b1t"))

        def evac_qk(nchunk, bank, bkey):
            if nchunk < 8:
                dst, key, scale = (A["R3"][:, nchunk * T:(nchunk + 1) * T],
                                   f"R3:{nchunk}", 0.125)
            else:
                dst, key, scale = (A["R4"][:, (nchunk - 8) * T:(nchunk - 7) * T],
                                   f"R4:{nchunk - 8}", 1.0)
            kb.op("act", lambda e, d=dst, nchunk=nchunk, scale=scale:
                  e.activation(d, bank[:, 0:512], AF.Identity,
                               bias=A["bqk_a"][:, nchunk:nchunk + 1], scale=scale),
                  reads=(bkey, "bqk_a"), writes=(key,))
        modeA3([(wv["Wqkv_a_hi"][:, :, j * 512:(j + 1) * 512],
                 wv["Wqkv_a_lo"][:, :, j * 512:(j + 1) * 512])
                for j in range(4)], "R2", 16, evac_qk)

        def evac_v(ts, nt, bank, bkey):
            dst = A["R5"].rearrange("p (s h e) -> p s h e", s=5, e=65)[
                :, ts + 1, 8 * nt:8 * (nt + 1), 0:64]
            src = bank.rearrange("p (h e) -> p h e", e=64)[:, 0:8, :]
            kb.op("dve", lambda e, d=dst, s=src: e.tensor_copy(d, s),
                  reads=(bkey,), pre=(f"vex:{ts + 1}",),
                  writes=(f"vex:{ts + 1}",))
        modeB3([(wv["Wqkv_a_hi"][:, :, 2048 + nt * 512: 2048 + (nt + 1) * 512],
                 wv["Wqkv_a_lo"][:, :, 2048 + nt * 512: 2048 + (nt + 1) * 512])
                for nt in range(2)], "R2", evac_v, bias_row=0)

        for c in range(NCH):
            for h in range(H):
                j, base, hp = h // 2, (h % 2) * 64, h % 2
                sc, sck = PS[4 + hp], f"P{4 + hp}"
                q = A["R3"][base:base + 64, j * T + c * 128: j * T + (c + 1) * 128]
                if c == 0:
                    kp = A["kprev"][base:base + 64, j * 128:(j + 1) * 128]
                    kcur = A["R4"][base:base + 64, j * T: j * T + 128]
                    kb.op("pe", lambda e, q=q, kp=kp, sc=sc: e.matmul(
                        sc[:, 0:128], q, kp, start=True, stop=False),
                        reads=(f"R3:{j}", "kprev"), pre=(sck,), sig=False)
                    kb.op("pe", lambda e, q=q, kc_=kcur, sc=sc: e.matmul(
                        sc[:, 128:256], q, kc_, start=False, stop=False),
                        reads=(f"R3:{j}", f"R4:{j}"), sig=False)
                else:
                    kc_ = A["R4"][base:base + 64,
                                  j * T + (c - 1) * 128: j * T + (c + 1) * 128]
                    kb.op("pe", lambda e, q=q, kc_=kc_, sc=sc: e.matmul(
                        sc[:, 0:256], q, kc_, start=True, stop=False),
                        reads=(f"R3:{j}", f"R4:{j}"), pre=(sck,), sig=False)
                mkey = "mf" if (is_first and c == 0) else "mb"
                kb.op("pe", lambda e, m=A[mkey], sc=sc: e.matmul(
                    sc[:, 0:256], A["cid"][:, 0:128], m, start=False, stop=True),
                    reads=("cid", mkey), writes=(sck,))
                nm = A["dens"][:, 0:1]
                kb.op("dve", lambda e, sc=sc: e.reduce_max(
                    nm, sc[:, 0:256], axis=AX.X, negate=True),
                    reads=(sck,), writes=("nm",))
                px = A["pexp"][:, hp * 256:hp * 256 + 256]
                kb.op("act", lambda e, sc=sc, px=px: e.activation(
                    px, sc[:, 0:256], AF.Exp, bias=nm, scale=1.0),
                    reads=(sck, "nm"),
                    pre=(f"pexp{hp}",) + scr_pre(f"pexp{hp}"),
                    writes=(f"pexp{hp}",))
                tb = PS[6]
                kb.op("pe", lambda e, px=px, tb=tb: e.transpose(
                    tb[:, 0:128], px[:, 0:128], A["cid"][:, 0:128]),
                    reads=(f"pexp{hp}", "cid"), pre=("P6",), sig=False)
                kb.op("pe", lambda e, px=px, tb=tb: e.transpose(
                    tb[:, 128:256], px[:, 128:256], A["cid"][:, 0:128]),
                    reads=(f"pexp{hp}", "cid"), writes=("P6",))
                pt = A["pT"][:, hp * 256:hp * 256 + 256]
                kb.op("act", lambda e, tb=tb, pt=pt: e.activation(
                    pt, tb[:, 0:256], AF.Copy),
                    reads=("P6",), pre=(f"pT{hp}",), writes=(f"pT{hp}",))
                av = PS[7]
                vprev = vexr[:, c, h * 65:h * 65 + 65]
                vcur = vexr[:, c + 1, h * 65:h * 65 + 65]
                kb.op("pe", lambda e, pt=pt, v=vprev, av=av: e.matmul(
                    av[:, 0:65], pt[:, 0:128], v, start=True, stop=False),
                    reads=(f"pT{hp}", f"vex:{c}", "R5ones"), pre=("P7",),
                    sig=False)
                kb.op("pe", lambda e, pt=pt, v=vcur, av=av: e.matmul(
                    av[:, 0:65], pt[:, 128:256], v, start=False, stop=True),
                    reads=(f"pT{hp}", f"vex:{c + 1}", "R5ones"), writes=("P7",))
                dent, rden = A["dens"][:, 1:2], A["dens"][:, 2:3]
                kb.op("dve", lambda e, av=av: e.tensor_copy(dent, av[:, 64:65]),
                      reads=("P7",), writes=("dent",))
                kb.op("dve", lambda e: e.reciprocal(rden, dent),
                      reads=("dent",), writes=("rden",))
                dst = A["R1"][:, c * D + h * 64: c * D + (h + 1) * 64]
                rkey = f"R1:{2 * c + h // 8}"
                kb.op("dve", lambda e, av=av, dst=dst: e.tensor_scalar(
                    dst, av[:, 0:64], rden, None, ALU.mult),
                    reads=("P7",), sreads=("rden",),
                    pre=(rkey,) if h % 8 == 0 else (), writes=(rkey,))
        tap(f"swa{g}", A["R1"], 4096, rkeys("R1"))

        kpdst = A["kprev"].rearrange("p (j q) -> p j q", q=128)
        kpsrc = A["R4"].rearrange("p (j t) -> p j t", t=T)[:, :, 384:512]
        kb.op("dve", lambda e: e.tensor_copy(kpdst, kpsrc),
              reads=rkeys("R4"), writes=("kprev",))
        kb.op("dve", lambda e: e.tensor_copy(A["R5"][:, 0:1040],
                                             A["R5"][:, 4160:5200]),
              reads=("vex:4", "R5ones"), pre=("vex:0",), writes=("vex:0",))

        def evac_qk_s(nchunk, bank, bkey):
            isq = nchunk < 8
            jj = nchunk if isq else nchunk - 8
            slab = A["R3"] if isq else A["R4"]
            dst = slab[:, jj * T:(jj + 1) * T]
            key = f"R3:{jj}" if isq else f"R4:{jj}"
            scale = 0.125 if isq else 1.0
            scr = A["eluscr"]
            kb.op("dve", lambda e, d=dst, nchunk=nchunk, scale=scale:
                  e.tensor_scalar(d, bank[:, 0:512],
                                  A["bqk_s"][:, nchunk:nchunk + 1], scale,
                                  ALU.add, ALU.mult),
                  reads=(bkey, "bqk_s"), pre=(key,), writes=(key,))
            kb.op("act", lambda e, d=dst: e.activation(scr, d, AF.Exp),
                  reads=(key,), pre=("elusc",) + scr_pre("elusc"),
                  writes=("elusc",))
            kb.op("dve", lambda e: e.tensor_scalar(scr, scr, 1.0, 1e-4,
                                                   ALU.min, ALU.add),
                  reads=("elusc",), writes=("elusc",))
            kb.op("dve", lambda e, d=dst: e.scalar_tensor_tensor(
                d, d, 0.0, scr, ALU.max, ALU.add),
                reads=(key, "elusc"), writes=(key,))
        modeA3([(wv["Wqkv_s_hi"][:, :, j * 512:(j + 1) * 512],
                 wv["Wqkv_s_lo"][:, :, j * 512:(j + 1) * 512])
                for j in range(4)], "R2", 16, evac_qk_s)
        modeB3([(wv["Wqkv_s_hi"][:, :, 2048 + nt * 512: 2048 + (nt + 1) * 512],
                 wv["Wqkv_s_lo"][:, :, 2048 + nt * 512: 2048 + (nt + 1) * 512])
                for nt in range(2)], "R2", evac_v, bias_row=1)
        tap(f"qps{g}", A["R3"], 4096, rkeys("R3"))
        tap(f"kps{g}", A["R4"], 4096, rkeys("R4"))

        for c in range(NCH):
            for h in range(H):
                j, base, hp = h // 2, (h % 2) * 64, h % 2
                qs = A["R3"][base:base + 64, j * T + c * 128: j * T + (c + 1) * 128]
                ks = A["R4"][base:base + 64, j * T + c * 128: j * T + (c + 1) * 128]
                at, atk = PS[4 + hp], f"P{4 + hp}"
                kb.op("pe", lambda e, ks=ks, qs=qs, at=at: e.matmul(
                    at[:, 0:128], ks, qs, start=True, stop=True),
                    reads=(f"R3:{j}", f"R4:{j}"), pre=(atk,), writes=(atk,))
                atm = A["ATm"][:, hp * 128:hp * 128 + 128]
                kb.op("dve", lambda e, at=at, atm=atm: e.scalar_tensor_tensor(
                    atm, at[:, 0:128], 1.0, A["triu"], ALU.mult, ALU.mult),
                    reads=(atk, "triu"), pre=(f"ATm{hp}",), writes=(f"ATm{hp}",))
                idb = A["cid"][0:64, 0:64] if base == 0 else \
                    A["cid"][64:128, 128:192]
                tb = PS[6]
                kb.op("pe", lambda e, ks=ks, idb=idb, tb=tb: e.transpose(
                    tb[:, 0:64], ks, idb),
                    reads=(f"R4:{j}", "cid"), pre=("P6",), writes=("P6",))
                kt = A["kT"][:, hp * 64:hp * 64 + 64]
                kb.op("act", lambda e, kt=kt, tb=tb: e.activation(
                    kt, tb[:, 0:64], AF.Copy),
                    reads=("P6",), pre=(f"kT{hp}",), writes=(f"kT{hp}",))
                nd, ndk = PS[5 - hp], f"P{5 - hp}"
                sz = A["SZ"][base:base + 64, j * 65:(j + 1) * 65]
                vex = vexr[:, c + 1, h * 65:h * 65 + 65]
                kb.op("pe", lambda e, qs=qs, sz=sz, nd=nd: e.matmul(
                    nd[:, 0:65], qs, sz, start=True, stop=False),
                    reads=(f"R3:{j}", f"SZ:{h}"), pre=(ndk,), sig=False)
                kb.op("pe", lambda e, atm=atm, vex=vex, nd=nd: e.matmul(
                    nd[:, 0:65], atm, vex, start=False, stop=True),
                    reads=(f"ATm{hp}", f"vex:{c + 1}", "R5ones"), writes=(ndk,))
                dent, rden = A["dens"][:, 4:5], A["dens"][:, 5:6]
                kb.op("dve", lambda e, nd=nd: e.tensor_scalar(
                    dent, nd[:, 64:65], 1e-6, None, ALU.add),
                    reads=(ndk,), writes=("dent2",))
                kb.op("dve", lambda e: e.reciprocal(rden, dent),
                      reads=("dent2",), writes=("rden2",))
                dst = A["R10"][:, c * D + h * 64: c * D + (h + 1) * 64]
                rkey = f"R10:{2 * c + h // 8}"
                kb.op("dve", lambda e, nd=nd, dst=dst: e.tensor_scalar(
                    dst, nd[:, 0:64], rden, None, ALU.mult),
                    reads=(ndk,), sreads=("rden2",),
                    pre=(rkey,) if h % 8 == 0 else (), writes=(rkey,))
                szp = PS[7]
                tp = None if base == 0 else (0, 64)
                kb.op("pe", lambda e, kt=kt, vex=vex, szp=szp, base=base, tp=tp:
                      e.matmul(szp[base:base + 64, 0:65], kt, vex,
                               start=True, stop=True, tile_position=tp),
                      reads=(f"kT{hp}", f"vex:{c + 1}", "R5ones"),
                      pre=("P7",), writes=("P7",))
                kb.op("dve", lambda e, sz=sz, szp=szp, base=base: e.tensor_add(
                    sz, sz, szp[base:base + 64, 0:65]),
                    reads=("P7", f"SZ:{h}"), writes=(f"SZ:{h}",))
        tap(f"o{g}", A["R10"], 4096, rkeys("R10"))

        transpose_slab(rkeys("R1"), A["R1"], "R3")

        def evac_wp(nchunk, bank, bkey):
            dst = A["R9"][:, nchunk * T:(nchunk + 1) * T]
            kb.op("act", lambda e, d=dst, nchunk=nchunk: e.activation(
                d, bank[:, 0:512], AF.Identity,
                bias=A["bp_a"][:, nchunk:nchunk + 1]),
                reads=(bkey, "bp_a"), writes=(f"R9:{nchunk}",))
        modeA3([(wv["Wp_a_hi"][:, :, j * 512:(j + 1) * 512],
                 wv["Wp_a_lo"][:, :, j * 512:(j + 1) * 512])
                for j in range(2)], "R3", 8, evac_wp)

        transpose_slab(rkeys("R10"), A["R10"], "R4")

        def evac_wos(nchunk, bank, bkey):
            dst = A["R9"][:, nchunk * T:(nchunk + 1) * T]
            kb.op("dve", lambda e, d=dst, nchunk=nchunk: e.scalar_tensor_tensor(
                d, bank[:, 0:512], A["bo_s"][:, nchunk:nchunk + 1], d,
                ALU.add, ALU.add),
                reads=(bkey, "bo_s", f"R9:{nchunk}"), writes=(f"R9:{nchunk}",))
        modeA3([(wv["Wo_s_hi"][:, :, j * 512:(j + 1) * 512],
                 wv["Wo_s_lo"][:, :, j * 512:(j + 1) * 512])
                for j in range(2)], "R4", 8, evac_wos)

        for j in range(KC):
            a32 = A["R9"][:, j * T:(j + 1) * T]
            hid, lod = phi("R3", j), plo("R3", j)
            kb.op("act", lambda e, a32=a32, hid=hid: e.activation(
                hid, a32, AF.Copy),
                reads=(f"R9:{j}",), pre=(f"R3:{j}",), writes=(f"R3:{j}",))
            kb.op("dve", lambda e, a32=a32, hid=hid, lod=lod:
                  e.scalar_tensor_tensor(lod, a32, 1.0, hid,
                                         ALU.mult, ALU.subtract),
                  reads=(f"R9:{j}", f"R3:{j}"), writes=(f"R3:{j}",))

        def evac_wo(ts, nt, bank, bkey):
            dst = A["R10"][:, ts * D + nt * 512: ts * D + (nt + 1) * 512]
            kb.op("act", lambda e, d=dst, bank=bank: e.activation(
                d, bank, AF.Copy),
                reads=(bkey,), pre=(f"R10:{2 * ts + nt}",),
                writes=(f"R10:{2 * ts + nt}",))
        modeB3([(wv["Wo_hi"][:, :, nt * 512:(nt + 1) * 512],
                 wv["Wo_lo"][:, :, nt * 512:(nt + 1) * 512])
                for nt in range(2)], "R3", evac_wo, bias_row=2,
               resid=[(A["R0"], ("R0",))])
        tap(f"x1_{g}", A["R10"], 4096, rkeys("R10"))

        ln_stats(rkeys("R10"), A["R10"])
        ln_norm(rkeys("R10"), A["R10"], "R1", A["R1"])
        tap(f"h{g}", A["R1"], 4096, rkeys("R1"))
        transpose_slab(rkeys("R1"), A["R1"], "R2", affine=("gmt", "bmt"))

        gel_reg = ["R3", "R4", "R9", "R1"]
        gel_slab = [A["R3"], A["R4"], A["R9"], A["R1"]]

        def evac_w1(nchunk, bank, bkey):
            regn = gel_reg[nchunk // 8]
            kk = nchunk % 8
            gk = "g32a" if nchunk % 2 == 0 else "g32b"
            g32 = A["sqdump"][:, 0:512] if nchunk % 2 == 0 \
                else A["sqdump"][:, 512:1024]
            dk = (f"{regn}:{kk}",)
            hid, lod = phi(regn, kk), plo(regn, kk)
            kb.op("act", lambda e, g32=g32, nchunk=nchunk: e.activation(
                g32, bank[:, 0:512], AF.Gelu,
                bias=A["b1mt"][:, nchunk:nchunk + 1]),
                reads=(bkey, "b1mt"), pre=(gk,) + scr_pre(gk), writes=(gk,))
            kb.op("act", lambda e, g32=g32, hid=hid: e.activation(
                hid, g32, AF.Copy),
                reads=(gk,), pre=dk, writes=dk)
            kb.op("dve", lambda e, g32=g32, hid=hid, lod=lod:
                  e.scalar_tensor_tensor(lod, g32, 1.0, hid,
                                         ALU.mult, ALU.subtract),
                  reads=(gk,) + dk, writes=dk)
        modeA3([(wv["W1_hi"][:, :, j * 512:(j + 1) * 512],
                 wv["W1_lo"][:, :, j * 512:(j + 1) * 512])
                for j in range(8)], "R2", 32, evac_w1)
        tap(f"gela{g}", A["R3"], 4096, rkeys("R3"))
        tap(f"gelb{g}", A["R4"], 4096, rkeys("R4"))
        tap(f"gelc{g}", A["R9"], 4096, rkeys("R9"))
        tap(f"geld{g}", A["R1"], 4096, rkeys("R1"))

        for nt in range(2):
            banks = [(PS[i], f"P{i}") for i in range(4)]
            for kg in range(4):
                bh, bl, rk = load_piece3(
                    wv["W2_hi"][kg][:, :, nt * 512:(nt + 1) * 512],
                    wv["W2_lo"][kg][:, :, nt * 512:(nt + 1) * 512])
                for k in range(KC):
                    kk = kg * 8 + k
                    regn = gel_reg[kk // 8]
                    for ts in range(NCH):
                        bank, bkey = banks[ts]
                        xh = phi(regn, kk % 8)[:, ts * 128:(ts + 1) * 128]
                        xl = plo(regn, kk % 8)[:, ts * 128:(ts + 1) * 128]
                        mh, ml = bh[:, k, :], bl[:, k, :]
                        ik = (rk, f"{regn}:{kk % 8}")
                        first = (kg == 0 and k == 0)
                        kb.op("pe", lambda e, xh=xh, mh=mh, bank=bank,
                              first=first: e.matmul(bank, xh, mh,
                                                    start=first, stop=False),
                              reads=ik, pre=(bkey,) if first else (),
                              sig=False)
                        kb.op("pe", lambda e, xh=xh, ml=ml, bank=bank:
                              e.matmul(bank, xh, ml, start=False, stop=False),
                              reads=ik, sig=False)
                        kb.op("pe", lambda e, xl=xl, mh=mh, bank=bank:
                              e.matmul(bank, xl, mh, start=False, stop=False),
                              reads=ik,
                              sig=(k == KC - 1 and ts == NCH - 1))
            for ts in range(NCH):
                bank, bkey = banks[ts]
                kb.op("pe", lambda e, bank=bank, nt=nt: e.matmul(
                    bank, A["allones"][0:1, 0:128],
                    A["brows"][0:1, 1024 + nt * 512:1024 + (nt + 1) * 512],
                    start=False, stop=False),
                    reads=("allones", "brows3"), sig=False)
                mvx1 = A["R10"][:, ts * D + nt * 512: ts * D + (nt + 1) * 512]
                kb.op("pe", lambda e, bank=bank, mv=mvx1: e.matmul(
                    bank, A["cid"][:, 0:128], mv, start=False, stop=False),
                    reads=("cid", f"R10:{2 * ts + nt}"), sig=False)
                mvx = A["R0"][:, ts * D + nt * 512: ts * D + (nt + 1) * 512]
                kb.op("pe", lambda e, bank=bank, mv=mvx: e.matmul(
                    bank, A["cid"][:, 0:128], mv, start=False, stop=True),
                    reads=("cid", "R0"), writes=(bkey,))
                dst = A["R11"][:, ts * D + nt * 512: ts * D + (nt + 1) * 512]
                kb.op("act", lambda e, d=dst, bank=bank: e.activation(
                    d, bank, AF.Copy),
                    reads=(bkey,), pre=(f"R11:{2 * ts + nt}",),
                    writes=(f"R11:{2 * ts + nt}",))
        tap(f"t{g}", A["R11"], 4096, rkeys("R11"))

        ln_stats(rkeys("R11"), A["R11"])
        ln_norm(rkeys("R11"), A["R11"], "R1", A["R1"])
        transpose_slab(rkeys("R1"), A["R1"], "R2")

        def evac_re(ts, nt, bank, bkey):
            dst = A["R10"][:, ts * D + nt * 512: ts * D + (nt + 1) * 512]
            kb.op("act", lambda e, d=dst, bank=bank: e.activation(
                d, bank, AF.Square),
                reads=(bkey,), pre=(f"R10:{2 * ts + nt}",),
                writes=(f"R10:{2 * ts + nt}",))
        def evac_im(ts, nt, bank, bkey):
            dst = A["R9"][:, ts * D + nt * 512: ts * D + (nt + 1) * 512]
            kb.op("act", lambda e, d=dst, bank=bank: e.activation(
                d, bank, AF.Square),
                reads=(bkey,), pre=(f"R9:{2 * ts + nt}",),
                writes=(f"R9:{2 * ts + nt}",))
        modeB3([(wv["csc_hi"][:, :, nt * 512:(nt + 1) * 512],
                 wv["csc_lo"][:, :, nt * 512:(nt + 1) * 512])
                for nt in range(2)], "R2", evac_re,
               wextra=("csc_hi", "csc_lo"))
        modeB3([(wv["ssc_hi"][:, :, nt * 512:(nt + 1) * 512],
                 wv["ssc_lo"][:, :, nt * 512:(nt + 1) * 512])
                for nt in range(2)], "R2", evac_im,
               wextra=("ssc_hi", "ssc_lo"))
        kb.op("dve", lambda e: e.tensor_add(A["R3"][:, 0:4096], A["R10"],
                                            A["R9"]),
              reads=rkeys("R10") + rkeys("R9"), pre=rkeys("R3"),
              writes=rkeys("R3"))
        kb.op("act", lambda e: e.activation(A["R10"], A["R3"][:, 0:4096],
                                            AF.Sqrt),
              reads=rkeys("R3"), pre=rkeys("R10"), writes=rkeys("R10"))
        kb.op("dve", lambda e: e.tensor_add(A["R9"], A["R1"], A["R10"]),
              reads=rkeys("R10") + rkeys("R1"), pre=rkeys("R9"),
              writes=rkeys("R9"))
        ln_stats(rkeys("R9"), A["R9"])
        ln_norm(rkeys("R9"), A["R9"], "R0", A["R0"])
        src = A["R0"].rearrange("p (c d) -> p c d", d=D)
        kb.op("gp", lambda e, g=g: e.dma_start(out=outr[g], in_=src),
              reads=("R0",))

    x_load(0)
    for g in range(NG):
        emit_group(g)
        if g + 1 < NG:
            x_load(g + 1)

    total_gp = kb.cnt["gp"]

    with ExitStack() as sst:
        sems = {n: sst.enter_context(nc.semaphore(f"sem_{n}")) for n in KB.ENGS}

        def make_body(eng_name):
            oplist = kb.ops[eng_name]
            mysem = sems[eng_name]
            mymul = 16 if eng_name in KB.DMAQ else 1

            def body(e):
                for fn, waits, selfwaits, sig in oplist:
                    for p, v in waits:
                        e.wait_ge(sems[p], v * (16 if p in KB.DMAQ else 1))
                    for v in selfwaits:
                        e.wait_ge(mysem, v * mymul)
                    inst = fn(e)
                    if sig:
                        inst.then_inc(mysem, mymul)
                if eng_name == "gp":
                    e.wait_ge(mysem, total_gp * 16)
            return body

        with nc.Block() as blk:
            blk.tensor(make_body("pe"))
            blk.vector(make_body("dve"))
            blk.scalar(make_body("act"))
            blk.sync(make_body("sp"))
            blk.gpsimd(make_body("gp"))
    st.close()
    return nc


def make_const_inputs():
    cid = np.zeros((128, 192), np.float32)
    cid[:, 0:128] = np.eye(128, dtype=np.float32)
    cid[64:128, 128:192] = np.eye(64, dtype=np.float32)
    triu = np.triu(np.ones((128, 128), np.float32))
    r = np.arange(128)[:, None]
    s = np.arange(256)[None, :]
    band = (s >= r + 1) & (s <= r + 128)
    first = band & (s >= 128)
    mb = np.where(band, 0.0, -1e30).astype(np.float32)
    mf = np.where(first, 0.0, -1e30).astype(np.float32)
    return cid, triu, mb, mf


def shard_inputs(inputs, b, L):
    cid, triu, mb, mf = make_const_inputs()
    g = lambda n: np.ascontiguousarray(np.asarray(inputs[n], dtype=np.float32))
    bq_a, bq_s = g("L0_attn_bqkv_a"), g("L0_attn_bqkv_s")
    brows = np.zeros((4, 1024), np.float32)
    brows[0] = bq_a[2048:3072]
    brows[1] = bq_s[2048:3072]
    brows[2] = g("L0_attn_bo")
    brows[3] = g("L0_attn_b2m")
    return {
        "x": np.ascontiguousarray(np.asarray(inputs["L0_attn_x"],
                                             dtype=np.float32)[b, :L]),
        "Wqkv_a": g("L0_attn_Wqkv_a"), "Wp_a": g("L0_attn_Wp_a"),
        "Wqkv_s": g("L0_attn_Wqkv_s"), "Wo_s": g("L0_attn_Wo_s"),
        "Wo": g("L0_attn_Wo"), "W1": g("L0_attn_W1"), "W2": g("L0_attn_W2"),
        "th": g("L0_ffn_theta"), "thn": g("L0_ffn_theta_noise"),
        "ph": g("L0_ffn_phi"), "phn": g("L0_ffn_phi_noise"),
        "cid": cid, "triu": triu, "mb": mb, "mf": mf,
        "bqk_a": np.ascontiguousarray(bq_a[0:2048].reshape(16, 128).T),
        "bqk_s": np.ascontiguousarray(bq_s[0:2048].reshape(16, 128).T),
        "bp_a": np.ascontiguousarray(g("L0_attn_bp_a").reshape(8, 128).T),
        "bo_s": np.ascontiguousarray(g("L0_attn_bo_s").reshape(8, 128).T),
        "b1mt": np.ascontiguousarray(g("L0_attn_b1m").reshape(32, 128).T),
        "g1t": np.ascontiguousarray(g("L0_attn_g1").reshape(8, 128).T),
        "b1t": np.ascontiguousarray(g("L0_attn_b1").reshape(8, 128).T),
        "gmt": np.ascontiguousarray(g("L0_attn_gm").reshape(8, 128).T),
        "bmt": np.ascontiguousarray(g("L0_attn_bm").reshape(8, 128).T),
        "brows": brows,
    }


def kernel(**inputs):
    L = 4096
    nc = build_nc(L)
    in_maps = [shard_inputs(inputs, b, L) for b in range(8)]
    res = run_bass_kernel_spmd(nc, in_maps, core_ids=list(range(8)))
    out = np.stack([np.asarray(r["out"]).reshape(L, D) for r in res.results])
    return out.astype(np.float32)


# version 03
# speedup vs baseline: 1.1177x; 1.0690x over previous
import ml_dtypes
import numpy as np
from contextlib import ExitStack

import concourse.bass as bass
import concourse.mybir as mybir
from concourse.bass_utils import run_bass_kernel_spmd

F32 = mybir.dt.float32
BF16 = mybir.dt.bfloat16
AF = mybir.ActivationFunctionType
ALU = mybir.AluOpType
AX = mybir.AxisListType

D, H, E = 1024, 16, 64
T, NCH, KC = 512, 4, 8
EPS = 1e-5

_AR = {}
_off = 0
def _reg(name, cols):
    global _off
    _AR[name] = (_off, cols)
    _off += (cols + 15) // 16 * 16
for _n, _c in [
    ("R0", 4096), ("R1", 4096), ("R2", 4096), ("R3", 4096), ("R4", 4096),
    ("R5", 5200), ("R9", 4096), ("R10", 4096), ("R11", 4096),
    ("ring0", 4096), ("ring1", 4096),
    ("kprev", 1024), ("pT", 512), ("ATm", 256), ("kT", 128),
    ("SZ", 520), ("sqdump", 1024), ("stats", 32),
    ("dens", 16),
    ("brows", 2048), ("allones", 128), ("cid", 192), ("triu", 128),
    ("mb", 256), ("mf", 256), ("m16", 320),
    ("bqk_a", 16), ("bqk_s", 16), ("bp_a", 8), ("bo_s", 8), ("b1mt", 32),
    ("g1t", 8), ("b1t", 8), ("gmt", 8), ("bmt", 8),
]:
    _reg(_n, _c)
ARENA_COLS = _off
assert ARENA_COLS <= 53200, ARENA_COLS


def rkeys(region, n=8):
    return tuple(f"{region}:{i}" for i in range(n))


class KB:
    ENGS = ("pe", "dve", "act", "sp", "gp")
    CHANS = ("sp", "gp", "r0", "r1", "ws0", "ws1", "ws2", "ws3")
    DMAQ = CHANS
    ALL = ENGS + ("r0", "r1", "ws0", "ws1", "ws2", "ws3")

    def __init__(self):
        self.ops = {n: [] for n in self.ENGS}
        self.cnt = {n: 0 for n in self.ALL}
        self.waited = {c: {p: 0 for p in self.ALL} for c in self.ENGS}
        self.reg = {}
        self.pend = {n: [] for n in self.ENGS}

    def _need(self, eng, waits, selfwaits, p, v):
        if v is None:
            raise RuntimeError("dependency on an unsignalled op")
        if p == eng:
            if eng == "pe":
                return
            if v > self.waited[eng][eng]:
                selfwaits.append(v)
                self.waited[eng][eng] = v
            return
        if p in self.CHANS:
            v = self.cnt[p]
        if v > self.waited[eng][p]:
            waits.append((p, v))
            self.waited[eng][p] = v

    def op(self, eng, fn, reads=(), writes=(), sreads=(), pre=(), sig=True,
           chan=None):
        sigchan = chan or eng
        waits, selfwaits = [], []
        if sigchan in self.CHANS and self.cnt[sigchan] > 0:
            self._need(eng, waits, selfwaits, sigchan, self.cnt[sigchan])
        for k in tuple(reads) + tuple(sreads):
            r = self.reg.get(k)
            if r and r["w"] is not None:
                self._need(eng, waits, selfwaits, *r["w"])
        for k in tuple(writes) + tuple(pre):
            r = self.reg.get(k)
            if r:
                if r["w"] is not None:
                    self._need(eng, waits, selfwaits, *r["w"])
                for p, v in r["r"].items():
                    self._need(eng, waits, selfwaits, p, v)
        if sig:
            self.cnt[sigchan] += 1
            val = self.cnt[sigchan]
            for k in self.pend[eng]:
                rr = self.reg.setdefault(k, {"w": None, "r": {}})
                rr["r"][eng] = val if sigchan == eng else self.cnt[eng]
            self.pend[eng].clear()
            for k in tuple(reads) + tuple(sreads):
                rr = self.reg.setdefault(k, {"w": None, "r": {}})
                rr["r"][sigchan] = val
        else:
            val = None
            for k in tuple(reads) + tuple(sreads):
                self.pend[eng].append(k)
        for k in writes:
            if not sig:
                raise RuntimeError("writes require sig=True")
            self.reg[k] = {"w": (sigchan, val), "r": {}}
        self.ops[eng].append((fn, waits, selfwaits, sig, sigchan))


def build_nc(L, taps=()):
    NG = L // T
    nc = bass.Bass("TRN2", target_bir_lowering=False, debug=False)

    dr = {}
    dr["x"] = nc.dram_tensor("x", [L, D], F32, kind="ExternalInput").ap()
    for n, shp in [
        ("Wqkv_a", [D, 3 * D]), ("Wp_a", [D, D]), ("Wqkv_s", [D, 3 * D]),
        ("Wo_s", [D, D]), ("Wo", [D, D]), ("W1", [D, 4 * D]), ("W2", [4 * D, D]),
        ("th", [D, D]), ("thn", [D, D]), ("ph", [D, D]), ("phn", [D, D]),
        ("cid", [128, 192]), ("triu", [128, 128]), ("mb", [128, 256]),
        ("mf", [128, 256]), ("bqk_a", [128, 16]), ("bqk_s", [128, 16]),
        ("bp_a", [128, 8]), ("bo_s", [128, 8]), ("b1mt", [128, 32]),
        ("g1t", [128, 8]), ("b1t", [128, 8]), ("gmt", [128, 8]), ("bmt", [128, 8]),
        ("brows", [4, 1024]),
    ]:
        dr[n] = nc.dram_tensor(n, shp, F32, kind="ExternalInput").ap()
    for n, shp in [("cid16", [128, 128]), ("mb16", [128, 256]),
                   ("mf16", [128, 256])]:
        dr[n] = nc.dram_tensor(n, shp, BF16, kind="ExternalInput").ap()
    dr["out"] = nc.dram_tensor("out", [L, D], F32, kind="ExternalOutput").ap()
    for w, shp in [("Wqkv_a", [D, 3 * D]), ("Wqkv_s", [D, 3 * D]),
                   ("Wp_a", [D, D]), ("Wo_s", [D, D]), ("Wo", [D, D]),
                   ("W1", [D, 4 * D]), ("W2", [4 * D, D]),
                   ("csc", [D, D]), ("ssc", [D, D])]:
        dr[w + "_hi"] = nc.dram_tensor(w + "_hi", shp, BF16, kind="Internal").ap()
        dr[w + "_lo"] = nc.dram_tensor(w + "_lo", shp, BF16, kind="Internal").ap()
    tap_dr = {}
    for tname, tcols in taps:
        tap_dr[tname] = nc.dram_tensor("tap_" + tname, [128, tcols], F32,
                                       kind="ExternalOutput").ap()

    st = ExitStack()
    arena = st.enter_context(nc.sbuf_tensor("arena", [128, ARENA_COLS], F32)).ap()
    A = {n: arena[:, o:o + c] for n, (o, c) in _AR.items()}
    A["eluscr"] = A["sqdump"][:, 0:512]
    _m16 = A["m16"].bitcast(BF16)
    A["cid16"] = _m16[:, 0:128]
    A["mb16"] = _m16[:, 128:384]
    A["mf16"] = _m16[:, 384:640]
    A["pexp"] = A["sqdump"][:, 0:512]
    SCRK = ("sqd", "elusc", "pexp0", "pexp1", "aff", "g32a", "g32b")
    def scr_pre(own):
        return tuple(k for k in SCRK if k != own)
    PS = [st.enter_context(nc.psum_tensor(f"pb{i}", [128, 512], F32)).ap()
          for i in range(8)]

    kb = KB()

    xr = dr["x"].rearrange("(g c p) d -> g p c d", c=NCH, p=128)
    outr = dr["out"].rearrange("(g c p) d -> g p c d", c=NCH, p=128)
    wv = {}
    for n in ("Wqkv_a", "Wqkv_s", "Wp_a", "Wo_s", "Wo", "W1"):
        wv[n] = dr[n].rearrange("(kc p) n -> p kc n", p=128)
    wv["W2"] = dr["W2"].rearrange("(kg kc p) n -> kg p kc n", kg=4, p=128)
    for w in ("Wqkv_a", "Wqkv_s", "Wp_a", "Wo_s", "Wo", "W1", "csc", "ssc"):
        for h in ("_hi", "_lo"):
            wv[w + h] = dr[w + h].rearrange("(kc p) n -> p kc n", p=128)
    for h in ("_hi", "_lo"):
        wv["W2" + h] = dr["W2" + h].rearrange("(kg kc p) n -> kg p kc n",
                                              kg=4, p=128)

    def phi(region, k):
        bc = A[region].bitcast(BF16)
        return bc[:, k * 512:(k + 1) * 512]

    def plo(region, k):
        bc = A[region].bitcast(BF16)
        return bc[:, 4096 + k * 512: 4096 + (k + 1) * 512]

    ring = [A["ring0"], A["ring1"]]
    ring_i = [0]

    def load_piece(src_ap, ncols=512, reads=()):
        i = ring_i[0] % 2
        ring_i[0] += 1
        buf = ring[i]
        key = f"ring{i}"
        dst = buf.rearrange("p (kc n) -> p kc n", n=ncols)
        kb.op("sp", lambda e, d=dst, s=src_ap: e.dma_start(out=d, in_=s),
              reads=reads, writes=(key,), chan=f"r{i}")
        return buf, key

    def load_piece3(src_hi, src_lo, reads=()):
        i = ring_i[0] % 2
        ring_i[0] += 1
        bc = ring[i].bitcast(BF16)
        key = f"ring{i}"
        bh = bc[:, 0:4096].rearrange("p (kc n) -> p kc n", n=512)
        bl = bc[:, 4096:8192].rearrange("p (kc n) -> p kc n", n=512)
        kb.op("sp", lambda e, s=src_hi: e.dma_start(out=bh, in_=s),
              reads=reads, writes=(key,), chan=f"r{i}")
        kb.op("sp", lambda e, s=src_lo: e.dma_start(out=bl, in_=s),
              reads=reads, writes=(key,), chan=f"r{i}")
        return bh, bl, key

    pj = [0]

    def next_bank():
        i = pj[0] % 2
        pj[0] += 1
        return PS[i], f"P{i}"

    _wsi = [0]
    for n in ("cid", "triu", "mb", "mf", "cid16", "mb16", "mf16",
              "bqk_a", "bqk_s", "bp_a", "bo_s",
              "b1mt", "g1t", "b1t", "gmt", "bmt"):
        kb.op("sp", lambda e, d=A[n], s=dr[n]: e.dma_start(out=d, in_=s),
              writes=(n,))
    _BR = {0: (0, 0), 1: (32, 0), 2: (64, 0), 3: (0, 1024)}
    for r in range(4):
        bp, bc = _BR[r]
        kb.op("sp", lambda e, d=A["brows"][bp:bp + 1, bc:bc + 1024],
              s=dr["brows"][r:r + 1, :]: e.dma_start(out=d, in_=s),
              writes=(f"brows{r}",))
    kb.op("dve", lambda e: e.memset(A["allones"], 1.0), writes=("allones",))
    kb.op("dve", lambda e: e.memset(A["SZ"], 0.0),
          writes=tuple(f"SZ:{h}" for h in range(H)))
    kb.op("dve", lambda e: e.memset(A["kprev"], 0.0), writes=("kprev",))
    kb.op("dve", lambda e: e.memset(A["R5"], 0.0),
          writes=tuple(f"vex:{s}" for s in range(5)))
    vext_ones = A["R5"].rearrange("p (g e) -> p g e", e=65)[:, :, 64:65]
    kb.op("dve", lambda e: e.memset(vext_ones, 1.0),
          pre=tuple(f"vex:{s_}" for s_ in range(5)), writes=("R5ones",))
    kb.op("dve", lambda e: e.tensor_scalar(A["bqk_a"][:, 0:8], A["bqk_a"][:, 0:8],
                                           0.125, None, ALU.mult),
          reads=("bqk_a",), writes=("bqk_a",))

    spanA = arena[:, _AR["R0"][0]:_AR["R0"][0] + 8192]
    spanB = arena[:, _AR["R2"][0]:_AR["R2"][0] + 8192]
    kA = ("R0",) + rkeys("R1")
    kB = rkeys("R2") + rkeys("R3")
    for src, nsrc, scr, iscos in (("th", "thn", "csc", True),
                                  ("ph", "phn", "ssc", False)):
        srcA = dr[src].rearrange("(kc p) n -> p kc n", p=128)
        srcB = dr[nsrc].rearrange("(kc p) n -> p kc n", p=128)
        dstA = spanA.rearrange("p (kc n) -> p kc n", n=1024)
        dstB = spanB.rearrange("p (kc n) -> p kc n", n=1024)
        kb.op("sp", lambda e, d=dstA, s=srcA: e.dma_start(out=d, in_=s), writes=kA)
        kb.op("sp", lambda e, d=dstB, s=srcB: e.dma_start(out=d, in_=s), writes=kB)
        kb.op("dve", lambda e: e.tensor_add(spanA, spanA, spanB),
              reads=kA + kB, writes=kA)
        if iscos:
            kb.op("act", lambda e: e.activation(spanB, spanA, AF.Sin, scale=0.5),
                  reads=kA, writes=kB)
            kb.op("act", lambda e: e.activation(spanA, spanB, AF.Square),
                  reads=kB, writes=kA)
            kb.op("dve", lambda e: e.tensor_scalar(spanA, spanA, -2.0, 1.0,
                                                   ALU.mult, ALU.add),
                  reads=kA, writes=kA)
        else:
            kb.op("act", lambda e: e.activation(spanA, spanA, AF.Sin, scale=1.0),
                  reads=kA, writes=kA)
        bcB = spanB.bitcast(BF16)
        hiv, lov = bcB[:, 0:8192], bcB[:, 8192:16384]
        kb.op("act", lambda e: e.activation(hiv, spanA, AF.Copy),
              reads=kA, writes=kB)
        kb.op("dve", lambda e: e.scalar_tensor_tensor(
            lov, spanA, 1.0, hiv, ALU.mult, ALU.subtract),
            reads=kA + kB, writes=kB)
        for h, v in (("_hi", hiv), ("_lo", lov)):
            dstv = wv[scr + h]
            srcv = v.rearrange("p (kc n) -> p kc n", n=1024)
            kb.op("gp", lambda e, d=dstv, s=srcv: e.dma_start(out=d, in_=s),
                  reads=kB, writes=(scr + h,), chan=f"ws{_wsi[0] % 4}")
            _wsi[0] += 1

    _splits = []
    for w, npieces in (("Wqkv_a", 6), ("Wqkv_s", 6), ("Wp_a", 2), ("Wo_s", 2),
                       ("Wo", 2), ("W1", 8)):
        for j in range(npieces):
            _splits.append((wv[w][:, :, j * 512:(j + 1) * 512],
                            wv[w + "_hi"][:, :, j * 512:(j + 1) * 512],
                            wv[w + "_lo"][:, :, j * 512:(j + 1) * 512]))
    for kg in range(4):
        for nt in range(2):
            _splits.append((wv["W2"][kg][:, :, nt * 512:(nt + 1) * 512],
                            wv["W2_hi"][kg][:, :, nt * 512:(nt + 1) * 512],
                            wv["W2_lo"][kg][:, :, nt * 512:(nt + 1) * 512]))
    spl_bc = A["R9"].bitcast(BF16)
    spl_hi = spl_bc[:, 0:4096].rearrange("p (kc n) -> p kc n", n=512)
    spl_lo = spl_bc[:, 4096:8192].rearrange("p (kc n) -> p kc n", n=512)
    for src, dhi, dlo in _splits:
        buf, rk = load_piece(src)
        b3 = buf.rearrange("p (kc n) -> p kc n", n=512)
        kb.op("act", lambda e, b3=b3: e.activation(spl_hi, b3, AF.Copy),
              reads=(rk,), pre=rkeys("R9"), writes=rkeys("R9"))
        kb.op("dve", lambda e, b3=b3: e.scalar_tensor_tensor(
            spl_lo, b3, 1.0, spl_hi, ALU.mult, ALU.subtract),
            reads=(rk,) + rkeys("R9"), writes=rkeys("R9"))
        kb.op("gp", lambda e, d=dhi: e.dma_start(out=d, in_=spl_hi),
              reads=rkeys("R9"), chan=f"ws{_wsi[0] % 4}")
        _wsi[0] += 1
        kb.op("gp", lambda e, d=dlo: e.dma_start(out=d, in_=spl_lo),
              reads=rkeys("R9"), chan=f"ws{_wsi[0] % 4}")
        _wsi[0] += 1

    def x_load(g):
        dst = A["R0"].rearrange("p (c d) -> p c d", d=D)
        kb.op("gp", lambda e, d=dst, s=xr[g]: e.dma_start(out=d, in_=s),
              writes=("R0",))

    stats = A["stats"]
    st_m, st_r = stats[:, 8:12], stats[:, 12:16]

    def ln_stats(src_keys, src_slab):
        sums, ssq = stats[:, 0:4], stats[:, 4:8]
        scr1, scr2 = stats[:, 16:20], stats[:, 20:24]
        for c in range(NCH):
            xc = src_slab[:, c * D:(c + 1) * D]
            kb.op("act", lambda e, xc=xc, c=c: e.activation(
                A["sqdump"], xc, AF.Square, accum_out=ssq[:, c:c + 1]),
                reads=src_keys,
                pre=(("st_ssq",) if c == 0 else ()) + ("sqd",) + scr_pre("sqd"),
                writes=(("st_ssq",) if c == 3 else ()) + ("sqd",), sig=True)
        kb.op("dve", lambda e: e.reduce_sum(
            sums, src_slab.rearrange("p (c d) -> p c d", d=D), axis=AX.X),
            reads=src_keys, pre=("st_sum",), writes=("st_sum",))
        kb.op("dve", lambda e: e.tensor_scalar(st_m, sums, 1.0 / D, None, ALU.mult),
              reads=("st_sum",), writes=("st_m",))
        kb.op("dve", lambda e: e.tensor_scalar(scr1, ssq, 1.0 / D, EPS,
                                               ALU.mult, ALU.add),
              reads=("st_ssq",), writes=("st_s1",))
        kb.op("act", lambda e: e.activation(scr2, st_m, AF.Square),
              reads=("st_m",), writes=("st_s2",))
        kb.op("dve", lambda e: e.tensor_sub(scr1, scr1, scr2),
              reads=("st_s1", "st_s2"), writes=("st_s1",))
        kb.op("act", lambda e: e.activation(scr2, scr1, AF.Ln),
              reads=("st_s1",), writes=("st_s2",))
        kb.op("act", lambda e: e.activation(st_r, scr2, AF.Exp, scale=-0.5),
              reads=("st_s2",), writes=("st_r",))

    def ln_norm(src_keys, src_slab, dst_region, dst_slab):
        for c in range(NCH):
            xc = src_slab[:, c * D:(c + 1) * D]
            oc = dst_slab[:, c * D:(c + 1) * D]
            wk = (f"{dst_region}:{2 * c}", f"{dst_region}:{2 * c + 1}") \
                if dst_region != "R0" else ("R0",)
            kb.op("dve", lambda e, xc=xc, oc=oc, c=c: e.tensor_scalar(
                oc, xc, st_m[:, c:c + 1], st_r[:, c:c + 1], ALU.subtract, ALU.mult),
                reads=src_keys, sreads=("st_m", "st_r"), writes=wk)

    def transpose_slab(src_keys, src_slab, dst_region, affine=None):
        for k in range(KC):
            bank = PS[6]
            for c in range(NCH):
                src = src_slab[:, c * D + k * 128: c * D + (k + 1) * 128]
                kb.op("pe", lambda e, s=src, c=c: e.transpose(
                    bank[:, c * 128:(c + 1) * 128], s, A["cid"][:, 0:128]),
                    reads=src_keys + ("cid",),
                    pre=("P6", "P6b") if c == 0 else (),
                    writes=("P6", "P6b") if c == 3 else (), sig=(c == 3))
            dk = (f"{dst_region}:{k}",)
            hid, lod = phi(dst_region, k), plo(dst_region, k)
            if affine is None:
                kb.op("act", lambda e, hid=hid: e.activation(
                    hid, bank[:, 0:512], AF.Copy),
                    reads=("P6", "P6b"), pre=dk, writes=dk)
                kb.op("dve", lambda e, hid=hid, lod=lod: e.scalar_tensor_tensor(
                    lod, bank[:, 0:512], 1.0, hid, ALU.mult, ALU.subtract),
                    reads=("P6", "P6b") + dk, writes=dk)
            else:
                gt, bt = affine
                aff = A["sqdump"][:, 0:512]
                kb.op("dve", lambda e, k=k, gt=gt, bt=bt: e.tensor_scalar(
                    aff, bank[:, 0:512], A[gt][:, k:k + 1], A[bt][:, k:k + 1],
                    ALU.mult, ALU.add),
                    reads=("P6", "P6b", gt, bt), pre=("aff",) + scr_pre("aff"),
                    writes=("aff",))
                kb.op("act", lambda e, hid=hid: e.activation(
                    hid, aff, AF.Copy),
                    reads=("aff",), pre=dk, writes=dk)
                kb.op("dve", lambda e, hid=hid, lod=lod: e.scalar_tensor_tensor(
                    lod, aff, 1.0, hid, ALU.mult, ALU.subtract),
                    reads=("aff",) + dk, writes=dk)

    def modeA(w_piece_srcs, in_region, in_slab, n_chunks, evac):
        for j, src in enumerate(w_piece_srcs):
            buf, rk = load_piece(src)
            for nsub in range(4):
                nchunk = 4 * j + nsub
                if nchunk >= n_chunks:
                    break
                bank, bkey = next_bank()
                for k in range(KC):
                    stat = buf[:, k * T + nsub * 128: k * T + (nsub + 1) * 128]
                    mov = in_slab[:, k * T:(k + 1) * T]
                    kb.op("pe", lambda e, st_=stat, mv=mov, k=k, bank=bank:
                          e.matmul(bank, st_, mv, start=(k == 0),
                                   stop=(k == KC - 1)),
                          reads=(rk, f"{in_region}:{k}"),
                          pre=(bkey,) if k == 0 else (),
                          writes=(bkey,) if k == KC - 1 else (),
                          sig=(k == KC - 1))
                evac(nchunk, bank, bkey)

    def modeB(w_piece_srcs, in_region, in_slab, evac, bias_row=None,
              resid=(), wextra=()):
        for nt, src in enumerate(w_piece_srcs):
            buf, rk = load_piece(src, reads=wextra)
            banks = [(PS[i], f"P{i}") for i in range(4)]
            tail = ([("bias", bias_row)] if bias_row is not None else []) + \
                   [("resid", rv) for rv in resid]
            for k in range(KC):
                for ts in range(NCH):
                    bank, bkey = banks[ts]
                    stat = in_slab[:, k * T + ts * 128: k * T + (ts + 1) * 128]
                    mov = buf[:, k * 512:(k + 1) * 512]
                    last = (k == KC - 1) and not tail
                    kb.op("pe", lambda e, st_=stat, mv=mov, bank=bank, k=k,
                          last=last: e.matmul(bank, st_, mv, start=(k == 0),
                                              stop=last),
                          reads=(rk, f"{in_region}:{k}"),
                          pre=(bkey,) if k == 0 else (),
                          writes=(bkey,) if last else (), sig=last)
            for ts in range(NCH):
                bank, bkey = banks[ts]
                for i, (kind, val) in enumerate(tail):
                    last = (i == len(tail) - 1)
                    if kind == "bias":
                        r = val
                        bp, bc = _BR[r]
                        kb.op("pe", lambda e, bank=bank, bp=bp, bc=bc, nt=nt,
                              last=last: e.matmul(
                                  bank, A["allones"][bp:bp + 1, 0:128],
                                  A["brows"][bp:bp + 1,
                                             bc + nt * 512:bc + (nt + 1) * 512],
                                  start=False, stop=last),
                              reads=("allones", f"brows{r}"),
                              writes=(bkey,) if last else (), sig=last)
                    else:
                        slab, skeys = val
                        mv = slab[:, ts * D + nt * 512: ts * D + (nt + 1) * 512]
                        kb.op("pe", lambda e, bank=bank, mv=mv, last=last:
                              e.matmul(bank, A["cid"][:, 0:128], mv,
                                       start=False, stop=last),
                              reads=("cid",) + tuple(skeys),
                              writes=(bkey,) if last else (), sig=last)
                evac(ts, nt, bank, bkey)

    def modeA3(w_srcs, in_region, n_chunks, evac):
        for j, (shi, slo) in enumerate(w_srcs):
            bh, bl, rk = load_piece3(shi, slo)
            for nsub in range(4):
                nchunk = 4 * j + nsub
                if nchunk >= n_chunks:
                    break
                bank, bkey = next_bank()
                for k in range(KC):
                    wh = bh[:, k, nsub * 128:(nsub + 1) * 128]
                    wl = bl[:, k, nsub * 128:(nsub + 1) * 128]
                    xh, xl = phi(in_region, k), plo(in_region, k)
                    ik = (rk, f"{in_region}:{k}")
                    kb.op("pe", lambda e, wh=wh, xh=xh, bank=bank, k=k:
                          e.matmul(bank, wh, xh, start=(k == 0), stop=False),
                          reads=ik, pre=(bkey,) if k == 0 else (), sig=False)
                    kb.op("pe", lambda e, wh=wh, xl=xl, bank=bank:
                          e.matmul(bank, wh, xl, start=False, stop=False),
                          reads=ik, sig=False)
                    kb.op("pe", lambda e, wl=wl, xh=xh, bank=bank, k=k:
                          e.matmul(bank, wl, xh, start=False,
                                   stop=(k == KC - 1)),
                          reads=ik, writes=(bkey,) if k == KC - 1 else (),
                          sig=(k == KC - 1))
                evac(nchunk, bank, bkey)

    def modeB3(w_srcs, in_region, evac, bias_row=None, resid=(), wextra=()):
        for nt, (shi, slo) in enumerate(w_srcs):
            bh, bl, rk = load_piece3(shi, slo, reads=wextra)
            banks = [(PS[i], f"P{i}") for i in range(4)]
            tail = ([("bias", bias_row)] if bias_row is not None else []) + \
                   [("resid", rv) for rv in resid]
            for k in range(KC):
                for ts in range(NCH):
                    bank, bkey = banks[ts]
                    xh = phi(in_region, k)[:, ts * 128:(ts + 1) * 128]
                    xl = plo(in_region, k)[:, ts * 128:(ts + 1) * 128]
                    mh, ml = bh[:, k, :], bl[:, k, :]
                    ik = (rk, f"{in_region}:{k}")
                    kb.op("pe", lambda e, xh=xh, mh=mh, bank=bank, k=k:
                          e.matmul(bank, xh, mh, start=(k == 0), stop=False),
                          reads=ik, pre=(bkey,) if k == 0 else (), sig=False)
                    kb.op("pe", lambda e, xh=xh, ml=ml, bank=bank:
                          e.matmul(bank, xh, ml, start=False, stop=False),
                          reads=ik, sig=False)
                    bank_last = (k == KC - 1 and not tail)
                    kb.op("pe", lambda e, xl=xl, mh=mh, bank=bank,
                          bank_last=bank_last: e.matmul(
                              bank, xl, mh, start=False, stop=bank_last),
                          reads=ik,
                          writes=(bkey,) if bank_last else (),
                          sig=(bank_last or (k == KC - 1 and ts == NCH - 1)))
            for ts in range(NCH):
                bank, bkey = banks[ts]
                for i, (kind, val) in enumerate(tail):
                    lastt = (i == len(tail) - 1)
                    if kind == "bias":
                        r = val
                        bp, bc = _BR[r]
                        kb.op("pe", lambda e, bank=bank, bp=bp, bc=bc, nt=nt,
                              lastt=lastt: e.matmul(
                                  bank, A["allones"][bp:bp + 1, 0:128],
                                  A["brows"][bp:bp + 1,
                                             bc + nt * 512:bc + (nt + 1) * 512],
                                  start=False, stop=lastt),
                              reads=("allones", f"brows{r}"),
                              writes=(bkey,) if lastt else (), sig=lastt)
                    else:
                        slab, skeys = val
                        mv = slab[:, ts * D + nt * 512: ts * D + (nt + 1) * 512]
                        kb.op("pe", lambda e, bank=bank, mv=mv, lastt=lastt:
                              e.matmul(bank, A["cid"][:, 0:128], mv,
                                       start=False, stop=lastt),
                              reads=("cid",) + tuple(skeys),
                              writes=(bkey,) if lastt else (), sig=lastt)
                evac(ts, nt, bank, bkey)

    def tap(name, slab, cols, reads):
        if name in tap_dr:
            kb.op("gp", lambda e: e.dma_start(out=tap_dr[name],
                                              in_=slab[:, 0:cols]),
                  reads=reads)

    vexr = A["R5"].rearrange("p (s q) -> p s q", s=5)

    def emit_group(g):
        is_first = (g == 0)

        ln_stats(("R0",), A["R0"])
        ln_norm(("R0",), A["R0"], "R1", A["R1"])
        tap(f"xn{g}", A["R1"], 4096, rkeys("R1"))
        transpose_slab(rkeys("R1"), A["R1"], "R2", affine=("g1t", "b1t"))

        def evac_qk(nchunk, bank, bkey):
            if nchunk < 8:
                dst, key, scale = (A["R3"][:, nchunk * T:(nchunk + 1) * T],
                                   f"R3:{nchunk}", 0.125)
            else:
                dst, key, scale = (A["R4"][:, (nchunk - 8) * T:(nchunk - 7) * T],
                                   f"R4:{nchunk - 8}", 1.0)
            kb.op("act", lambda e, d=dst, nchunk=nchunk, scale=scale:
                  e.activation(d, bank[:, 0:512], AF.Identity,
                               bias=A["bqk_a"][:, nchunk:nchunk + 1], scale=scale),
                  reads=(bkey, "bqk_a"), writes=(key,))
        modeA3([(wv["Wqkv_a_hi"][:, :, j * 512:(j + 1) * 512],
                 wv["Wqkv_a_lo"][:, :, j * 512:(j + 1) * 512])
                for j in range(4)], "R2", 16, evac_qk)

        def evac_v(ts, nt, bank, bkey):
            dst = A["R5"].rearrange("p (s h e) -> p s h e", s=5, e=65)[
                :, ts + 1, 8 * nt:8 * (nt + 1), 0:64]
            src = bank.rearrange("p (h e) -> p h e", e=64)[:, 0:8, :]
            kb.op("dve", lambda e, d=dst, s=src: e.tensor_copy(d, s),
                  reads=(bkey,), pre=(f"vex:{ts + 1}",),
                  writes=(f"vex:{ts + 1}",))
        modeB3([(wv["Wqkv_a_hi"][:, :, 2048 + nt * 512: 2048 + (nt + 1) * 512],
                 wv["Wqkv_a_lo"][:, :, 2048 + nt * 512: 2048 + (nt + 1) * 512])
                for nt in range(2)], "R2", evac_v, bias_row=0)

        for c in range(NCH):
            def swa_scores(h):
                j, base, hp = h // 2, (h % 2) * 64, h % 2
                sc, sck = PS[4 + hp], f"P{4 + hp}"
                q = A["R3"][base:base + 64, j * T + c * 128: j * T + (c + 1) * 128]
                if c == 0:
                    kp = A["kprev"][base:base + 64, j * 128:(j + 1) * 128]
                    kcur = A["R4"][base:base + 64, j * T: j * T + 128]
                    kb.op("pe", lambda e, q=q, kp=kp, sc=sc: e.matmul(
                        sc[:, 0:128], q, kp, start=True, stop=False),
                        reads=(f"R3:{j}", "kprev"), pre=(sck,), sig=False)
                    kb.op("pe", lambda e, q=q, kc_=kcur, sc=sc: e.matmul(
                        sc[:, 128:256], q, kc_, start=False, stop=False),
                        reads=(f"R3:{j}", f"R4:{j}"), sig=False)
                else:
                    kc_ = A["R4"][base:base + 64,
                                  j * T + (c - 1) * 128: j * T + (c + 1) * 128]
                    kb.op("pe", lambda e, q=q, kc_=kc_, sc=sc: e.matmul(
                        sc[:, 0:256], q, kc_, start=True, stop=False),
                        reads=(f"R3:{j}", f"R4:{j}"), pre=(sck,), sig=False)
                mkey = "mf16" if (is_first and c == 0) else "mb16"
                kb.op("pe", lambda e, m=A[mkey], sc=sc: e.matmul(
                    sc[:, 0:256], A["cid16"], m, start=False, stop=True),
                    reads=("cid16", mkey), writes=(sck,))

            def swa_exp(h):
                hp = h % 2
                sc, sck = PS[4 + hp], f"P{4 + hp}"
                nm = A["dens"][:, 0:1]
                kb.op("dve", lambda e, sc=sc, nm=nm: e.reduce_max(
                    nm, sc[:, 0:256], axis=AX.X, negate=True),
                    reads=(sck,), writes=("nm",))
                px = A["pexp"][:, hp * 256:hp * 256 + 256]
                kb.op("act", lambda e, sc=sc, px=px, nm=nm: e.activation(
                    px, sc[:, 0:256], AF.Exp, bias=nm, scale=1.0),
                    reads=(sck, "nm"), pre=(f"pexp{hp}",) + scr_pre(f"pexp{hp}"),
                    writes=(f"pexp{hp}",))

            def swa_trans(h):
                hp = h % 2
                px = A["pexp"][:, hp * 256:hp * 256 + 256]
                tb = PS[6]
                kb.op("pe", lambda e, px=px, tb=tb: e.transpose(
                    tb[:, 0:128], px[:, 0:128], A["cid"][:, 0:128]),
                    reads=(f"pexp{hp}", "cid"), pre=("P6",), sig=False)
                kb.op("pe", lambda e, px=px, tb=tb: e.transpose(
                    tb[:, 128:256], px[:, 128:256], A["cid"][:, 0:128]),
                    reads=(f"pexp{hp}", "cid"), writes=("P6",))
                pt = A["pT"][:, hp * 256:hp * 256 + 256]
                kb.op("act", lambda e, tb=tb, pt=pt: e.activation(
                    pt, tb[:, 0:256], AF.Copy),
                    reads=("P6",), pre=(f"pT{hp}",), writes=(f"pT{hp}",))

            def swa_av(h):
                hp = h % 2
                pt = A["pT"][:, hp * 256:hp * 256 + 256]
                av = PS[7]
                vprev = vexr[:, c, h * 65:h * 65 + 65]
                vcur = vexr[:, c + 1, h * 65:h * 65 + 65]
                kb.op("pe", lambda e, pt=pt, v=vprev, av=av: e.matmul(
                    av[:, 0:65], pt[:, 0:128], v, start=True, stop=False),
                    reads=(f"pT{hp}", f"vex:{c}", "R5ones"), pre=("P7",),
                    sig=False)
                kb.op("pe", lambda e, pt=pt, v=vcur, av=av: e.matmul(
                    av[:, 0:65], pt[:, 128:256], v, start=False, stop=True),
                    reads=(f"pT{hp}", f"vex:{c + 1}", "R5ones"), writes=("P7",))
                dent, rden = A["dens"][:, 1:2], A["dens"][:, 2:3]
                kb.op("dve", lambda e, av=av, dent=dent: e.tensor_copy(
                    dent, av[:, 64:65]), reads=("P7",), writes=("dent",))
                kb.op("dve", lambda e, dent=dent, rden=rden: e.reciprocal(
                    rden, dent), reads=("dent",), writes=("rden",))
                dst = A["R1"][:, c * D + h * 64: c * D + (h + 1) * 64]
                rkey = f"R1:{2 * c + h // 8}"
                kb.op("dve", lambda e, av=av, dst=dst, rden=rden: e.tensor_scalar(
                    dst, av[:, 0:64], rden, None, ALU.mult),
                    reads=("P7",), sreads=("rden",),
                    pre=(rkey,) if h % 8 == 0 else (), writes=(rkey,))

            swa_scores(0)
            swa_scores(1)
            for h in range(H):
                swa_exp(h)
                swa_trans(h)
                if h + 2 < H:
                    swa_scores(h + 2)
                swa_av(h)
        tap(f"swa{g}", A["R1"], 4096, rkeys("R1"))

        kpdst = A["kprev"].rearrange("p (j q) -> p j q", q=128)
        kpsrc = A["R4"].rearrange("p (j t) -> p j t", t=T)[:, :, 384:512]
        kb.op("dve", lambda e: e.tensor_copy(kpdst, kpsrc),
              reads=rkeys("R4"), writes=("kprev",))
        kb.op("dve", lambda e: e.tensor_copy(A["R5"][:, 0:1040],
                                             A["R5"][:, 4160:5200]),
              reads=("vex:4", "R5ones"), pre=("vex:0",), writes=("vex:0",))

        def evac_qk_s(nchunk, bank, bkey):
            isq = nchunk < 8
            jj = nchunk if isq else nchunk - 8
            slab = A["R3"] if isq else A["R4"]
            dst = slab[:, jj * T:(jj + 1) * T]
            key = f"R3:{jj}" if isq else f"R4:{jj}"
            scale = 0.125 if isq else 1.0
            scr = A["eluscr"]
            kb.op("dve", lambda e, d=dst, nchunk=nchunk, scale=scale:
                  e.tensor_scalar(d, bank[:, 0:512],
                                  A["bqk_s"][:, nchunk:nchunk + 1], scale,
                                  ALU.add, ALU.mult),
                  reads=(bkey, "bqk_s"), pre=(key,), writes=(key,))
            kb.op("act", lambda e, d=dst: e.activation(scr, d, AF.Exp),
                  reads=(key,), pre=("elusc",) + scr_pre("elusc"),
                  writes=("elusc",))
            kb.op("dve", lambda e: e.tensor_scalar(scr, scr, 1.0, 1e-4,
                                                   ALU.min, ALU.add),
                  reads=("elusc",), writes=("elusc",))
            kb.op("dve", lambda e, d=dst: e.scalar_tensor_tensor(
                d, d, 0.0, scr, ALU.max, ALU.add),
                reads=(key, "elusc"), writes=(key,))
        modeA3([(wv["Wqkv_s_hi"][:, :, j * 512:(j + 1) * 512],
                 wv["Wqkv_s_lo"][:, :, j * 512:(j + 1) * 512])
                for j in range(4)], "R2", 16, evac_qk_s)
        modeB3([(wv["Wqkv_s_hi"][:, :, 2048 + nt * 512: 2048 + (nt + 1) * 512],
                 wv["Wqkv_s_lo"][:, :, 2048 + nt * 512: 2048 + (nt + 1) * 512])
                for nt in range(2)], "R2", evac_v, bias_row=1)
        tap(f"qps{g}", A["R3"], 4096, rkeys("R3"))
        tap(f"kps{g}", A["R4"], 4096, rkeys("R4"))

        for c in range(NCH):
            def lin_at(h):
                j, base, hp = h // 2, (h % 2) * 64, h % 2
                qs = A["R3"][base:base + 64, j * T + c * 128: j * T + (c + 1) * 128]
                ks = A["R4"][base:base + 64, j * T + c * 128: j * T + (c + 1) * 128]
                at, atk = PS[4 + hp], f"P{4 + hp}"
                kb.op("pe", lambda e, ks=ks, qs=qs, at=at: e.matmul(
                    at[:, 0:128], ks, qs, start=True, stop=True),
                    reads=(f"R3:{j}", f"R4:{j}"), pre=(atk,), writes=(atk,))
                idb = A["cid"][0:64, 0:64] if base == 0 else \
                    A["cid"][64:128, 128:192]
                tb = PS[2 + hp]
                pk = f"P{2 + hp}"
                kb.op("pe", lambda e, ks=ks, idb=idb, tb=tb: e.transpose(
                    tb[:, 0:64], ks, idb),
                    reads=(f"R4:{j}", "cid"), pre=(pk,), writes=(pk,))

            def lin_mask(h):
                hp = h % 2
                at, atk = PS[4 + hp], f"P{4 + hp}"
                atm = A["ATm"][:, hp * 128:hp * 128 + 128]
                kb.op("dve", lambda e, at=at, atm=atm: e.scalar_tensor_tensor(
                    atm, at[:, 0:128], 1.0, A["triu"], ALU.mult, ALU.mult),
                    reads=(atk, "triu"), pre=(f"ATm{hp}",), writes=(f"ATm{hp}",))
                kt = A["kT"][:, hp * 64:hp * 64 + 64]
                pk = f"P{2 + hp}"
                kb.op("act", lambda e, kt=kt, hp=hp: e.activation(
                    kt, PS[2 + hp][:, 0:64], AF.Copy),
                    reads=(pk,), pre=(f"kT{hp}",), writes=(f"kT{hp}",))

            def lin_nd(h):
                j, base, hp = h // 2, (h % 2) * 64, h % 2
                qs = A["R3"][base:base + 64, j * T + c * 128: j * T + (c + 1) * 128]
                atm = A["ATm"][:, hp * 128:hp * 128 + 128]
                nd, ndk = PS[hp], f"P{hp}"
                sz = A["SZ"][base:base + 64, j * 65:(j + 1) * 65]
                vex = vexr[:, c + 1, h * 65:h * 65 + 65]
                kb.op("pe", lambda e, qs=qs, sz=sz, nd=nd: e.matmul(
                    nd[:, 0:65], qs, sz, start=True, stop=False),
                    reads=(f"R3:{j}", f"SZ:{h}"), pre=(ndk,), sig=False)
                kb.op("pe", lambda e, atm=atm, vex=vex, nd=nd: e.matmul(
                    nd[:, 0:65], atm, vex, start=False, stop=True),
                    reads=(f"ATm{hp}", f"vex:{c + 1}", "R5ones"), writes=(ndk,))
                kt = A["kT"][:, hp * 64:hp * 64 + 64]
                szp = PS[7]
                tp = None if base == 0 else (0, 64)
                kb.op("pe", lambda e, kt=kt, vex=vex, szp=szp, base=base, tp=tp:
                      e.matmul(szp[base:base + 64, 0:65], kt, vex,
                               start=True, stop=True, tile_position=tp),
                      reads=(f"kT{hp}", f"vex:{c + 1}", "R5ones"),
                      pre=("P7",), writes=("P7",))

            def lin_out(h):
                j, base, hp = h // 2, (h % 2) * 64, h % 2
                nd, ndk = PS[hp], f"P{hp}"
                dent, rden = A["dens"][:, 4:5], A["dens"][:, 5:6]
                kb.op("dve", lambda e, nd=nd, dent=dent: e.tensor_scalar(
                    dent, nd[:, 64:65], 1e-6, None, ALU.add),
                    reads=(ndk,), writes=("dent2",))
                kb.op("dve", lambda e, dent=dent, rden=rden: e.reciprocal(
                    rden, dent), reads=("dent2",), writes=("rden2",))
                dst = A["R10"][:, c * D + h * 64: c * D + (h + 1) * 64]
                rkey = f"R10:{2 * c + h // 8}"
                kb.op("dve", lambda e, nd=nd, dst=dst, rden=rden: e.tensor_scalar(
                    dst, nd[:, 0:64], rden, None, ALU.mult),
                    reads=(ndk,), sreads=("rden2",),
                    pre=(rkey,) if h % 8 == 0 else (), writes=(rkey,))
                sz = A["SZ"][base:base + 64, j * 65:(j + 1) * 65]
                szp = PS[7]
                kb.op("dve", lambda e, sz=sz, szp=szp, base=base: e.tensor_add(
                    sz, sz, szp[base:base + 64, 0:65]),
                    reads=("P7", f"SZ:{h}"), writes=(f"SZ:{h}",))

            lin_at(0)
            lin_at(1)
            for h in range(H):
                lin_mask(h)
                if h + 2 < H:
                    lin_at(h + 2)
                lin_nd(h)
                lin_out(h)
        tap(f"o{g}", A["R10"], 4096, rkeys("R10"))

        transpose_slab(rkeys("R1"), A["R1"], "R3")

        def evac_wp(nchunk, bank, bkey):
            dst = A["R9"][:, nchunk * T:(nchunk + 1) * T]
            kb.op("act", lambda e, d=dst, nchunk=nchunk: e.activation(
                d, bank[:, 0:512], AF.Identity,
                bias=A["bp_a"][:, nchunk:nchunk + 1]),
                reads=(bkey, "bp_a"), writes=(f"R9:{nchunk}",))
        modeA3([(wv["Wp_a_hi"][:, :, j * 512:(j + 1) * 512],
                 wv["Wp_a_lo"][:, :, j * 512:(j + 1) * 512])
                for j in range(2)], "R3", 8, evac_wp)

        transpose_slab(rkeys("R10"), A["R10"], "R4")

        def evac_wos(nchunk, bank, bkey):
            dst = A["R9"][:, nchunk * T:(nchunk + 1) * T]
            kb.op("dve", lambda e, d=dst, nchunk=nchunk: e.scalar_tensor_tensor(
                d, bank[:, 0:512], A["bo_s"][:, nchunk:nchunk + 1], d,
                ALU.add, ALU.add),
                reads=(bkey, "bo_s", f"R9:{nchunk}"), writes=(f"R9:{nchunk}",))
        modeA3([(wv["Wo_s_hi"][:, :, j * 512:(j + 1) * 512],
                 wv["Wo_s_lo"][:, :, j * 512:(j + 1) * 512])
                for j in range(2)], "R4", 8, evac_wos)

        for j in range(KC):
            a32 = A["R9"][:, j * T:(j + 1) * T]
            hid, lod = phi("R3", j), plo("R3", j)
            kb.op("act", lambda e, a32=a32, hid=hid: e.activation(
                hid, a32, AF.Copy),
                reads=(f"R9:{j}",), pre=(f"R3:{j}",), writes=(f"R3:{j}",))
            kb.op("dve", lambda e, a32=a32, hid=hid, lod=lod:
                  e.scalar_tensor_tensor(lod, a32, 1.0, hid,
                                         ALU.mult, ALU.subtract),
                  reads=(f"R9:{j}", f"R3:{j}"), writes=(f"R3:{j}",))

        def evac_wo(ts, nt, bank, bkey):
            dst = A["R10"][:, ts * D + nt * 512: ts * D + (nt + 1) * 512]
            kb.op("act", lambda e, d=dst, bank=bank: e.activation(
                d, bank, AF.Copy),
                reads=(bkey,), pre=(f"R10:{2 * ts + nt}",),
                writes=(f"R10:{2 * ts + nt}",))
        modeB3([(wv["Wo_hi"][:, :, nt * 512:(nt + 1) * 512],
                 wv["Wo_lo"][:, :, nt * 512:(nt + 1) * 512])
                for nt in range(2)], "R3", evac_wo, bias_row=2,
               resid=[(A["R0"], ("R0",))])
        tap(f"x1_{g}", A["R10"], 4096, rkeys("R10"))

        ln_stats(rkeys("R10"), A["R10"])
        ln_norm(rkeys("R10"), A["R10"], "R1", A["R1"])
        tap(f"h{g}", A["R1"], 4096, rkeys("R1"))
        transpose_slab(rkeys("R1"), A["R1"], "R2", affine=("gmt", "bmt"))

        gel_reg = ["R3", "R4", "R9", "R1"]
        gel_slab = [A["R3"], A["R4"], A["R9"], A["R1"]]

        def evac_w1(nchunk, bank, bkey):
            regn = gel_reg[nchunk // 8]
            kk = nchunk % 8
            gk = "g32a" if nchunk % 2 == 0 else "g32b"
            g32 = A["sqdump"][:, 0:512] if nchunk % 2 == 0 \
                else A["sqdump"][:, 512:1024]
            dk = (f"{regn}:{kk}",)
            hid, lod = phi(regn, kk), plo(regn, kk)
            kb.op("act", lambda e, g32=g32, nchunk=nchunk: e.activation(
                g32, bank[:, 0:512], AF.Gelu,
                bias=A["b1mt"][:, nchunk:nchunk + 1]),
                reads=(bkey, "b1mt"), pre=(gk,) + scr_pre(gk), writes=(gk,))
            kb.op("act", lambda e, g32=g32, hid=hid: e.activation(
                hid, g32, AF.Copy),
                reads=(gk,), pre=dk, writes=dk)
            kb.op("dve", lambda e, g32=g32, hid=hid, lod=lod:
                  e.scalar_tensor_tensor(lod, g32, 1.0, hid,
                                         ALU.mult, ALU.subtract),
                  reads=(gk,) + dk, writes=dk)
        modeA3([(wv["W1_hi"][:, :, j * 512:(j + 1) * 512],
                 wv["W1_lo"][:, :, j * 512:(j + 1) * 512])
                for j in range(8)], "R2", 32, evac_w1)
        tap(f"gela{g}", A["R3"], 4096, rkeys("R3"))
        tap(f"gelb{g}", A["R4"], 4096, rkeys("R4"))
        tap(f"gelc{g}", A["R9"], 4096, rkeys("R9"))
        tap(f"geld{g}", A["R1"], 4096, rkeys("R1"))

        for nt in range(2):
            banks = [(PS[i], f"P{i}") for i in range(4)]
            for kg in range(4):
                bh, bl, rk = load_piece3(
                    wv["W2_hi"][kg][:, :, nt * 512:(nt + 1) * 512],
                    wv["W2_lo"][kg][:, :, nt * 512:(nt + 1) * 512])
                for k in range(KC):
                    kk = kg * 8 + k
                    regn = gel_reg[kk // 8]
                    for ts in range(NCH):
                        bank, bkey = banks[ts]
                        xh = phi(regn, kk % 8)[:, ts * 128:(ts + 1) * 128]
                        xl = plo(regn, kk % 8)[:, ts * 128:(ts + 1) * 128]
                        mh, ml = bh[:, k, :], bl[:, k, :]
                        ik = (rk, f"{regn}:{kk % 8}")
                        first = (kg == 0 and k == 0)
                        kb.op("pe", lambda e, xh=xh, mh=mh, bank=bank,
                              first=first: e.matmul(bank, xh, mh,
                                                    start=first, stop=False),
                              reads=ik, pre=(bkey,) if first else (),
                              sig=False)
                        kb.op("pe", lambda e, xh=xh, ml=ml, bank=bank:
                              e.matmul(bank, xh, ml, start=False, stop=False),
                              reads=ik, sig=False)
                        kb.op("pe", lambda e, xl=xl, mh=mh, bank=bank:
                              e.matmul(bank, xl, mh, start=False, stop=False),
                              reads=ik,
                              sig=(k == KC - 1 and ts == NCH - 1))
            for ts in range(NCH):
                bank, bkey = banks[ts]
                kb.op("pe", lambda e, bank=bank, nt=nt: e.matmul(
                    bank, A["allones"][0:1, 0:128],
                    A["brows"][0:1, 1024 + nt * 512:1024 + (nt + 1) * 512],
                    start=False, stop=False),
                    reads=("allones", "brows3"), sig=False)
                mvx1 = A["R10"][:, ts * D + nt * 512: ts * D + (nt + 1) * 512]
                kb.op("pe", lambda e, bank=bank, mv=mvx1: e.matmul(
                    bank, A["cid"][:, 0:128], mv, start=False, stop=False),
                    reads=("cid", f"R10:{2 * ts + nt}"), sig=False)
                mvx = A["R0"][:, ts * D + nt * 512: ts * D + (nt + 1) * 512]
                kb.op("pe", lambda e, bank=bank, mv=mvx: e.matmul(
                    bank, A["cid"][:, 0:128], mv, start=False, stop=True),
                    reads=("cid", "R0"), writes=(bkey,))
                dst = A["R11"][:, ts * D + nt * 512: ts * D + (nt + 1) * 512]
                kb.op("act", lambda e, d=dst, bank=bank: e.activation(
                    d, bank, AF.Copy),
                    reads=(bkey,), pre=(f"R11:{2 * ts + nt}",),
                    writes=(f"R11:{2 * ts + nt}",))
        tap(f"t{g}", A["R11"], 4096, rkeys("R11"))

        ln_stats(rkeys("R11"), A["R11"])
        ln_norm(rkeys("R11"), A["R11"], "R1", A["R1"])
        transpose_slab(rkeys("R1"), A["R1"], "R2")

        def evac_re(ts, nt, bank, bkey):
            dst = A["R10"][:, ts * D + nt * 512: ts * D + (nt + 1) * 512]
            kb.op("act", lambda e, d=dst, bank=bank: e.activation(
                d, bank, AF.Square),
                reads=(bkey,), pre=(f"R10:{2 * ts + nt}",),
                writes=(f"R10:{2 * ts + nt}",))
        def evac_im(ts, nt, bank, bkey):
            dst = A["R9"][:, ts * D + nt * 512: ts * D + (nt + 1) * 512]
            kb.op("act", lambda e, d=dst, bank=bank: e.activation(
                d, bank, AF.Square),
                reads=(bkey,), pre=(f"R9:{2 * ts + nt}",),
                writes=(f"R9:{2 * ts + nt}",))
        modeB3([(wv["csc_hi"][:, :, nt * 512:(nt + 1) * 512],
                 wv["csc_lo"][:, :, nt * 512:(nt + 1) * 512])
                for nt in range(2)], "R2", evac_re,
               wextra=("csc_hi", "csc_lo"))
        modeB3([(wv["ssc_hi"][:, :, nt * 512:(nt + 1) * 512],
                 wv["ssc_lo"][:, :, nt * 512:(nt + 1) * 512])
                for nt in range(2)], "R2", evac_im,
               wextra=("ssc_hi", "ssc_lo"))
        kb.op("dve", lambda e: e.tensor_add(A["R3"][:, 0:4096], A["R10"],
                                            A["R9"]),
              reads=rkeys("R10") + rkeys("R9"), pre=rkeys("R3"),
              writes=rkeys("R3"))
        kb.op("act", lambda e: e.activation(A["R10"], A["R3"][:, 0:4096],
                                            AF.Sqrt),
              reads=rkeys("R3"), pre=rkeys("R10"), writes=rkeys("R10"))
        kb.op("dve", lambda e: e.tensor_add(A["R9"], A["R1"], A["R10"]),
              reads=rkeys("R10") + rkeys("R1"), pre=rkeys("R9"),
              writes=rkeys("R9"))
        ln_stats(rkeys("R9"), A["R9"])
        ln_norm(rkeys("R9"), A["R9"], "R11", A["R11"])
        src = A["R11"].rearrange("p (c d) -> p c d", d=D)
        kb.op("gp", lambda e, g=g: e.dma_start(out=outr[g], in_=src),
              reads=rkeys("R11"))

    x_load(0)
    for g in range(NG):
        emit_group(g)
        if g + 1 < NG:
            x_load(g + 1)


    with ExitStack() as sst:
        sems = {n: sst.enter_context(nc.semaphore(f"sem_{n}")) for n in KB.ALL}

        def make_body(eng_name):
            oplist = kb.ops[eng_name]
            mysem = sems[eng_name]
            mymul = 16 if eng_name in KB.DMAQ else 1

            def body(e):
                for fn, waits, selfwaits, sig, sigchan in oplist:
                    for p, v in waits:
                        e.wait_ge(sems[p], v * (16 if p in KB.DMAQ else 1))
                    for v in selfwaits:
                        e.wait_ge(mysem, v * mymul)
                    inst = fn(e)
                    if sig:
                        inst.then_inc(sems[sigchan],
                                      16 if sigchan in KB.DMAQ else 1)
                if eng_name == "gp":
                    for ch in KB.CHANS:
                        if kb.cnt[ch]:
                            e.wait_ge(sems[ch], kb.cnt[ch] * 16)
            return body

        with nc.Block() as blk:
            blk.tensor(make_body("pe"))
            blk.vector(make_body("dve"))
            blk.scalar(make_body("act"))
            blk.sync(make_body("sp"))
            blk.gpsimd(make_body("gp"))
    st.close()
    return nc


def make_const_inputs():
    cid = np.zeros((128, 192), np.float32)
    cid[:, 0:128] = np.eye(128, dtype=np.float32)
    cid[64:128, 128:192] = np.eye(64, dtype=np.float32)
    triu = np.triu(np.ones((128, 128), np.float32))
    r = np.arange(128)[:, None]
    s = np.arange(256)[None, :]
    band = (s >= r + 1) & (s <= r + 128)
    first = band & (s >= 128)
    mb = np.where(band, 0.0, -1e30).astype(np.float32)
    mf = np.where(first, 0.0, -1e30).astype(np.float32)
    return cid, triu, mb, mf


def shard_inputs(inputs, b, L):
    cid, triu, mb, mf = make_const_inputs()
    g = lambda n: np.ascontiguousarray(np.asarray(inputs[n], dtype=np.float32))
    bq_a, bq_s = g("L0_attn_bqkv_a"), g("L0_attn_bqkv_s")
    brows = np.zeros((4, 1024), np.float32)
    brows[0] = bq_a[2048:3072]
    brows[1] = bq_s[2048:3072]
    brows[2] = g("L0_attn_bo")
    brows[3] = g("L0_attn_b2m")
    return {
        "x": np.ascontiguousarray(np.asarray(inputs["L0_attn_x"],
                                             dtype=np.float32)[b, :L]),
        "Wqkv_a": g("L0_attn_Wqkv_a"), "Wp_a": g("L0_attn_Wp_a"),
        "Wqkv_s": g("L0_attn_Wqkv_s"), "Wo_s": g("L0_attn_Wo_s"),
        "Wo": g("L0_attn_Wo"), "W1": g("L0_attn_W1"), "W2": g("L0_attn_W2"),
        "th": g("L0_ffn_theta"), "thn": g("L0_ffn_theta_noise"),
        "ph": g("L0_ffn_phi"), "phn": g("L0_ffn_phi_noise"),
        "cid": cid, "triu": triu, "mb": mb, "mf": mf,
        "cid16": cid[:, 0:128].astype(ml_dtypes.bfloat16),
        "mb16": mb.astype(ml_dtypes.bfloat16),
        "mf16": mf.astype(ml_dtypes.bfloat16),
        "bqk_a": np.ascontiguousarray(bq_a[0:2048].reshape(16, 128).T),
        "bqk_s": np.ascontiguousarray(bq_s[0:2048].reshape(16, 128).T),
        "bp_a": np.ascontiguousarray(g("L0_attn_bp_a").reshape(8, 128).T),
        "bo_s": np.ascontiguousarray(g("L0_attn_bo_s").reshape(8, 128).T),
        "b1mt": np.ascontiguousarray(g("L0_attn_b1m").reshape(32, 128).T),
        "g1t": np.ascontiguousarray(g("L0_attn_g1").reshape(8, 128).T),
        "b1t": np.ascontiguousarray(g("L0_attn_b1").reshape(8, 128).T),
        "gmt": np.ascontiguousarray(g("L0_attn_gm").reshape(8, 128).T),
        "bmt": np.ascontiguousarray(g("L0_attn_bm").reshape(8, 128).T),
        "brows": brows,
    }


def kernel(**inputs):
    L = 4096
    nc = build_nc(L)
    in_maps = [shard_inputs(inputs, b, L) for b in range(8)]
    res = run_bass_kernel_spmd(nc, in_maps, core_ids=list(range(8)))
    out = np.stack([np.asarray(r["out"]).reshape(L, D) for r in res.results])
    return out.astype(np.float32)


# version 04
# speedup vs baseline: 1.1404x; 1.0203x over previous
import ml_dtypes
import numpy as np
from contextlib import ExitStack

import concourse.bass as bass
import concourse.mybir as mybir
from concourse.bass_utils import run_bass_kernel_spmd

F32 = mybir.dt.float32
BF16 = mybir.dt.bfloat16
AF = mybir.ActivationFunctionType
ALU = mybir.AluOpType
AX = mybir.AxisListType

D, H, E = 1024, 16, 64
T, NCH, KC = 512, 4, 8
EPS = 1e-5

_AR = {}
_off = 0
def _reg(name, cols):
    global _off
    _AR[name] = (_off, cols)
    _off += (cols + 15) // 16 * 16
for _n, _c in [
    ("R0", 4096), ("R1", 4096), ("R2", 4096), ("R3", 4096), ("R4", 4096),
    ("R5", 5200), ("R9", 4096), ("R10", 4096), ("R11", 4096),
    ("ring0", 4096), ("ring1", 4096),
    ("kprev", 1024), ("pT", 512), ("ATm", 256), ("kT", 192),
    ("SZ", 520), ("sqdump", 1024), ("stats", 32),
    ("dens", 16),
    ("brows", 2048), ("allones", 128), ("cid", 192), ("triu", 128),
    ("m16", 320),
    ("bqk_a", 16), ("bqk_s", 16), ("bp_a", 8), ("bo_s", 8), ("b1mt", 32),
    ("g1t", 8), ("b1t", 8), ("gmt", 8), ("bmt", 8),
]:
    _reg(_n, _c)
ARENA_COLS = _off
assert ARENA_COLS <= 53200, ARENA_COLS


def rkeys(region, n=8):
    return tuple(f"{region}:{i}" for i in range(n))


class KB:
    ENGS = ("pe", "dve", "act", "sp", "gp")
    CHANS = ("sp", "gp", "r0", "r1", "ws0", "ws1", "ws2", "ws3")
    DMAQ = CHANS
    ALL = ENGS + ("r0", "r1", "ws0", "ws1", "ws2", "ws3")

    def __init__(self):
        self.ops = {n: [] for n in self.ENGS}
        self.cnt = {n: 0 for n in self.ALL}
        self.waited = {c: {p: 0 for p in self.ALL} for c in self.ENGS}
        self.reg = {}
        self.pend = {n: [] for n in self.ENGS}

    def _need(self, eng, waits, selfwaits, p, v):
        if v is None:
            raise RuntimeError("dependency on an unsignalled op")
        if p == eng:
            if eng == "pe":
                return
            if v > self.waited[eng][eng]:
                selfwaits.append(v)
                self.waited[eng][eng] = v
            return
        if p in self.CHANS:
            v = self.cnt[p]
        if v > self.waited[eng][p]:
            waits.append((p, v))
            self.waited[eng][p] = v

    def op(self, eng, fn, reads=(), writes=(), sreads=(), pre=(), sig=True,
           chan=None):
        sigchan = chan or eng
        waits, selfwaits = [], []
        if sigchan in self.CHANS and self.cnt[sigchan] > 0:
            self._need(eng, waits, selfwaits, sigchan, self.cnt[sigchan])
        for k in tuple(reads) + tuple(sreads):
            r = self.reg.get(k)
            if r and r["w"] is not None:
                self._need(eng, waits, selfwaits, *r["w"])
        for k in tuple(writes) + tuple(pre):
            r = self.reg.get(k)
            if r:
                if r["w"] is not None:
                    self._need(eng, waits, selfwaits, *r["w"])
                for p, v in r["r"].items():
                    self._need(eng, waits, selfwaits, p, v)
        if sig:
            self.cnt[sigchan] += 1
            val = self.cnt[sigchan]
            for k in self.pend[eng]:
                rr = self.reg.setdefault(k, {"w": None, "r": {}})
                rr["r"][eng] = val if sigchan == eng else self.cnt[eng]
            self.pend[eng].clear()
            for k in tuple(reads) + tuple(sreads):
                rr = self.reg.setdefault(k, {"w": None, "r": {}})
                rr["r"][sigchan] = val
        else:
            val = None
            for k in tuple(reads) + tuple(sreads):
                self.pend[eng].append(k)
        for k in writes:
            if not sig:
                raise RuntimeError("writes require sig=True")
            self.reg[k] = {"w": (sigchan, val), "r": {}}
        self.ops[eng].append((fn, waits, selfwaits, sig, sigchan))


def build_nc(L, taps=()):
    NG = L // T
    nc = bass.Bass("TRN2", target_bir_lowering=False, debug=False)

    dr = {}
    dr["x"] = nc.dram_tensor("x", [L, D], F32, kind="ExternalInput").ap()
    for n, shp in [
        ("Wqkv_a", [D, 3 * D]), ("Wp_a", [D, D]), ("Wqkv_s", [D, 3 * D]),
        ("Wo_s", [D, D]), ("Wo", [D, D]), ("W1", [D, 4 * D]), ("W2", [4 * D, D]),
        ("th", [D, D]), ("thn", [D, D]), ("ph", [D, D]), ("phn", [D, D]),
        ("cid", [128, 192]), ("triu", [128, 128]), ("mb", [128, 256]),
        ("mf", [128, 256]), ("bqk_a", [128, 16]), ("bqk_s", [128, 16]),
        ("bp_a", [128, 8]), ("bo_s", [128, 8]), ("b1mt", [128, 32]),
        ("g1t", [128, 8]), ("b1t", [128, 8]), ("gmt", [128, 8]), ("bmt", [128, 8]),
        ("brows", [4, 1024]),
    ]:
        dr[n] = nc.dram_tensor(n, shp, F32, kind="ExternalInput").ap()
    for n, shp in [("cid16", [128, 128]), ("mb16", [128, 256]),
                   ("mf16", [128, 256])]:
        dr[n] = nc.dram_tensor(n, shp, BF16, kind="ExternalInput").ap()
    dr["out"] = nc.dram_tensor("out", [L, D], F32, kind="ExternalOutput").ap()
    for w, shp in [("Wqkv_a", [D, 3 * D]), ("Wqkv_s", [D, 3 * D]),
                   ("Wp_a", [D, D]), ("Wo_s", [D, D]), ("Wo", [D, D]),
                   ("W1", [D, 4 * D]), ("W2", [4 * D, D]),
                   ("csc", [D, D]), ("ssc", [D, D])]:
        dr[w + "_hi"] = nc.dram_tensor(w + "_hi", shp, BF16, kind="Internal").ap()
        dr[w + "_lo"] = nc.dram_tensor(w + "_lo", shp, BF16, kind="Internal").ap()
    tap_dr = {}
    for tname, tcols in taps:
        tap_dr[tname] = nc.dram_tensor("tap_" + tname, [128, tcols], F32,
                                       kind="ExternalOutput").ap()

    st = ExitStack()
    arena = st.enter_context(nc.sbuf_tensor("arena", [128, ARENA_COLS], F32)).ap()
    A = {n: arena[:, o:o + c] for n, (o, c) in _AR.items()}
    A["eluscr"] = A["sqdump"][:, 0:512]
    _m16 = A["m16"].bitcast(BF16)
    A["cid16"] = _m16[:, 0:128]
    A["mb16"] = _m16[:, 128:384]
    A["mf16"] = _m16[:, 384:640]
    A["pexp"] = A["sqdump"][:, 0:512]
    SCRK = ("sqd", "elusc", "pexp0", "pexp1", "aff", "g32a", "g32b")
    def scr_pre(own):
        return tuple(k for k in SCRK if k != own)
    PS = [st.enter_context(nc.psum_tensor(f"pb{i}", [128, 512], F32)).ap()
          for i in range(8)]

    kb = KB()

    xr = dr["x"].rearrange("(g c p) d -> g p c d", c=NCH, p=128)
    outr = dr["out"].rearrange("(g c p) d -> g p c d", c=NCH, p=128)
    wv = {}
    for n in ("Wqkv_a", "Wqkv_s", "Wp_a", "Wo_s", "Wo", "W1"):
        wv[n] = dr[n].rearrange("(kc p) n -> p kc n", p=128)
    wv["W2"] = dr["W2"].rearrange("(kg kc p) n -> kg p kc n", kg=4, p=128)
    for w in ("Wqkv_a", "Wqkv_s", "Wp_a", "Wo_s", "Wo", "W1", "csc", "ssc"):
        for h in ("_hi", "_lo"):
            wv[w + h] = dr[w + h].rearrange("(kc p) n -> p kc n", p=128)
    for h in ("_hi", "_lo"):
        wv["W2" + h] = dr["W2" + h].rearrange("(kg kc p) n -> kg p kc n",
                                              kg=4, p=128)

    def phi(region, k):
        bc = A[region].bitcast(BF16)
        return bc[:, k * 512:(k + 1) * 512]

    def plo(region, k):
        bc = A[region].bitcast(BF16)
        return bc[:, 4096 + k * 512: 4096 + (k + 1) * 512]

    ring = [A["ring0"], A["ring1"]]
    ring_i = [0]

    def load_piece(src_ap, ncols=512, reads=()):
        i = ring_i[0] % 2
        ring_i[0] += 1
        buf = ring[i]
        key = f"ring{i}"
        dst = buf.rearrange("p (kc n) -> p kc n", n=ncols)
        kb.op("sp", lambda e, d=dst, s=src_ap: e.dma_start(out=d, in_=s),
              reads=reads, writes=(key,), chan=f"r{i}")
        return buf, key

    def load_piece3(src_hi, src_lo, reads=()):
        i = ring_i[0] % 2
        ring_i[0] += 1
        bc = ring[i].bitcast(BF16)
        key = f"ring{i}"
        bh = bc[:, 0:4096].rearrange("p (kc n) -> p kc n", n=512)
        bl = bc[:, 4096:8192].rearrange("p (kc n) -> p kc n", n=512)
        kb.op("sp", lambda e, s=src_hi: e.dma_start(out=bh, in_=s),
              reads=reads, writes=(key,), chan=f"r{i}")
        kb.op("sp", lambda e, s=src_lo: e.dma_start(out=bl, in_=s),
              reads=reads, writes=(key,), chan=f"r{i}")
        return bh, bl, key

    pj = [0]

    def next_bank():
        i = pj[0] % 2
        pj[0] += 1
        return PS[i], f"P{i}"

    _wsi = [0]
    for n in ("cid", "triu", "cid16", "mb16", "mf16",
              "bqk_a", "bqk_s", "bp_a", "bo_s",
              "b1mt", "g1t", "b1t", "gmt", "bmt"):
        kb.op("sp", lambda e, d=A[n], s=dr[n]: e.dma_start(out=d, in_=s),
              writes=(n,))
    _BR = {0: (0, 0), 1: (32, 0), 2: (64, 0), 3: (0, 1024)}
    for r in range(4):
        bp, bc = _BR[r]
        kb.op("sp", lambda e, d=A["brows"][bp:bp + 1, bc:bc + 1024],
              s=dr["brows"][r:r + 1, :]: e.dma_start(out=d, in_=s),
              writes=(f"brows{r}",))
    kb.op("dve", lambda e: e.memset(A["allones"], 1.0), writes=("allones",))
    kb.op("dve", lambda e: e.memset(A["SZ"], 0.0),
          writes=tuple(f"SZ:{h}" for h in range(H)))
    kb.op("dve", lambda e: e.memset(A["kprev"], 0.0), writes=("kprev",))
    kb.op("dve", lambda e: e.memset(A["R5"], 0.0),
          writes=tuple(f"vex:{s}" for s in range(5)))
    vext_ones = A["R5"].rearrange("p (g e) -> p g e", e=65)[:, :, 64:65]
    kb.op("dve", lambda e: e.memset(vext_ones, 1.0),
          pre=tuple(f"vex:{s_}" for s_ in range(5)), writes=("R5ones",))
    kb.op("dve", lambda e: e.tensor_scalar(A["bqk_a"][:, 0:8], A["bqk_a"][:, 0:8],
                                           0.125, None, ALU.mult),
          reads=("bqk_a",), writes=("bqk_a",))

    spanA = arena[:, _AR["R0"][0]:_AR["R0"][0] + 8192]
    spanB = arena[:, _AR["R2"][0]:_AR["R2"][0] + 8192]
    kA = ("R0",) + rkeys("R1")
    kB = rkeys("R2") + rkeys("R3")
    for src, nsrc, scr, iscos in (("th", "thn", "csc", True),
                                  ("ph", "phn", "ssc", False)):
        srcA = dr[src].rearrange("(kc p) n -> p kc n", p=128)
        srcB = dr[nsrc].rearrange("(kc p) n -> p kc n", p=128)
        dstA = spanA.rearrange("p (kc n) -> p kc n", n=1024)
        dstB = spanB.rearrange("p (kc n) -> p kc n", n=1024)
        kb.op("sp", lambda e, d=dstA, s=srcA: e.dma_start(out=d, in_=s), writes=kA)
        kb.op("sp", lambda e, d=dstB, s=srcB: e.dma_start(out=d, in_=s), writes=kB)
        kb.op("dve", lambda e: e.tensor_add(spanA, spanA, spanB),
              reads=kA + kB, writes=kA)
        if iscos:
            kb.op("act", lambda e: e.activation(spanB, spanA, AF.Sin, scale=0.5),
                  reads=kA, writes=kB)
            kb.op("act", lambda e: e.activation(spanA, spanB, AF.Square),
                  reads=kB, writes=kA)
            kb.op("dve", lambda e: e.tensor_scalar(spanA, spanA, -2.0, 1.0,
                                                   ALU.mult, ALU.add),
                  reads=kA, writes=kA)
        else:
            kb.op("act", lambda e: e.activation(spanA, spanA, AF.Sin, scale=1.0),
                  reads=kA, writes=kA)
        bcB = spanB.bitcast(BF16)
        hiv, lov = bcB[:, 0:8192], bcB[:, 8192:16384]
        kb.op("act", lambda e: e.activation(hiv, spanA, AF.Copy),
              reads=kA, writes=kB)
        kb.op("dve", lambda e: e.scalar_tensor_tensor(
            lov, spanA, 1.0, hiv, ALU.mult, ALU.subtract),
            reads=kA + kB, writes=kB)
        for h, v in (("_hi", hiv), ("_lo", lov)):
            dstv = wv[scr + h]
            srcv = v.rearrange("p (kc n) -> p kc n", n=1024)
            kb.op("gp", lambda e, d=dstv, s=srcv: e.dma_start(out=d, in_=s),
                  reads=kB, writes=(scr + h,), chan=f"ws{_wsi[0] % 4}")
            _wsi[0] += 1

    _splits = []
    for w, npieces in (("Wqkv_a", 6), ("Wqkv_s", 6), ("Wp_a", 2), ("Wo_s", 2),
                       ("Wo", 2), ("W1", 8)):
        for j in range(npieces):
            _splits.append((wv[w][:, :, j * 512:(j + 1) * 512],
                            wv[w + "_hi"][:, :, j * 512:(j + 1) * 512],
                            wv[w + "_lo"][:, :, j * 512:(j + 1) * 512]))
    for kg in range(4):
        for nt in range(2):
            _splits.append((wv["W2"][kg][:, :, nt * 512:(nt + 1) * 512],
                            wv["W2_hi"][kg][:, :, nt * 512:(nt + 1) * 512],
                            wv["W2_lo"][kg][:, :, nt * 512:(nt + 1) * 512]))
    spl_bc = A["R9"].bitcast(BF16)
    spl_hi = spl_bc[:, 0:4096].rearrange("p (kc n) -> p kc n", n=512)
    spl_lo = spl_bc[:, 4096:8192].rearrange("p (kc n) -> p kc n", n=512)
    for src, dhi, dlo in _splits:
        buf, rk = load_piece(src)
        b3 = buf.rearrange("p (kc n) -> p kc n", n=512)
        kb.op("act", lambda e, b3=b3: e.activation(spl_hi, b3, AF.Copy),
              reads=(rk,), pre=rkeys("R9"), writes=rkeys("R9"))
        kb.op("dve", lambda e, b3=b3: e.scalar_tensor_tensor(
            spl_lo, b3, 1.0, spl_hi, ALU.mult, ALU.subtract),
            reads=(rk,) + rkeys("R9"), writes=rkeys("R9"))
        kb.op("gp", lambda e, d=dhi: e.dma_start(out=d, in_=spl_hi),
              reads=rkeys("R9"), chan=f"ws{_wsi[0] % 4}")
        _wsi[0] += 1
        kb.op("gp", lambda e, d=dlo: e.dma_start(out=d, in_=spl_lo),
              reads=rkeys("R9"), chan=f"ws{_wsi[0] % 4}")
        _wsi[0] += 1

    def x_load(g):
        dst = A["R0"].rearrange("p (c d) -> p c d", d=D)
        kb.op("gp", lambda e, d=dst, s=xr[g]: e.dma_start(out=d, in_=s),
              writes=("R0",))

    stats = A["stats"]
    st_m, st_r = stats[:, 8:12], stats[:, 12:16]

    def ln_stats(src_keys, src_slab):
        sums, ssq = stats[:, 0:4], stats[:, 4:8]
        scr1, scr2 = stats[:, 16:20], stats[:, 20:24]
        for c in range(NCH):
            xc = src_slab[:, c * D:(c + 1) * D]
            kb.op("act", lambda e, xc=xc, c=c: e.activation(
                A["sqdump"], xc, AF.Square, accum_out=ssq[:, c:c + 1]),
                reads=src_keys,
                pre=(("st_ssq",) if c == 0 else ()) + ("sqd",) + scr_pre("sqd"),
                writes=(("st_ssq",) if c == 3 else ()) + ("sqd",), sig=True)
        kb.op("dve", lambda e: e.reduce_sum(
            sums, src_slab.rearrange("p (c d) -> p c d", d=D), axis=AX.X),
            reads=src_keys, pre=("st_sum",), writes=("st_sum",))
        kb.op("dve", lambda e: e.tensor_scalar(st_m, sums, 1.0 / D, None, ALU.mult),
              reads=("st_sum",), writes=("st_m",))
        kb.op("dve", lambda e: e.tensor_scalar(scr1, ssq, 1.0 / D, EPS,
                                               ALU.mult, ALU.add),
              reads=("st_ssq",), writes=("st_s1",))
        kb.op("act", lambda e: e.activation(scr2, st_m, AF.Square),
              reads=("st_m",), writes=("st_s2",))
        kb.op("dve", lambda e: e.tensor_sub(scr1, scr1, scr2),
              reads=("st_s1", "st_s2"), writes=("st_s1",))
        kb.op("act", lambda e: e.activation(scr2, scr1, AF.Ln),
              reads=("st_s1",), writes=("st_s2",))
        kb.op("act", lambda e: e.activation(st_r, scr2, AF.Exp, scale=-0.5),
              reads=("st_s2",), writes=("st_r",))

    def ln_norm(src_keys, src_slab, dst_region, dst_slab):
        for c in range(NCH):
            xc = src_slab[:, c * D:(c + 1) * D]
            oc = dst_slab[:, c * D:(c + 1) * D]
            wk = (f"{dst_region}:{2 * c}", f"{dst_region}:{2 * c + 1}") \
                if dst_region != "R0" else ("R0",)
            kb.op("dve", lambda e, xc=xc, oc=oc, c=c: e.tensor_scalar(
                oc, xc, st_m[:, c:c + 1], st_r[:, c:c + 1], ALU.subtract, ALU.mult),
                reads=src_keys, sreads=("st_m", "st_r"), writes=wk)

    def transpose_slab(src_keys, src_slab, dst_region, affine=None):
        for k in range(KC):
            bi = [6, 2, 3][k % 3]
            bank, bk = PS[bi], f"P{bi}"
            for c in range(NCH):
                src = src_slab[:, c * D + k * 128: c * D + (k + 1) * 128]
                kb.op("pe", lambda e, s=src, c=c, bank=bank: e.transpose(
                    bank[:, c * 128:(c + 1) * 128], s, A["cid"][:, 0:128]),
                    reads=src_keys + ("cid",),
                    pre=(bk,) if c == 0 else (),
                    writes=(bk,) if c == 3 else (), sig=(c == 3))
            dk = (f"{dst_region}:{k}",)
            hid, lod = phi(dst_region, k), plo(dst_region, k)
            if affine is None:
                kb.op("act", lambda e, hid=hid, bank=bank: e.activation(
                    hid, bank[:, 0:512], AF.Copy),
                    reads=(bk,), pre=dk, writes=dk)
                kb.op("dve", lambda e, hid=hid, lod=lod, bank=bank:
                      e.scalar_tensor_tensor(
                          lod, bank[:, 0:512], 1.0, hid, ALU.mult, ALU.subtract),
                      reads=(bk,) + dk, writes=dk)
            else:
                gt, bt = affine
                ak = "aff" if k % 2 == 0 else "g32a"
                aff = A["sqdump"][:, 0:512] if k % 2 == 0 \
                    else A["sqdump"][:, 512:1024]
                kb.op("dve", lambda e, k=k, gt=gt, bt=bt, aff=aff, bank=bank:
                      e.tensor_scalar(
                          aff, bank[:, 0:512], A[gt][:, k:k + 1],
                          A[bt][:, k:k + 1], ALU.mult, ALU.add),
                      reads=(bk, gt, bt), pre=(ak,) + scr_pre(ak),
                      writes=(ak,))
                kb.op("act", lambda e, hid=hid, aff=aff: e.activation(
                    hid, aff, AF.Copy),
                    reads=(ak,), pre=dk, writes=dk)
                kb.op("dve", lambda e, hid=hid, lod=lod, aff=aff:
                      e.scalar_tensor_tensor(
                          lod, aff, 1.0, hid, ALU.mult, ALU.subtract),
                      reads=(ak,) + dk, writes=dk)

    def modeA(w_piece_srcs, in_region, in_slab, n_chunks, evac):
        for j, src in enumerate(w_piece_srcs):
            buf, rk = load_piece(src)
            for nsub in range(4):
                nchunk = 4 * j + nsub
                if nchunk >= n_chunks:
                    break
                bank, bkey = next_bank()
                for k in range(KC):
                    stat = buf[:, k * T + nsub * 128: k * T + (nsub + 1) * 128]
                    mov = in_slab[:, k * T:(k + 1) * T]
                    kb.op("pe", lambda e, st_=stat, mv=mov, k=k, bank=bank:
                          e.matmul(bank, st_, mv, start=(k == 0),
                                   stop=(k == KC - 1)),
                          reads=(rk, f"{in_region}:{k}"),
                          pre=(bkey,) if k == 0 else (),
                          writes=(bkey,) if k == KC - 1 else (),
                          sig=(k == KC - 1))
                evac(nchunk, bank, bkey)

    def modeB(w_piece_srcs, in_region, in_slab, evac, bias_row=None,
              resid=(), wextra=()):
        for nt, src in enumerate(w_piece_srcs):
            buf, rk = load_piece(src, reads=wextra)
            banks = [(PS[i], f"P{i}") for i in range(4)]
            tail = ([("bias", bias_row)] if bias_row is not None else []) + \
                   [("resid", rv) for rv in resid]
            for k in range(KC):
                for ts in range(NCH):
                    bank, bkey = banks[ts]
                    stat = in_slab[:, k * T + ts * 128: k * T + (ts + 1) * 128]
                    mov = buf[:, k * 512:(k + 1) * 512]
                    last = (k == KC - 1) and not tail
                    kb.op("pe", lambda e, st_=stat, mv=mov, bank=bank, k=k,
                          last=last: e.matmul(bank, st_, mv, start=(k == 0),
                                              stop=last),
                          reads=(rk, f"{in_region}:{k}"),
                          pre=(bkey,) if k == 0 else (),
                          writes=(bkey,) if last else (), sig=last)
            for ts in range(NCH):
                bank, bkey = banks[ts]
                for i, (kind, val) in enumerate(tail):
                    last = (i == len(tail) - 1)
                    if kind == "bias":
                        r = val
                        bp, bc = _BR[r]
                        kb.op("pe", lambda e, bank=bank, bp=bp, bc=bc, nt=nt,
                              last=last: e.matmul(
                                  bank, A["allones"][bp:bp + 1, 0:128],
                                  A["brows"][bp:bp + 1,
                                             bc + nt * 512:bc + (nt + 1) * 512],
                                  start=False, stop=last),
                              reads=("allones", f"brows{r}"),
                              writes=(bkey,) if last else (), sig=last)
                    else:
                        slab, skeys = val
                        mv = slab[:, ts * D + nt * 512: ts * D + (nt + 1) * 512]
                        kb.op("pe", lambda e, bank=bank, mv=mv, last=last:
                              e.matmul(bank, A["cid"][:, 0:128], mv,
                                       start=False, stop=last),
                              reads=("cid",) + tuple(skeys),
                              writes=(bkey,) if last else (), sig=last)
                evac(ts, nt, bank, bkey)

    def modeA3(w_srcs, in_region, n_chunks, evac):
        for j, (shi, slo) in enumerate(w_srcs):
            bh, bl, rk = load_piece3(shi, slo)
            for nsub in range(4):
                nchunk = 4 * j + nsub
                if nchunk >= n_chunks:
                    break
                bank, bkey = next_bank()
                for k in range(KC):
                    wh = bh[:, k, nsub * 128:(nsub + 1) * 128]
                    wl = bl[:, k, nsub * 128:(nsub + 1) * 128]
                    xh, xl = phi(in_region, k), plo(in_region, k)
                    ik = (rk, f"{in_region}:{k}")
                    kb.op("pe", lambda e, wh=wh, xh=xh, bank=bank, k=k:
                          e.matmul(bank, wh, xh, start=(k == 0), stop=False),
                          reads=ik, pre=(bkey,) if k == 0 else (), sig=False)
                    kb.op("pe", lambda e, wh=wh, xl=xl, bank=bank:
                          e.matmul(bank, wh, xl, start=False, stop=False),
                          reads=ik, sig=False)
                    kb.op("pe", lambda e, wl=wl, xh=xh, bank=bank, k=k:
                          e.matmul(bank, wl, xh, start=False,
                                   stop=(k == KC - 1)),
                          reads=ik, writes=(bkey,) if k == KC - 1 else (),
                          sig=(k == KC - 1))
                evac(nchunk, bank, bkey)

    def modeB3(w_srcs, in_region, evac, bias_row=None, resid=(), wextra=()):
        for nt, (shi, slo) in enumerate(w_srcs):
            bh, bl, rk = load_piece3(shi, slo, reads=wextra)
            banks = [(PS[i], f"P{i}") for i in range(4)]
            tail = ([("bias", bias_row)] if bias_row is not None else []) + \
                   [("resid", rv) for rv in resid]
            for k in range(KC):
                for ts in range(NCH):
                    bank, bkey = banks[ts]
                    xh = phi(in_region, k)[:, ts * 128:(ts + 1) * 128]
                    xl = plo(in_region, k)[:, ts * 128:(ts + 1) * 128]
                    mh, ml = bh[:, k, :], bl[:, k, :]
                    ik = (rk, f"{in_region}:{k}")
                    kb.op("pe", lambda e, xh=xh, mh=mh, bank=bank, k=k:
                          e.matmul(bank, xh, mh, start=(k == 0), stop=False),
                          reads=ik, pre=(bkey,) if k == 0 else (), sig=False)
                    kb.op("pe", lambda e, xh=xh, ml=ml, bank=bank:
                          e.matmul(bank, xh, ml, start=False, stop=False),
                          reads=ik, sig=False)
                    bank_last = (k == KC - 1 and not tail)
                    kb.op("pe", lambda e, xl=xl, mh=mh, bank=bank,
                          bank_last=bank_last: e.matmul(
                              bank, xl, mh, start=False, stop=bank_last),
                          reads=ik,
                          writes=(bkey,) if bank_last else (),
                          sig=(bank_last or (k == KC - 1 and ts == NCH - 1)))
            for ts in range(NCH):
                bank, bkey = banks[ts]
                for i, (kind, val) in enumerate(tail):
                    lastt = (i == len(tail) - 1)
                    if kind == "bias":
                        r = val
                        bp, bc = _BR[r]
                        kb.op("pe", lambda e, bank=bank, bp=bp, bc=bc, nt=nt,
                              lastt=lastt: e.matmul(
                                  bank, A["allones"][bp:bp + 1, 0:128],
                                  A["brows"][bp:bp + 1,
                                             bc + nt * 512:bc + (nt + 1) * 512],
                                  start=False, stop=lastt),
                              reads=("allones", f"brows{r}"),
                              writes=(bkey,) if lastt else (), sig=lastt)
                    else:
                        slab, skeys = val
                        mv = slab[:, ts * D + nt * 512: ts * D + (nt + 1) * 512]
                        kb.op("pe", lambda e, bank=bank, mv=mv, lastt=lastt:
                              e.matmul(bank, A["cid"][:, 0:128], mv,
                                       start=False, stop=lastt),
                              reads=("cid",) + tuple(skeys),
                              writes=(bkey,) if lastt else (), sig=lastt)
                evac(ts, nt, bank, bkey)

    def tap(name, slab, cols, reads):
        if name in tap_dr:
            kb.op("gp", lambda e: e.dma_start(out=tap_dr[name],
                                              in_=slab[:, 0:cols]),
                  reads=reads)

    vexr = A["R5"].rearrange("p (s q) -> p s q", s=5)

    def emit_group(g):
        is_first = (g == 0)

        ln_stats(("R0",), A["R0"])
        ln_norm(("R0",), A["R0"], "R1", A["R1"])
        tap(f"xn{g}", A["R1"], 4096, rkeys("R1"))
        transpose_slab(rkeys("R1"), A["R1"], "R2", affine=("g1t", "b1t"))

        def evac_qk(nchunk, bank, bkey):
            if nchunk < 8:
                dst, key, scale = (A["R3"][:, nchunk * T:(nchunk + 1) * T],
                                   f"R3:{nchunk}", 0.125)
            else:
                dst, key, scale = (A["R4"][:, (nchunk - 8) * T:(nchunk - 7) * T],
                                   f"R4:{nchunk - 8}", 1.0)
            kb.op("act", lambda e, d=dst, nchunk=nchunk, scale=scale:
                  e.activation(d, bank[:, 0:512], AF.Identity,
                               bias=A["bqk_a"][:, nchunk:nchunk + 1], scale=scale),
                  reads=(bkey, "bqk_a"), writes=(key,))
        modeA3([(wv["Wqkv_a_hi"][:, :, j * 512:(j + 1) * 512],
                 wv["Wqkv_a_lo"][:, :, j * 512:(j + 1) * 512])
                for j in range(4)], "R2", 16, evac_qk)

        def evac_v(ts, nt, bank, bkey):
            dst = A["R5"].rearrange("p (s h e) -> p s h e", s=5, e=65)[
                :, ts + 1, 8 * nt:8 * (nt + 1), 0:64]
            src = bank.rearrange("p (h e) -> p h e", e=64)[:, 0:8, :]
            kb.op("dve", lambda e, d=dst, s=src: e.tensor_copy(d, s),
                  reads=(bkey,), pre=(f"vex:{ts + 1}",),
                  writes=(f"vex:{ts + 1}",))
        modeB3([(wv["Wqkv_a_hi"][:, :, 2048 + nt * 512: 2048 + (nt + 1) * 512],
                 wv["Wqkv_a_lo"][:, :, 2048 + nt * 512: 2048 + (nt + 1) * 512])
                for nt in range(2)], "R2", evac_v, bias_row=0)

        for c in range(NCH):
            def swa_scores(h):
                j, base, hp = h // 2, (h % 2) * 64, h % 2
                sc, sck = PS[4 + hp], f"P{4 + hp}"
                q = A["R3"][base:base + 64, j * T + c * 128: j * T + (c + 1) * 128]
                if c == 0:
                    kp = A["kprev"][base:base + 64, j * 128:(j + 1) * 128]
                    kcur = A["R4"][base:base + 64, j * T: j * T + 128]
                    kb.op("pe", lambda e, q=q, kp=kp, sc=sc: e.matmul(
                        sc[:, 0:128], q, kp, start=True, stop=False),
                        reads=(f"R3:{j}", "kprev"), pre=(sck,), sig=False)
                    kb.op("pe", lambda e, q=q, kc_=kcur, sc=sc: e.matmul(
                        sc[:, 128:256], q, kc_, start=False, stop=False),
                        reads=(f"R3:{j}", f"R4:{j}"), sig=False)
                else:
                    kc_ = A["R4"][base:base + 64,
                                  j * T + (c - 1) * 128: j * T + (c + 1) * 128]
                    kb.op("pe", lambda e, q=q, kc_=kc_, sc=sc: e.matmul(
                        sc[:, 0:256], q, kc_, start=True, stop=False),
                        reads=(f"R3:{j}", f"R4:{j}"), pre=(sck,), sig=False)
                mkey = "mf16" if (is_first and c == 0) else "mb16"
                kb.op("pe", lambda e, m=A[mkey], sc=sc: e.matmul(
                    sc[:, 0:256], A["cid16"], m, start=False, stop=True),
                    reads=("cid16", mkey), writes=(sck,))

            def swa_exp(h):
                hp = h % 2
                sc, sck = PS[4 + hp], f"P{4 + hp}"
                nm = A["dens"][:, 0:1]
                kb.op("dve", lambda e, sc=sc, nm=nm: e.reduce_max(
                    nm, sc[:, 0:256], axis=AX.X, negate=True),
                    reads=(sck,), writes=("nm",))
                px = A["pexp"][:, hp * 256:hp * 256 + 256]
                kb.op("act", lambda e, sc=sc, px=px, nm=nm: e.activation(
                    px, sc[:, 0:256], AF.Exp, bias=nm, scale=1.0),
                    reads=(sck, "nm"), pre=(f"pexp{hp}",) + scr_pre(f"pexp{hp}"),
                    writes=(f"pexp{hp}",))

            def swa_trans(h):
                hp = h % 2
                px = A["pexp"][:, hp * 256:hp * 256 + 256]
                tb = PS[6]
                kb.op("pe", lambda e, px=px, tb=tb: e.transpose(
                    tb[:, 0:128], px[:, 0:128], A["cid"][:, 0:128]),
                    reads=(f"pexp{hp}", "cid"), pre=("P6",), sig=False)
                kb.op("pe", lambda e, px=px, tb=tb: e.transpose(
                    tb[:, 128:256], px[:, 128:256], A["cid"][:, 0:128]),
                    reads=(f"pexp{hp}", "cid"), writes=("P6",))
                pt = A["pT"][:, hp * 256:hp * 256 + 256]
                kb.op("act", lambda e, tb=tb, pt=pt: e.activation(
                    pt, tb[:, 0:256], AF.Copy),
                    reads=("P6",), pre=(f"pT{hp}",), writes=(f"pT{hp}",))

            def swa_av(h):
                hp = h % 2
                pt = A["pT"][:, hp * 256:hp * 256 + 256]
                av = PS[7]
                vprev = vexr[:, c, h * 65:h * 65 + 65]
                vcur = vexr[:, c + 1, h * 65:h * 65 + 65]
                kb.op("pe", lambda e, pt=pt, v=vprev, av=av: e.matmul(
                    av[:, 0:65], pt[:, 0:128], v, start=True, stop=False),
                    reads=(f"pT{hp}", f"vex:{c}", "R5ones"), pre=("P7",),
                    sig=False)
                kb.op("pe", lambda e, pt=pt, v=vcur, av=av: e.matmul(
                    av[:, 0:65], pt[:, 128:256], v, start=False, stop=True),
                    reads=(f"pT{hp}", f"vex:{c + 1}", "R5ones"), writes=("P7",))
                dent, rden = A["dens"][:, 1:2], A["dens"][:, 2:3]
                kb.op("dve", lambda e, av=av, dent=dent: e.tensor_copy(
                    dent, av[:, 64:65]), reads=("P7",), writes=("dent",))
                kb.op("dve", lambda e, dent=dent, rden=rden: e.reciprocal(
                    rden, dent), reads=("dent",), writes=("rden",))
                dst = A["R1"][:, c * D + h * 64: c * D + (h + 1) * 64]
                rkey = f"R1:{2 * c + h // 8}"
                kb.op("dve", lambda e, av=av, dst=dst, rden=rden: e.tensor_scalar(
                    dst, av[:, 0:64], rden, None, ALU.mult),
                    reads=("P7",), sreads=("rden",),
                    pre=(rkey,) if h % 8 == 0 else (), writes=(rkey,))

            swa_scores(0)
            swa_scores(1)
            for h in range(H):
                swa_exp(h)
                swa_trans(h)
                if h + 2 < H:
                    swa_scores(h + 2)
                swa_av(h)
        tap(f"swa{g}", A["R1"], 4096, rkeys("R1"))

        kpdst = A["kprev"].rearrange("p (j q) -> p j q", q=128)
        kpsrc = A["R4"].rearrange("p (j t) -> p j t", t=T)[:, :, 384:512]
        kb.op("dve", lambda e: e.tensor_copy(kpdst, kpsrc),
              reads=rkeys("R4"), writes=("kprev",))
        kb.op("dve", lambda e: e.tensor_copy(A["R5"][:, 0:1040],
                                             A["R5"][:, 4160:5200]),
              reads=("vex:4", "R5ones"), pre=("vex:0",), writes=("vex:0",))

        def evac_qk_s(nchunk, bank, bkey):
            isq = nchunk < 8
            jj = nchunk if isq else nchunk - 8
            slab = A["R3"] if isq else A["R4"]
            dst = slab[:, jj * T:(jj + 1) * T]
            key = f"R3:{jj}" if isq else f"R4:{jj}"
            scale = 0.125 if isq else 1.0
            scr = A["eluscr"]
            kb.op("dve", lambda e, d=dst, nchunk=nchunk, scale=scale:
                  e.tensor_scalar(d, bank[:, 0:512],
                                  A["bqk_s"][:, nchunk:nchunk + 1], scale,
                                  ALU.add, ALU.mult),
                  reads=(bkey, "bqk_s"), pre=(key,), writes=(key,))
            kb.op("act", lambda e, d=dst: e.activation(scr, d, AF.Exp),
                  reads=(key,), pre=("elusc",) + scr_pre("elusc"),
                  writes=("elusc",))
            kb.op("dve", lambda e: e.tensor_scalar(scr, scr, 1.0, 1e-4,
                                                   ALU.min, ALU.add),
                  reads=("elusc",), writes=("elusc",))
            kb.op("dve", lambda e, d=dst: e.scalar_tensor_tensor(
                d, d, 0.0, scr, ALU.max, ALU.add),
                reads=(key, "elusc"), writes=(key,))
        modeA3([(wv["Wqkv_s_hi"][:, :, j * 512:(j + 1) * 512],
                 wv["Wqkv_s_lo"][:, :, j * 512:(j + 1) * 512])
                for j in range(4)], "R2", 16, evac_qk_s)
        modeB3([(wv["Wqkv_s_hi"][:, :, 2048 + nt * 512: 2048 + (nt + 1) * 512],
                 wv["Wqkv_s_lo"][:, :, 2048 + nt * 512: 2048 + (nt + 1) * 512])
                for nt in range(2)], "R2", evac_v, bias_row=1)
        tap(f"qps{g}", A["R3"], 4096, rkeys("R3"))
        tap(f"kps{g}", A["R4"], 4096, rkeys("R4"))

        for c in range(NCH):
            def lin_at(h):
                j, base, hp = h // 2, (h % 2) * 64, h % 2
                qs = A["R3"][base:base + 64, j * T + c * 128: j * T + (c + 1) * 128]
                ks = A["R4"][base:base + 64, j * T + c * 128: j * T + (c + 1) * 128]
                at, atk = PS[4 + hp], f"P{4 + hp}"
                kb.op("pe", lambda e, ks=ks, qs=qs, at=at: e.matmul(
                    at[:, 0:128], ks, qs, start=True, stop=True),
                    reads=(f"R3:{j}", f"R4:{j}"), pre=(atk,), writes=(atk,))
                idb = A["cid"][0:64, 0:64] if base == 0 else \
                    A["cid"][64:128, 128:192]
                sl = h % 3
                tb = PS[[2, 3, 6][sl]]
                pk = f"P{[2, 3, 6][sl]}"
                kb.op("pe", lambda e, ks=ks, idb=idb, tb=tb: e.transpose(
                    tb[:, 0:64], ks, idb),
                    reads=(f"R4:{j}", "cid"), pre=(pk,), writes=(pk,))

            def lin_mask(h):
                hp = h % 2
                at, atk = PS[4 + hp], f"P{4 + hp}"
                atm = A["ATm"][:, hp * 128:hp * 128 + 128]
                kb.op("dve", lambda e, at=at, atm=atm: e.scalar_tensor_tensor(
                    atm, at[:, 0:128], 1.0, A["triu"], ALU.mult, ALU.mult),
                    reads=(atk, "triu"), pre=(f"ATm{hp}",), writes=(f"ATm{hp}",))
                sl = h % 3
                kt = A["kT"][:, sl * 64:sl * 64 + 64]
                pk = f"P{[2, 3, 6][sl]}"
                kb.op("act", lambda e, kt=kt, sl=sl: e.activation(
                    kt, PS[[2, 3, 6][sl]][:, 0:64], AF.Copy),
                    reads=(pk,), pre=(f"kT{sl}",), writes=(f"kT{sl}",))

            def lin_nd(h):
                j, base, hp = h // 2, (h % 2) * 64, h % 2
                qs = A["R3"][base:base + 64, j * T + c * 128: j * T + (c + 1) * 128]
                atm = A["ATm"][:, hp * 128:hp * 128 + 128]
                nd, ndk = PS[hp], f"P{hp}"
                sz = A["SZ"][base:base + 64, j * 65:(j + 1) * 65]
                vex = vexr[:, c + 1, h * 65:h * 65 + 65]
                kb.op("pe", lambda e, qs=qs, sz=sz, nd=nd: e.matmul(
                    nd[:, 0:65], qs, sz, start=True, stop=False),
                    reads=(f"R3:{j}", f"SZ:{h}"), pre=(ndk,), sig=False)
                kb.op("pe", lambda e, atm=atm, vex=vex, nd=nd: e.matmul(
                    nd[:, 0:65], atm, vex, start=False, stop=True),
                    reads=(f"ATm{hp}", f"vex:{c + 1}", "R5ones"), writes=(ndk,))
                sl = h % 3
                kt = A["kT"][:, sl * 64:sl * 64 + 64]
                szp = PS[7]
                tp = None if base == 0 else (0, 64)
                kb.op("pe", lambda e, kt=kt, vex=vex, szp=szp, base=base, tp=tp:
                      e.matmul(szp[base:base + 64, 0:65], kt, vex,
                               start=True, stop=True, tile_position=tp),
                      reads=(f"kT{sl}", f"vex:{c + 1}", "R5ones"),
                      pre=("P7",), writes=("P7",))

            def lin_out(h):
                j, base, hp = h // 2, (h % 2) * 64, h % 2
                nd, ndk = PS[hp], f"P{hp}"
                dent, rden = A["dens"][:, 4:5], A["dens"][:, 5:6]
                kb.op("dve", lambda e, nd=nd, dent=dent: e.tensor_scalar(
                    dent, nd[:, 64:65], 1e-6, None, ALU.add),
                    reads=(ndk,), writes=("dent2",))
                kb.op("dve", lambda e, dent=dent, rden=rden: e.reciprocal(
                    rden, dent), reads=("dent2",), writes=("rden2",))
                dst = A["R10"][:, c * D + h * 64: c * D + (h + 1) * 64]
                rkey = f"R10:{2 * c + h // 8}"
                kb.op("dve", lambda e, nd=nd, dst=dst, rden=rden: e.tensor_scalar(
                    dst, nd[:, 0:64], rden, None, ALU.mult),
                    reads=(ndk,), sreads=("rden2",),
                    pre=(rkey,) if h % 8 == 0 else (), writes=(rkey,))
                sz = A["SZ"][base:base + 64, j * 65:(j + 1) * 65]
                szp = PS[7]
                kb.op("dve", lambda e, sz=sz, szp=szp, base=base: e.tensor_add(
                    sz, sz, szp[base:base + 64, 0:65]),
                    reads=("P7", f"SZ:{h}"), writes=(f"SZ:{h}",))

            lin_at(0)
            lin_at(1)
            for h in range(H):
                lin_mask(h)
                if h + 2 < H:
                    lin_at(h + 2)
                lin_nd(h)
                lin_out(h)
        tap(f"o{g}", A["R10"], 4096, rkeys("R10"))

        transpose_slab(rkeys("R1"), A["R1"], "R3")

        def evac_wp(nchunk, bank, bkey):
            dst = A["R9"][:, nchunk * T:(nchunk + 1) * T]
            kb.op("act", lambda e, d=dst, nchunk=nchunk: e.activation(
                d, bank[:, 0:512], AF.Identity,
                bias=A["bp_a"][:, nchunk:nchunk + 1]),
                reads=(bkey, "bp_a"), writes=(f"R9:{nchunk}",))
        modeA3([(wv["Wp_a_hi"][:, :, j * 512:(j + 1) * 512],
                 wv["Wp_a_lo"][:, :, j * 512:(j + 1) * 512])
                for j in range(2)], "R3", 8, evac_wp)

        transpose_slab(rkeys("R10"), A["R10"], "R4")

        def evac_wos(nchunk, bank, bkey):
            dst = A["R9"][:, nchunk * T:(nchunk + 1) * T]
            kb.op("dve", lambda e, d=dst, nchunk=nchunk: e.scalar_tensor_tensor(
                d, bank[:, 0:512], A["bo_s"][:, nchunk:nchunk + 1], d,
                ALU.add, ALU.add),
                reads=(bkey, "bo_s", f"R9:{nchunk}"), writes=(f"R9:{nchunk}",))
        modeA3([(wv["Wo_s_hi"][:, :, j * 512:(j + 1) * 512],
                 wv["Wo_s_lo"][:, :, j * 512:(j + 1) * 512])
                for j in range(2)], "R4", 8, evac_wos)

        for j in range(KC):
            a32 = A["R9"][:, j * T:(j + 1) * T]
            hid, lod = phi("R3", j), plo("R3", j)
            kb.op("act", lambda e, a32=a32, hid=hid: e.activation(
                hid, a32, AF.Copy),
                reads=(f"R9:{j}",), pre=(f"R3:{j}",), writes=(f"R3:{j}",))
            kb.op("dve", lambda e, a32=a32, hid=hid, lod=lod:
                  e.scalar_tensor_tensor(lod, a32, 1.0, hid,
                                         ALU.mult, ALU.subtract),
                  reads=(f"R9:{j}", f"R3:{j}"), writes=(f"R3:{j}",))

        def evac_wo(ts, nt, bank, bkey):
            dst = A["R10"][:, ts * D + nt * 512: ts * D + (nt + 1) * 512]
            kb.op("act", lambda e, d=dst, bank=bank: e.activation(
                d, bank, AF.Copy),
                reads=(bkey,), pre=(f"R10:{2 * ts + nt}",),
                writes=(f"R10:{2 * ts + nt}",))
        modeB3([(wv["Wo_hi"][:, :, nt * 512:(nt + 1) * 512],
                 wv["Wo_lo"][:, :, nt * 512:(nt + 1) * 512])
                for nt in range(2)], "R3", evac_wo, bias_row=2,
               resid=[(A["R0"], ("R0",))])
        tap(f"x1_{g}", A["R10"], 4096, rkeys("R10"))

        ln_stats(rkeys("R10"), A["R10"])
        ln_norm(rkeys("R10"), A["R10"], "R1", A["R1"])
        tap(f"h{g}", A["R1"], 4096, rkeys("R1"))
        transpose_slab(rkeys("R1"), A["R1"], "R2", affine=("gmt", "bmt"))

        gel_reg = ["R3", "R4", "R9", "R1"]
        gel_slab = [A["R3"], A["R4"], A["R9"], A["R1"]]

        def evac_w1(nchunk, bank, bkey):
            regn = gel_reg[nchunk // 8]
            kk = nchunk % 8
            gk = "g32a" if nchunk % 2 == 0 else "g32b"
            g32 = A["sqdump"][:, 0:512] if nchunk % 2 == 0 \
                else A["sqdump"][:, 512:1024]
            dk = (f"{regn}:{kk}",)
            hid, lod = phi(regn, kk), plo(regn, kk)
            kb.op("act", lambda e, g32=g32, nchunk=nchunk: e.activation(
                g32, bank[:, 0:512], AF.Gelu,
                bias=A["b1mt"][:, nchunk:nchunk + 1]),
                reads=(bkey, "b1mt"), pre=(gk,) + scr_pre(gk), writes=(gk,))
            kb.op("act", lambda e, g32=g32, hid=hid: e.activation(
                hid, g32, AF.Copy),
                reads=(gk,), pre=dk, writes=dk)
            kb.op("dve", lambda e, g32=g32, hid=hid, lod=lod:
                  e.scalar_tensor_tensor(lod, g32, 1.0, hid,
                                         ALU.mult, ALU.subtract),
                  reads=(gk,) + dk, writes=dk)
        modeA3([(wv["W1_hi"][:, :, j * 512:(j + 1) * 512],
                 wv["W1_lo"][:, :, j * 512:(j + 1) * 512])
                for j in range(8)], "R2", 32, evac_w1)
        tap(f"gela{g}", A["R3"], 4096, rkeys("R3"))
        tap(f"gelb{g}", A["R4"], 4096, rkeys("R4"))
        tap(f"gelc{g}", A["R9"], 4096, rkeys("R9"))
        tap(f"geld{g}", A["R1"], 4096, rkeys("R1"))

        for nt in range(2):
            banks = [(PS[i], f"P{i}") for i in range(4)]
            for kg in range(4):
                bh, bl, rk = load_piece3(
                    wv["W2_hi"][kg][:, :, nt * 512:(nt + 1) * 512],
                    wv["W2_lo"][kg][:, :, nt * 512:(nt + 1) * 512])
                for k in range(KC):
                    kk = kg * 8 + k
                    regn = gel_reg[kk // 8]
                    for ts in range(NCH):
                        bank, bkey = banks[ts]
                        xh = phi(regn, kk % 8)[:, ts * 128:(ts + 1) * 128]
                        xl = plo(regn, kk % 8)[:, ts * 128:(ts + 1) * 128]
                        mh, ml = bh[:, k, :], bl[:, k, :]
                        ik = (rk, f"{regn}:{kk % 8}")
                        first = (kg == 0 and k == 0)
                        kb.op("pe", lambda e, xh=xh, mh=mh, bank=bank,
                              first=first: e.matmul(bank, xh, mh,
                                                    start=first, stop=False),
                              reads=ik, pre=(bkey,) if first else (),
                              sig=False)
                        kb.op("pe", lambda e, xh=xh, ml=ml, bank=bank:
                              e.matmul(bank, xh, ml, start=False, stop=False),
                              reads=ik, sig=False)
                        kb.op("pe", lambda e, xl=xl, mh=mh, bank=bank:
                              e.matmul(bank, xl, mh, start=False, stop=False),
                              reads=ik,
                              sig=(k == KC - 1 and ts == NCH - 1))
            for ts in range(NCH):
                bank, bkey = banks[ts]
                kb.op("pe", lambda e, bank=bank, nt=nt: e.matmul(
                    bank, A["allones"][0:1, 0:128],
                    A["brows"][0:1, 1024 + nt * 512:1024 + (nt + 1) * 512],
                    start=False, stop=False),
                    reads=("allones", "brows3"), sig=False)
                mvx1 = A["R10"][:, ts * D + nt * 512: ts * D + (nt + 1) * 512]
                kb.op("pe", lambda e, bank=bank, mv=mvx1: e.matmul(
                    bank, A["cid"][:, 0:128], mv, start=False, stop=False),
                    reads=("cid", f"R10:{2 * ts + nt}"), sig=False)
                mvx = A["R0"][:, ts * D + nt * 512: ts * D + (nt + 1) * 512]
                kb.op("pe", lambda e, bank=bank, mv=mvx: e.matmul(
                    bank, A["cid"][:, 0:128], mv, start=False, stop=True),
                    reads=("cid", "R0"), writes=(bkey,))
                dst = A["R11"][:, ts * D + nt * 512: ts * D + (nt + 1) * 512]
                kb.op("act", lambda e, d=dst, bank=bank: e.activation(
                    d, bank, AF.Copy),
                    reads=(bkey,), pre=(f"R11:{2 * ts + nt}",),
                    writes=(f"R11:{2 * ts + nt}",))
        tap(f"t{g}", A["R11"], 4096, rkeys("R11"))

        ln_stats(rkeys("R11"), A["R11"])
        ln_norm(rkeys("R11"), A["R11"], "R1", A["R1"])
        transpose_slab(rkeys("R1"), A["R1"], "R2")

        def evac_re(ts, nt, bank, bkey):
            dst = A["R10"][:, ts * D + nt * 512: ts * D + (nt + 1) * 512]
            kb.op("act", lambda e, d=dst, bank=bank: e.activation(
                d, bank, AF.Square),
                reads=(bkey,), pre=(f"R10:{2 * ts + nt}",),
                writes=(f"R10:{2 * ts + nt}",))
        def evac_im(ts, nt, bank, bkey):
            dst = A["R9"][:, ts * D + nt * 512: ts * D + (nt + 1) * 512]
            kb.op("act", lambda e, d=dst, bank=bank: e.activation(
                d, bank, AF.Square),
                reads=(bkey,), pre=(f"R9:{2 * ts + nt}",),
                writes=(f"R9:{2 * ts + nt}",))
        modeB3([(wv["csc_hi"][:, :, nt * 512:(nt + 1) * 512],
                 wv["csc_lo"][:, :, nt * 512:(nt + 1) * 512])
                for nt in range(2)], "R2", evac_re,
               wextra=("csc_hi", "csc_lo"))
        modeB3([(wv["ssc_hi"][:, :, nt * 512:(nt + 1) * 512],
                 wv["ssc_lo"][:, :, nt * 512:(nt + 1) * 512])
                for nt in range(2)], "R2", evac_im,
               wextra=("ssc_hi", "ssc_lo"))
        kb.op("dve", lambda e: e.tensor_add(A["R3"][:, 0:4096], A["R10"],
                                            A["R9"]),
              reads=rkeys("R10") + rkeys("R9"), pre=rkeys("R3"),
              writes=rkeys("R3"))
        kb.op("act", lambda e: e.activation(A["R10"], A["R3"][:, 0:4096],
                                            AF.Sqrt),
              reads=rkeys("R3"), pre=rkeys("R10"), writes=rkeys("R10"))
        kb.op("dve", lambda e: e.tensor_add(A["R9"], A["R1"], A["R10"]),
              reads=rkeys("R10") + rkeys("R1"), pre=rkeys("R9"),
              writes=rkeys("R9"))
        ln_stats(rkeys("R9"), A["R9"])
        ln_norm(rkeys("R9"), A["R9"], "R11", A["R11"])
        src = A["R11"].rearrange("p (c d) -> p c d", d=D)
        kb.op("gp", lambda e, g=g: e.dma_start(out=outr[g], in_=src),
              reads=rkeys("R11"))

    x_load(0)
    for g in range(NG):
        emit_group(g)
        if g + 1 < NG:
            x_load(g + 1)


    with ExitStack() as sst:
        sems = {n: sst.enter_context(nc.semaphore(f"sem_{n}")) for n in KB.ALL}

        def make_body(eng_name):
            oplist = kb.ops[eng_name]
            mysem = sems[eng_name]
            mymul = 16 if eng_name in KB.DMAQ else 1

            def body(e):
                for fn, waits, selfwaits, sig, sigchan in oplist:
                    for p, v in waits:
                        e.wait_ge(sems[p], v * (16 if p in KB.DMAQ else 1))
                    for v in selfwaits:
                        e.wait_ge(mysem, v * mymul)
                    inst = fn(e)
                    if sig:
                        inst.then_inc(sems[sigchan],
                                      16 if sigchan in KB.DMAQ else 1)
                if eng_name == "gp":
                    for ch in KB.CHANS:
                        if kb.cnt[ch]:
                            e.wait_ge(sems[ch], kb.cnt[ch] * 16)
            return body

        with nc.Block() as blk:
            blk.tensor(make_body("pe"))
            blk.vector(make_body("dve"))
            blk.scalar(make_body("act"))
            blk.sync(make_body("sp"))
            blk.gpsimd(make_body("gp"))
    st.close()
    return nc


def make_const_inputs():
    cid = np.zeros((128, 192), np.float32)
    cid[:, 0:128] = np.eye(128, dtype=np.float32)
    cid[64:128, 128:192] = np.eye(64, dtype=np.float32)
    triu = np.triu(np.ones((128, 128), np.float32))
    r = np.arange(128)[:, None]
    s = np.arange(256)[None, :]
    band = (s >= r + 1) & (s <= r + 128)
    first = band & (s >= 128)
    mb = np.where(band, 0.0, -1e30).astype(np.float32)
    mf = np.where(first, 0.0, -1e30).astype(np.float32)
    return cid, triu, mb, mf


def shard_inputs(inputs, b, L):
    cid, triu, mb, mf = make_const_inputs()
    g = lambda n: np.ascontiguousarray(np.asarray(inputs[n], dtype=np.float32))
    bq_a, bq_s = g("L0_attn_bqkv_a"), g("L0_attn_bqkv_s")
    brows = np.zeros((4, 1024), np.float32)
    brows[0] = bq_a[2048:3072]
    brows[1] = bq_s[2048:3072]
    brows[2] = g("L0_attn_bo")
    brows[3] = g("L0_attn_b2m")
    return {
        "x": np.ascontiguousarray(np.asarray(inputs["L0_attn_x"],
                                             dtype=np.float32)[b, :L]),
        "Wqkv_a": g("L0_attn_Wqkv_a"), "Wp_a": g("L0_attn_Wp_a"),
        "Wqkv_s": g("L0_attn_Wqkv_s"), "Wo_s": g("L0_attn_Wo_s"),
        "Wo": g("L0_attn_Wo"), "W1": g("L0_attn_W1"), "W2": g("L0_attn_W2"),
        "th": g("L0_ffn_theta"), "thn": g("L0_ffn_theta_noise"),
        "ph": g("L0_ffn_phi"), "phn": g("L0_ffn_phi_noise"),
        "cid": cid, "triu": triu, "mb": mb, "mf": mf,
        "cid16": cid[:, 0:128].astype(ml_dtypes.bfloat16),
        "mb16": mb.astype(ml_dtypes.bfloat16),
        "mf16": mf.astype(ml_dtypes.bfloat16),
        "bqk_a": np.ascontiguousarray(bq_a[0:2048].reshape(16, 128).T),
        "bqk_s": np.ascontiguousarray(bq_s[0:2048].reshape(16, 128).T),
        "bp_a": np.ascontiguousarray(g("L0_attn_bp_a").reshape(8, 128).T),
        "bo_s": np.ascontiguousarray(g("L0_attn_bo_s").reshape(8, 128).T),
        "b1mt": np.ascontiguousarray(g("L0_attn_b1m").reshape(32, 128).T),
        "g1t": np.ascontiguousarray(g("L0_attn_g1").reshape(8, 128).T),
        "b1t": np.ascontiguousarray(g("L0_attn_b1").reshape(8, 128).T),
        "gmt": np.ascontiguousarray(g("L0_attn_gm").reshape(8, 128).T),
        "bmt": np.ascontiguousarray(g("L0_attn_bm").reshape(8, 128).T),
        "brows": brows,
    }


def kernel(**inputs):
    L = 4096
    nc = build_nc(L)
    in_maps = [shard_inputs(inputs, b, L) for b in range(8)]
    res = run_bass_kernel_spmd(nc, in_maps, core_ids=list(range(8)))
    out = np.stack([np.asarray(r["out"]).reshape(L, D) for r in res.results])
    return out.astype(np.float32)


# version 05
# speedup vs baseline: 1.1417x; 1.0011x over previous
import ml_dtypes
import numpy as np
from contextlib import ExitStack

import concourse.bass as bass
import concourse.mybir as mybir
from concourse.bass_utils import run_bass_kernel_spmd

F32 = mybir.dt.float32
BF16 = mybir.dt.bfloat16
AF = mybir.ActivationFunctionType
ALU = mybir.AluOpType
AX = mybir.AxisListType

D, H, E = 1024, 16, 64
T, NCH, KC = 512, 4, 8
EPS = 1e-5

_AR = {}
_off = 0
def _reg(name, cols):
    global _off
    _AR[name] = (_off, cols)
    _off += (cols + 15) // 16 * 16
for _n, _c in [
    ("R0", 4096), ("R1", 4096), ("R2", 4096), ("R3", 4096), ("R4", 4096),
    ("R5", 5200), ("R9", 4096), ("R10", 4096), ("R11", 4096),
    ("ring0", 4096), ("ring1", 4096),
    ("kprev", 1024), ("pT", 512), ("ATm", 256), ("kT", 192),
    ("SZ", 520), ("sqdump", 1024), ("stats", 32), ("stats2", 32),
    ("dens", 16),
    ("brows", 2048), ("allones", 128), ("cid", 192), ("triu", 128),
    ("m16", 320),
    ("bqk_a", 16), ("bqk_s", 16), ("bp_a", 8), ("bo_s", 8), ("b1mt", 32),
    ("g1t", 8), ("b1t", 8), ("gmt", 8), ("bmt", 8),
]:
    _reg(_n, _c)
ARENA_COLS = _off
assert ARENA_COLS <= 53200, ARENA_COLS


def rkeys(region, n=8):
    return tuple(f"{region}:{i}" for i in range(n))


class KB:
    ENGS = ("pe", "dve", "act", "sp", "gp")
    CHANS = ("sp", "gp", "r0", "r1", "ws0", "ws1", "ws2", "ws3")
    DMAQ = CHANS
    ALL = ENGS + ("r0", "r1", "ws0", "ws1", "ws2", "ws3")

    def __init__(self):
        self.ops = {n: [] for n in self.ENGS}
        self.cnt = {n: 0 for n in self.ALL}
        self.waited = {c: {p: 0 for p in self.ALL} for c in self.ENGS}
        self.reg = {}
        self.pend = {n: [] for n in self.ENGS}

    def _need(self, eng, waits, selfwaits, p, v):
        if v is None:
            raise RuntimeError("dependency on an unsignalled op")
        if p == eng:
            if eng == "pe":
                return
            if v > self.waited[eng][eng]:
                selfwaits.append(v)
                self.waited[eng][eng] = v
            return
        if p in self.CHANS:
            v = self.cnt[p]
        if v > self.waited[eng][p]:
            waits.append((p, v))
            self.waited[eng][p] = v

    def op(self, eng, fn, reads=(), writes=(), sreads=(), pre=(), sig=True,
           chan=None):
        sigchan = chan or eng
        waits, selfwaits = [], []
        if sigchan in self.CHANS and self.cnt[sigchan] > 0:
            self._need(eng, waits, selfwaits, sigchan, self.cnt[sigchan])
        for k in tuple(reads) + tuple(sreads):
            r = self.reg.get(k)
            if r and r["w"] is not None:
                self._need(eng, waits, selfwaits, *r["w"])
        for k in tuple(writes) + tuple(pre):
            r = self.reg.get(k)
            if r:
                if r["w"] is not None:
                    self._need(eng, waits, selfwaits, *r["w"])
                for p, v in r["r"].items():
                    self._need(eng, waits, selfwaits, p, v)
        if sig:
            self.cnt[sigchan] += 1
            val = self.cnt[sigchan]
            for k in self.pend[eng]:
                rr = self.reg.setdefault(k, {"w": None, "r": {}})
                rr["r"][eng] = val if sigchan == eng else self.cnt[eng]
            self.pend[eng].clear()
            for k in tuple(reads) + tuple(sreads):
                rr = self.reg.setdefault(k, {"w": None, "r": {}})
                rr["r"][sigchan] = val
        else:
            val = None
            for k in tuple(reads) + tuple(sreads):
                self.pend[eng].append(k)
        for k in writes:
            if not sig:
                raise RuntimeError("writes require sig=True")
            self.reg[k] = {"w": (sigchan, val), "r": {}}
        self.ops[eng].append((fn, waits, selfwaits, sig, sigchan))


def build_nc(L, taps=()):
    NG = L // T
    nc = bass.Bass("TRN2", target_bir_lowering=False, debug=False)

    dr = {}
    dr["x"] = nc.dram_tensor("x", [L, D], F32, kind="ExternalInput").ap()
    for n, shp in [
        ("Wqkv_a", [D, 3 * D]), ("Wp_a", [D, D]), ("Wqkv_s", [D, 3 * D]),
        ("Wo_s", [D, D]), ("Wo", [D, D]), ("W1", [D, 4 * D]), ("W2", [4 * D, D]),
        ("th", [D, D]), ("thn", [D, D]), ("ph", [D, D]), ("phn", [D, D]),
        ("cid", [128, 192]), ("triu", [128, 128]), ("mb", [128, 256]),
        ("mf", [128, 256]), ("bqk_a", [128, 16]), ("bqk_s", [128, 16]),
        ("bp_a", [128, 8]), ("bo_s", [128, 8]), ("b1mt", [128, 32]),
        ("g1t", [128, 8]), ("b1t", [128, 8]), ("gmt", [128, 8]), ("bmt", [128, 8]),
        ("brows", [4, 1024]),
    ]:
        dr[n] = nc.dram_tensor(n, shp, F32, kind="ExternalInput").ap()
    for n, shp in [("cid16", [128, 128]), ("mb16", [128, 256]),
                   ("mf16", [128, 256])]:
        dr[n] = nc.dram_tensor(n, shp, BF16, kind="ExternalInput").ap()
    dr["out"] = nc.dram_tensor("out", [L, D], F32, kind="ExternalOutput").ap()
    for w, shp in [("Wqkv_a", [D, 3 * D]), ("Wqkv_s", [D, 3 * D]),
                   ("Wp_a", [D, D]), ("Wo_s", [D, D]), ("Wo", [D, D]),
                   ("W1", [D, 4 * D]), ("W2", [4 * D, D]),
                   ("csc", [D, D]), ("ssc", [D, D])]:
        dr[w + "_hi"] = nc.dram_tensor(w + "_hi", shp, BF16, kind="Internal").ap()
        dr[w + "_lo"] = nc.dram_tensor(w + "_lo", shp, BF16, kind="Internal").ap()
    tap_dr = {}
    for tname, tcols in taps:
        tap_dr[tname] = nc.dram_tensor("tap_" + tname, [128, tcols], F32,
                                       kind="ExternalOutput").ap()

    st = ExitStack()
    arena = st.enter_context(nc.sbuf_tensor("arena", [128, ARENA_COLS], F32)).ap()
    A = {n: arena[:, o:o + c] for n, (o, c) in _AR.items()}
    A["eluscr"] = A["sqdump"][:, 0:512]
    _m16 = A["m16"].bitcast(BF16)
    A["cid16"] = _m16[:, 0:128]
    A["mb16"] = _m16[:, 128:384]
    A["mf16"] = _m16[:, 384:640]
    A["pexp"] = A["sqdump"][:, 0:512]
    SCRK = ("sqd", "elusc", "pexp0", "pexp1", "aff", "g32a", "g32b")
    def scr_pre(own):
        return tuple(k for k in SCRK if k != own)
    PS = [st.enter_context(nc.psum_tensor(f"pb{i}", [128, 512], F32)).ap()
          for i in range(8)]

    kb = KB()

    xr = dr["x"].rearrange("(g c p) d -> g p c d", c=NCH, p=128)
    outr = dr["out"].rearrange("(g c p) d -> g p c d", c=NCH, p=128)
    wv = {}
    for n in ("Wqkv_a", "Wqkv_s", "Wp_a", "Wo_s", "Wo", "W1"):
        wv[n] = dr[n].rearrange("(kc p) n -> p kc n", p=128)
    wv["W2"] = dr["W2"].rearrange("(kg kc p) n -> kg p kc n", kg=4, p=128)
    for w in ("Wqkv_a", "Wqkv_s", "Wp_a", "Wo_s", "Wo", "W1", "csc", "ssc"):
        for h in ("_hi", "_lo"):
            wv[w + h] = dr[w + h].rearrange("(kc p) n -> p kc n", p=128)
    for h in ("_hi", "_lo"):
        wv["W2" + h] = dr["W2" + h].rearrange("(kg kc p) n -> kg p kc n",
                                              kg=4, p=128)

    def phi(region, k):
        bc = A[region].bitcast(BF16)
        return bc[:, k * 512:(k + 1) * 512]

    def plo(region, k):
        bc = A[region].bitcast(BF16)
        return bc[:, 4096 + k * 512: 4096 + (k + 1) * 512]

    ring = [A["ring0"], A["ring1"]]
    ring_i = [0]

    def load_piece(src_ap, ncols=512, reads=()):
        i = ring_i[0] % 2
        ring_i[0] += 1
        buf = ring[i]
        key = f"ring{i}"
        dst = buf.rearrange("p (kc n) -> p kc n", n=ncols)
        kb.op("sp", lambda e, d=dst, s=src_ap: e.dma_start(out=d, in_=s),
              reads=reads, writes=(key,), chan=f"r{i}")
        return buf, key

    def load_piece3(src_hi, src_lo, reads=()):
        i = ring_i[0] % 2
        ring_i[0] += 1
        bc = ring[i].bitcast(BF16)
        key = f"ring{i}"
        bh = bc[:, 0:4096].rearrange("p (kc n) -> p kc n", n=512)
        bl = bc[:, 4096:8192].rearrange("p (kc n) -> p kc n", n=512)
        kb.op("sp", lambda e, s=src_hi: e.dma_start(out=bh, in_=s),
              reads=reads, writes=(key,), chan=f"r{i}")
        kb.op("sp", lambda e, s=src_lo: e.dma_start(out=bl, in_=s),
              reads=reads, writes=(key,), chan=f"r{i}")
        return bh, bl, key

    pj = [0]

    def next_bank():
        i = pj[0] % 2
        pj[0] += 1
        return PS[i], f"P{i}"

    _wsi = [0]
    for n in ("cid", "triu", "cid16", "mb16", "mf16",
              "bqk_a", "bqk_s", "bp_a", "bo_s",
              "b1mt", "g1t", "b1t", "gmt", "bmt"):
        kb.op("sp", lambda e, d=A[n], s=dr[n]: e.dma_start(out=d, in_=s),
              writes=(n,))
    _BR = {0: (0, 0), 1: (32, 0), 2: (64, 0), 3: (0, 1024)}
    for r in range(4):
        bp, bc = _BR[r]
        kb.op("sp", lambda e, d=A["brows"][bp:bp + 1, bc:bc + 1024],
              s=dr["brows"][r:r + 1, :]: e.dma_start(out=d, in_=s),
              writes=(f"brows{r}",))
    kb.op("dve", lambda e: e.memset(A["allones"], 1.0), writes=("allones",))
    kb.op("dve", lambda e: e.memset(A["SZ"], 0.0),
          writes=tuple(f"SZ:{h}" for h in range(H)))
    kb.op("dve", lambda e: e.memset(A["kprev"], 0.0), writes=("kprev",))
    kb.op("dve", lambda e: e.memset(A["R5"], 0.0),
          writes=tuple(f"vex:{s}" for s in range(5)))
    vext_ones = A["R5"].rearrange("p (g e) -> p g e", e=65)[:, :, 64:65]
    kb.op("dve", lambda e: e.memset(vext_ones, 1.0),
          pre=tuple(f"vex:{s_}" for s_ in range(5)), writes=("R5ones",))
    kb.op("dve", lambda e: e.tensor_scalar(A["bqk_a"][:, 0:8], A["bqk_a"][:, 0:8],
                                           0.125, None, ALU.mult),
          reads=("bqk_a",), writes=("bqk_a",))

    spanA = arena[:, _AR["R0"][0]:_AR["R0"][0] + 8192]
    spanB = arena[:, _AR["R2"][0]:_AR["R2"][0] + 8192]
    kA = ("R0",) + rkeys("R1")
    kB = rkeys("R2") + rkeys("R3")
    for src, nsrc, scr, iscos in (("th", "thn", "csc", True),
                                  ("ph", "phn", "ssc", False)):
        srcA = dr[src].rearrange("(kc p) n -> p kc n", p=128)
        srcB = dr[nsrc].rearrange("(kc p) n -> p kc n", p=128)
        dstA = spanA.rearrange("p (kc n) -> p kc n", n=1024)
        dstB = spanB.rearrange("p (kc n) -> p kc n", n=1024)
        kb.op("sp", lambda e, d=dstA, s=srcA: e.dma_start(out=d, in_=s), writes=kA)
        kb.op("sp", lambda e, d=dstB, s=srcB: e.dma_start(out=d, in_=s), writes=kB)
        kb.op("dve", lambda e: e.tensor_add(spanA, spanA, spanB),
              reads=kA + kB, writes=kA)
        if iscos:
            kb.op("act", lambda e: e.activation(spanB, spanA, AF.Sin, scale=0.5),
                  reads=kA, writes=kB)
            kb.op("act", lambda e: e.activation(spanA, spanB, AF.Square),
                  reads=kB, writes=kA)
            kb.op("dve", lambda e: e.tensor_scalar(spanA, spanA, -2.0, 1.0,
                                                   ALU.mult, ALU.add),
                  reads=kA, writes=kA)
        else:
            kb.op("act", lambda e: e.activation(spanA, spanA, AF.Sin, scale=1.0),
                  reads=kA, writes=kA)
        bcB = spanB.bitcast(BF16)
        hiv, lov = bcB[:, 0:8192], bcB[:, 8192:16384]
        kb.op("act", lambda e: e.activation(hiv, spanA, AF.Copy),
              reads=kA, writes=kB)
        kb.op("dve", lambda e: e.scalar_tensor_tensor(
            lov, spanA, 1.0, hiv, ALU.mult, ALU.subtract),
            reads=kA + kB, writes=kB)
        for h, v in (("_hi", hiv), ("_lo", lov)):
            dstv = wv[scr + h]
            srcv = v.rearrange("p (kc n) -> p kc n", n=1024)
            kb.op("gp", lambda e, d=dstv, s=srcv: e.dma_start(out=d, in_=s),
                  reads=kB, writes=(scr + h,), chan=f"ws{_wsi[0] % 4}")
            _wsi[0] += 1

    _splits = []
    for w, npieces in (("Wqkv_a", 6), ("Wqkv_s", 6), ("Wp_a", 2), ("Wo_s", 2),
                       ("Wo", 2), ("W1", 8)):
        for j in range(npieces):
            _splits.append((wv[w][:, :, j * 512:(j + 1) * 512],
                            wv[w + "_hi"][:, :, j * 512:(j + 1) * 512],
                            wv[w + "_lo"][:, :, j * 512:(j + 1) * 512]))
    for kg in range(4):
        for nt in range(2):
            _splits.append((wv["W2"][kg][:, :, nt * 512:(nt + 1) * 512],
                            wv["W2_hi"][kg][:, :, nt * 512:(nt + 1) * 512],
                            wv["W2_lo"][kg][:, :, nt * 512:(nt + 1) * 512]))
    spl_bc = A["R9"].bitcast(BF16)
    spl_hi = spl_bc[:, 0:4096].rearrange("p (kc n) -> p kc n", n=512)
    spl_lo = spl_bc[:, 4096:8192].rearrange("p (kc n) -> p kc n", n=512)
    for src, dhi, dlo in _splits:
        buf, rk = load_piece(src)
        b3 = buf.rearrange("p (kc n) -> p kc n", n=512)
        kb.op("act", lambda e, b3=b3: e.activation(spl_hi, b3, AF.Copy),
              reads=(rk,), pre=rkeys("R9"), writes=rkeys("R9"))
        kb.op("dve", lambda e, b3=b3: e.scalar_tensor_tensor(
            spl_lo, b3, 1.0, spl_hi, ALU.mult, ALU.subtract),
            reads=(rk,) + rkeys("R9"), writes=rkeys("R9"))
        kb.op("gp", lambda e, d=dhi: e.dma_start(out=d, in_=spl_hi),
              reads=rkeys("R9"), chan=f"ws{_wsi[0] % 4}")
        _wsi[0] += 1
        kb.op("gp", lambda e, d=dlo: e.dma_start(out=d, in_=spl_lo),
              reads=rkeys("R9"), chan=f"ws{_wsi[0] % 4}")
        _wsi[0] += 1

    def x_load(g):
        dst = A["R0"].rearrange("p (c d) -> p c d", d=D)
        kb.op("gp", lambda e, d=dst, s=xr[g]: e.dma_start(out=d, in_=s),
              writes=("R0",))

    def ln_stats(src_keys, src_slab, sb="stats"):
        stt_ = A[sb]
        sums, ssq = stt_[:, 0:4], stt_[:, 4:8]
        st_m, st_r = stt_[:, 8:12], stt_[:, 12:16]
        scr1, scr2 = stt_[:, 16:20], stt_[:, 20:24]
        p = sb
        if sb == "stats":
            dump, dkeys = A["sqdump"], ("sqd",) + scr_pre("sqd")
        else:
            dump, dkeys = A["R4"][:, 0:1024], ("R4:0", "R4:1")
        for c in range(NCH):
            xc = src_slab[:, c * D:(c + 1) * D]
            kb.op("act", lambda e, xc=xc, c=c, dump=dump: e.activation(
                dump, xc, AF.Square, accum_out=ssq[:, c:c + 1]),
                reads=src_keys,
                pre=((f"{p}_ssq",) if c == 0 else ()) + dkeys,
                writes=((f"{p}_ssq",) if c == 3 else ()) + dkeys, sig=True)
        kb.op("dve", lambda e: e.reduce_sum(
            sums, src_slab.rearrange("p (c d) -> p c d", d=D), axis=AX.X),
            reads=src_keys, pre=(f"{p}_sum",), writes=(f"{p}_sum",))
        kb.op("dve", lambda e: e.tensor_scalar(st_m, sums, 1.0 / D, None, ALU.mult),
              reads=(f"{p}_sum",), writes=(f"{p}_m",))
        kb.op("dve", lambda e: e.tensor_scalar(scr1, ssq, 1.0 / D, EPS,
                                               ALU.mult, ALU.add),
              reads=(f"{p}_ssq",), writes=(f"{p}_s1",))
        kb.op("act", lambda e: e.activation(scr2, st_m, AF.Square),
              reads=(f"{p}_m",), writes=(f"{p}_s2",))
        kb.op("dve", lambda e: e.tensor_sub(scr1, scr1, scr2),
              reads=(f"{p}_s1", f"{p}_s2"), writes=(f"{p}_s1",))
        kb.op("act", lambda e: e.activation(scr2, scr1, AF.Ln),
              reads=(f"{p}_s1",), writes=(f"{p}_s2",))
        kb.op("act", lambda e: e.activation(st_r, scr2, AF.Exp, scale=-0.5),
              reads=(f"{p}_s2",), writes=(f"{p}_r",))

    def ln_norm(src_keys, src_slab, dst_region, dst_slab, sb="stats"):
        stt_ = A[sb]
        st_m, st_r = stt_[:, 8:12], stt_[:, 12:16]
        p = sb
        for c in range(NCH):
            xc = src_slab[:, c * D:(c + 1) * D]
            oc = dst_slab[:, c * D:(c + 1) * D]
            wk = (f"{dst_region}:{2 * c}", f"{dst_region}:{2 * c + 1}") \
                if dst_region != "R0" else ("R0",)
            kb.op("dve", lambda e, xc=xc, oc=oc, c=c, st_m=st_m, st_r=st_r:
                  e.tensor_scalar(oc, xc, st_m[:, c:c + 1], st_r[:, c:c + 1],
                                  ALU.subtract, ALU.mult),
                  reads=src_keys, sreads=(f"{p}_m", f"{p}_r"), writes=wk)

    def transpose_slab(src_keys, src_slab, dst_region, affine=None):
        for k in range(KC):
            bi = [6, 2, 3][k % 3]
            bank, bk = PS[bi], f"P{bi}"
            for c in range(NCH):
                src = src_slab[:, c * D + k * 128: c * D + (k + 1) * 128]
                kb.op("pe", lambda e, s=src, c=c, bank=bank: e.transpose(
                    bank[:, c * 128:(c + 1) * 128], s, A["cid"][:, 0:128]),
                    reads=src_keys + ("cid",),
                    pre=(bk,) if c == 0 else (),
                    writes=(bk,) if c == 3 else (), sig=(c == 3))
            dk = (f"{dst_region}:{k}",)
            hid, lod = phi(dst_region, k), plo(dst_region, k)
            if affine is None:
                kb.op("act", lambda e, hid=hid, bank=bank: e.activation(
                    hid, bank[:, 0:512], AF.Copy),
                    reads=(bk,), pre=dk, writes=dk)
                kb.op("dve", lambda e, hid=hid, lod=lod, bank=bank:
                      e.scalar_tensor_tensor(
                          lod, bank[:, 0:512], 1.0, hid, ALU.mult, ALU.subtract),
                      reads=(bk,) + dk, writes=dk)
            else:
                gt, bt = affine
                ak = "aff" if k % 2 == 0 else "g32a"
                aff = A["sqdump"][:, 0:512] if k % 2 == 0 \
                    else A["sqdump"][:, 512:1024]
                kb.op("dve", lambda e, k=k, gt=gt, bt=bt, aff=aff, bank=bank:
                      e.tensor_scalar(
                          aff, bank[:, 0:512], A[gt][:, k:k + 1],
                          A[bt][:, k:k + 1], ALU.mult, ALU.add),
                      reads=(bk, gt, bt), pre=(ak,) + scr_pre(ak),
                      writes=(ak,))
                kb.op("act", lambda e, hid=hid, aff=aff: e.activation(
                    hid, aff, AF.Copy),
                    reads=(ak,), pre=dk, writes=dk)
                kb.op("dve", lambda e, hid=hid, lod=lod, aff=aff:
                      e.scalar_tensor_tensor(
                          lod, aff, 1.0, hid, ALU.mult, ALU.subtract),
                      reads=(ak,) + dk, writes=dk)

    def modeA(w_piece_srcs, in_region, in_slab, n_chunks, evac):
        for j, src in enumerate(w_piece_srcs):
            buf, rk = load_piece(src)
            for nsub in range(4):
                nchunk = 4 * j + nsub
                if nchunk >= n_chunks:
                    break
                bank, bkey = next_bank()
                for k in range(KC):
                    stat = buf[:, k * T + nsub * 128: k * T + (nsub + 1) * 128]
                    mov = in_slab[:, k * T:(k + 1) * T]
                    kb.op("pe", lambda e, st_=stat, mv=mov, k=k, bank=bank:
                          e.matmul(bank, st_, mv, start=(k == 0),
                                   stop=(k == KC - 1)),
                          reads=(rk, f"{in_region}:{k}"),
                          pre=(bkey,) if k == 0 else (),
                          writes=(bkey,) if k == KC - 1 else (),
                          sig=(k == KC - 1))
                evac(nchunk, bank, bkey)

    def modeB(w_piece_srcs, in_region, in_slab, evac, bias_row=None,
              resid=(), wextra=()):
        for nt, src in enumerate(w_piece_srcs):
            buf, rk = load_piece(src, reads=wextra)
            banks = [(PS[i], f"P{i}") for i in range(4)]
            tail = ([("bias", bias_row)] if bias_row is not None else []) + \
                   [("resid", rv) for rv in resid]
            for k in range(KC):
                for ts in range(NCH):
                    bank, bkey = banks[ts]
                    stat = in_slab[:, k * T + ts * 128: k * T + (ts + 1) * 128]
                    mov = buf[:, k * 512:(k + 1) * 512]
                    last = (k == KC - 1) and not tail
                    kb.op("pe", lambda e, st_=stat, mv=mov, bank=bank, k=k,
                          last=last: e.matmul(bank, st_, mv, start=(k == 0),
                                              stop=last),
                          reads=(rk, f"{in_region}:{k}"),
                          pre=(bkey,) if k == 0 else (),
                          writes=(bkey,) if last else (), sig=last)
            for ts in range(NCH):
                bank, bkey = banks[ts]
                for i, (kind, val) in enumerate(tail):
                    last = (i == len(tail) - 1)
                    if kind == "bias":
                        r = val
                        bp, bc = _BR[r]
                        kb.op("pe", lambda e, bank=bank, bp=bp, bc=bc, nt=nt,
                              last=last: e.matmul(
                                  bank, A["allones"][bp:bp + 1, 0:128],
                                  A["brows"][bp:bp + 1,
                                             bc + nt * 512:bc + (nt + 1) * 512],
                                  start=False, stop=last),
                              reads=("allones", f"brows{r}"),
                              writes=(bkey,) if last else (), sig=last)
                    else:
                        slab, skeys = val
                        mv = slab[:, ts * D + nt * 512: ts * D + (nt + 1) * 512]
                        kb.op("pe", lambda e, bank=bank, mv=mv, last=last:
                              e.matmul(bank, A["cid"][:, 0:128], mv,
                                       start=False, stop=last),
                              reads=("cid",) + tuple(skeys),
                              writes=(bkey,) if last else (), sig=last)
                evac(ts, nt, bank, bkey)

    def modeA3(w_srcs, in_region, n_chunks, evac):
        for j, (shi, slo) in enumerate(w_srcs):
            bh, bl, rk = load_piece3(shi, slo)
            for nsub in range(4):
                nchunk = 4 * j + nsub
                if nchunk >= n_chunks:
                    break
                bank, bkey = next_bank()
                for k in range(KC):
                    wh = bh[:, k, nsub * 128:(nsub + 1) * 128]
                    wl = bl[:, k, nsub * 128:(nsub + 1) * 128]
                    xh, xl = phi(in_region, k), plo(in_region, k)
                    ik = (rk, f"{in_region}:{k}")
                    kb.op("pe", lambda e, wh=wh, xh=xh, bank=bank, k=k:
                          e.matmul(bank, wh, xh, start=(k == 0), stop=False),
                          reads=ik, pre=(bkey,) if k == 0 else (), sig=False)
                    kb.op("pe", lambda e, wh=wh, xl=xl, bank=bank:
                          e.matmul(bank, wh, xl, start=False, stop=False),
                          reads=ik, sig=False)
                    kb.op("pe", lambda e, wl=wl, xh=xh, bank=bank, k=k:
                          e.matmul(bank, wl, xh, start=False,
                                   stop=(k == KC - 1)),
                          reads=ik, writes=(bkey,) if k == KC - 1 else (),
                          sig=(k == KC - 1))
                evac(nchunk, bank, bkey)

    def modeB3(w_srcs, in_region, evac, bias_row=None, resid=(), wextra=()):
        for nt, (shi, slo) in enumerate(w_srcs):
            bh, bl, rk = load_piece3(shi, slo, reads=wextra)
            banks = [(PS[i], f"P{i}") for i in range(4)]
            tail = ([("bias", bias_row)] if bias_row is not None else []) + \
                   [("resid", rv) for rv in resid]
            for k in range(KC):
                for ts in range(NCH):
                    bank, bkey = banks[ts]
                    xh = phi(in_region, k)[:, ts * 128:(ts + 1) * 128]
                    xl = plo(in_region, k)[:, ts * 128:(ts + 1) * 128]
                    mh, ml = bh[:, k, :], bl[:, k, :]
                    ik = (rk, f"{in_region}:{k}")
                    kb.op("pe", lambda e, xh=xh, mh=mh, bank=bank, k=k:
                          e.matmul(bank, xh, mh, start=(k == 0), stop=False),
                          reads=ik, pre=(bkey,) if k == 0 else (), sig=False)
                    kb.op("pe", lambda e, xh=xh, ml=ml, bank=bank:
                          e.matmul(bank, xh, ml, start=False, stop=False),
                          reads=ik, sig=False)
                    bank_last = (k == KC - 1 and not tail)
                    kb.op("pe", lambda e, xl=xl, mh=mh, bank=bank,
                          bank_last=bank_last: e.matmul(
                              bank, xl, mh, start=False, stop=bank_last),
                          reads=ik,
                          writes=(bkey,) if bank_last else (),
                          sig=(bank_last or (k == KC - 1 and ts == NCH - 1)))
            for ts in range(NCH):
                bank, bkey = banks[ts]
                for i, (kind, val) in enumerate(tail):
                    lastt = (i == len(tail) - 1)
                    if kind == "bias":
                        r = val
                        bp, bc = _BR[r]
                        kb.op("pe", lambda e, bank=bank, bp=bp, bc=bc, nt=nt,
                              lastt=lastt: e.matmul(
                                  bank, A["allones"][bp:bp + 1, 0:128],
                                  A["brows"][bp:bp + 1,
                                             bc + nt * 512:bc + (nt + 1) * 512],
                                  start=False, stop=lastt),
                              reads=("allones", f"brows{r}"),
                              writes=(bkey,) if lastt else (), sig=lastt)
                    else:
                        slab, skeys = val
                        mv = slab[:, ts * D + nt * 512: ts * D + (nt + 1) * 512]
                        kb.op("pe", lambda e, bank=bank, mv=mv, lastt=lastt:
                              e.matmul(bank, A["cid"][:, 0:128], mv,
                                       start=False, stop=lastt),
                              reads=("cid",) + tuple(skeys),
                              writes=(bkey,) if lastt else (), sig=lastt)
                evac(ts, nt, bank, bkey)

    def tap(name, slab, cols, reads):
        if name in tap_dr:
            kb.op("gp", lambda e: e.dma_start(out=tap_dr[name],
                                              in_=slab[:, 0:cols]),
                  reads=reads)

    vexr = A["R5"].rearrange("p (s q) -> p s q", s=5)

    def emit_group(g):
        is_first = (g == 0)

        ln_stats(("R0",), A["R0"], sb="stats2")
        ln_norm(("R0",), A["R0"], "R10", A["R10"], sb="stats2")
        tap(f"xn{g}", A["R10"], 4096, rkeys("R10"))
        transpose_slab(rkeys("R10"), A["R10"], "R2", affine=("g1t", "b1t"))

        def evac_qk(nchunk, bank, bkey):
            if nchunk < 8:
                dst, key, scale = (A["R3"][:, nchunk * T:(nchunk + 1) * T],
                                   f"R3:{nchunk}", 0.125)
            else:
                dst, key, scale = (A["R4"][:, (nchunk - 8) * T:(nchunk - 7) * T],
                                   f"R4:{nchunk - 8}", 1.0)
            kb.op("act", lambda e, d=dst, nchunk=nchunk, scale=scale:
                  e.activation(d, bank[:, 0:512], AF.Identity,
                               bias=A["bqk_a"][:, nchunk:nchunk + 1], scale=scale),
                  reads=(bkey, "bqk_a"), writes=(key,))
        modeA3([(wv["Wqkv_a_hi"][:, :, j * 512:(j + 1) * 512],
                 wv["Wqkv_a_lo"][:, :, j * 512:(j + 1) * 512])
                for j in range(4)], "R2", 16, evac_qk)

        def evac_v(ts, nt, bank, bkey):
            dst = A["R5"].rearrange("p (s h e) -> p s h e", s=5, e=65)[
                :, ts + 1, 8 * nt:8 * (nt + 1), 0:64]
            src = bank.rearrange("p (h e) -> p h e", e=64)[:, 0:8, :]
            kb.op("dve", lambda e, d=dst, s=src: e.tensor_copy(d, s),
                  reads=(bkey,), pre=(f"vex:{ts + 1}",),
                  writes=(f"vex:{ts + 1}",))
        modeB3([(wv["Wqkv_a_hi"][:, :, 2048 + nt * 512: 2048 + (nt + 1) * 512],
                 wv["Wqkv_a_lo"][:, :, 2048 + nt * 512: 2048 + (nt + 1) * 512])
                for nt in range(2)], "R2", evac_v, bias_row=0)

        for c in range(NCH):
            def swa_scores(h):
                j, base, hp = h // 2, (h % 2) * 64, h % 2
                sc, sck = PS[4 + hp], f"P{4 + hp}"
                q = A["R3"][base:base + 64, j * T + c * 128: j * T + (c + 1) * 128]
                if c == 0:
                    kp = A["kprev"][base:base + 64, j * 128:(j + 1) * 128]
                    kcur = A["R4"][base:base + 64, j * T: j * T + 128]
                    kb.op("pe", lambda e, q=q, kp=kp, sc=sc: e.matmul(
                        sc[:, 0:128], q, kp, start=True, stop=False),
                        reads=(f"R3:{j}", "kprev"), pre=(sck,), sig=False)
                    kb.op("pe", lambda e, q=q, kc_=kcur, sc=sc: e.matmul(
                        sc[:, 128:256], q, kc_, start=False, stop=False),
                        reads=(f"R3:{j}", f"R4:{j}"), sig=False)
                else:
                    kc_ = A["R4"][base:base + 64,
                                  j * T + (c - 1) * 128: j * T + (c + 1) * 128]
                    kb.op("pe", lambda e, q=q, kc_=kc_, sc=sc: e.matmul(
                        sc[:, 0:256], q, kc_, start=True, stop=False),
                        reads=(f"R3:{j}", f"R4:{j}"), pre=(sck,), sig=False)
                mkey = "mf16" if (is_first and c == 0) else "mb16"
                kb.op("pe", lambda e, m=A[mkey], sc=sc: e.matmul(
                    sc[:, 0:256], A["cid16"], m, start=False, stop=True),
                    reads=("cid16", mkey), writes=(sck,))

            def swa_exp(h):
                hp = h % 2
                sc, sck = PS[4 + hp], f"P{4 + hp}"
                nm = A["dens"][:, 0:1]
                kb.op("dve", lambda e, sc=sc, nm=nm: e.reduce_max(
                    nm, sc[:, 0:256], axis=AX.X, negate=True),
                    reads=(sck,), writes=("nm",))
                px = A["pexp"][:, hp * 256:hp * 256 + 256]
                kb.op("act", lambda e, sc=sc, px=px, nm=nm: e.activation(
                    px, sc[:, 0:256], AF.Exp, bias=nm, scale=1.0),
                    reads=(sck, "nm"), pre=(f"pexp{hp}",) + scr_pre(f"pexp{hp}"),
                    writes=(f"pexp{hp}",))

            def swa_trans(h):
                hp = h % 2
                px = A["pexp"][:, hp * 256:hp * 256 + 256]
                tb = PS[6]
                kb.op("pe", lambda e, px=px, tb=tb: e.transpose(
                    tb[:, 0:128], px[:, 0:128], A["cid"][:, 0:128]),
                    reads=(f"pexp{hp}", "cid"), pre=("P6",), sig=False)
                kb.op("pe", lambda e, px=px, tb=tb: e.transpose(
                    tb[:, 128:256], px[:, 128:256], A["cid"][:, 0:128]),
                    reads=(f"pexp{hp}", "cid"), writes=("P6",))
                pt = A["pT"][:, hp * 256:hp * 256 + 256]
                kb.op("act", lambda e, tb=tb, pt=pt: e.activation(
                    pt, tb[:, 0:256], AF.Copy),
                    reads=("P6",), pre=(f"pT{hp}",), writes=(f"pT{hp}",))

            def swa_av(h):
                hp = h % 2
                pt = A["pT"][:, hp * 256:hp * 256 + 256]
                av = PS[7]
                vprev = vexr[:, c, h * 65:h * 65 + 65]
                vcur = vexr[:, c + 1, h * 65:h * 65 + 65]
                kb.op("pe", lambda e, pt=pt, v=vprev, av=av: e.matmul(
                    av[:, 0:65], pt[:, 0:128], v, start=True, stop=False),
                    reads=(f"pT{hp}", f"vex:{c}", "R5ones"), pre=("P7",),
                    sig=False)
                kb.op("pe", lambda e, pt=pt, v=vcur, av=av: e.matmul(
                    av[:, 0:65], pt[:, 128:256], v, start=False, stop=True),
                    reads=(f"pT{hp}", f"vex:{c + 1}", "R5ones"), writes=("P7",))
                dent, rden = A["dens"][:, 1:2], A["dens"][:, 2:3]
                kb.op("dve", lambda e, av=av, dent=dent: e.tensor_copy(
                    dent, av[:, 64:65]), reads=("P7",), writes=("dent",))
                kb.op("dve", lambda e, dent=dent, rden=rden: e.reciprocal(
                    rden, dent), reads=("dent",), writes=("rden",))
                dst = A["R1"][:, c * D + h * 64: c * D + (h + 1) * 64]
                rkey = f"R1:{2 * c + h // 8}"
                kb.op("dve", lambda e, av=av, dst=dst, rden=rden: e.tensor_scalar(
                    dst, av[:, 0:64], rden, None, ALU.mult),
                    reads=("P7",), sreads=("rden",),
                    pre=(rkey,) if h % 8 == 0 else (), writes=(rkey,))

            swa_scores(0)
            swa_scores(1)
            for h in range(H):
                swa_exp(h)
                swa_trans(h)
                if h + 2 < H:
                    swa_scores(h + 2)
                swa_av(h)
        tap(f"swa{g}", A["R1"], 4096, rkeys("R1"))

        kpdst = A["kprev"].rearrange("p (j q) -> p j q", q=128)
        kpsrc = A["R4"].rearrange("p (j t) -> p j t", t=T)[:, :, 384:512]
        kb.op("dve", lambda e: e.tensor_copy(kpdst, kpsrc),
              reads=rkeys("R4"), writes=("kprev",))
        kb.op("dve", lambda e: e.tensor_copy(A["R5"][:, 0:1040],
                                             A["R5"][:, 4160:5200]),
              reads=("vex:4", "R5ones"), pre=("vex:0",), writes=("vex:0",))

        def evac_qk_s(nchunk, bank, bkey):
            isq = nchunk < 8
            jj = nchunk if isq else nchunk - 8
            slab = A["R3"] if isq else A["R4"]
            dst = slab[:, jj * T:(jj + 1) * T]
            key = f"R3:{jj}" if isq else f"R4:{jj}"
            scale = 0.125 if isq else 1.0
            scr = A["eluscr"]
            kb.op("dve", lambda e, d=dst, nchunk=nchunk, scale=scale:
                  e.tensor_scalar(d, bank[:, 0:512],
                                  A["bqk_s"][:, nchunk:nchunk + 1], scale,
                                  ALU.add, ALU.mult),
                  reads=(bkey, "bqk_s"), pre=(key,), writes=(key,))
            kb.op("act", lambda e, d=dst: e.activation(scr, d, AF.Exp),
                  reads=(key,), pre=("elusc",) + scr_pre("elusc"),
                  writes=("elusc",))
            kb.op("dve", lambda e: e.tensor_scalar(scr, scr, 1.0, 1e-4,
                                                   ALU.min, ALU.add),
                  reads=("elusc",), writes=("elusc",))
            kb.op("dve", lambda e, d=dst: e.scalar_tensor_tensor(
                d, d, 0.0, scr, ALU.max, ALU.add),
                reads=(key, "elusc"), writes=(key,))
        modeA3([(wv["Wqkv_s_hi"][:, :, j * 512:(j + 1) * 512],
                 wv["Wqkv_s_lo"][:, :, j * 512:(j + 1) * 512])
                for j in range(4)], "R2", 16, evac_qk_s)
        modeB3([(wv["Wqkv_s_hi"][:, :, 2048 + nt * 512: 2048 + (nt + 1) * 512],
                 wv["Wqkv_s_lo"][:, :, 2048 + nt * 512: 2048 + (nt + 1) * 512])
                for nt in range(2)], "R2", evac_v, bias_row=1)
        tap(f"qps{g}", A["R3"], 4096, rkeys("R3"))
        tap(f"kps{g}", A["R4"], 4096, rkeys("R4"))

        for c in range(NCH):
            def lin_at(h):
                j, base, hp = h // 2, (h % 2) * 64, h % 2
                qs = A["R3"][base:base + 64, j * T + c * 128: j * T + (c + 1) * 128]
                ks = A["R4"][base:base + 64, j * T + c * 128: j * T + (c + 1) * 128]
                at, atk = PS[4 + hp], f"P{4 + hp}"
                kb.op("pe", lambda e, ks=ks, qs=qs, at=at: e.matmul(
                    at[:, 0:128], ks, qs, start=True, stop=True),
                    reads=(f"R3:{j}", f"R4:{j}"), pre=(atk,), writes=(atk,))
                idb = A["cid"][0:64, 0:64] if base == 0 else \
                    A["cid"][64:128, 128:192]
                sl = h % 3
                tb = PS[[2, 3, 6][sl]]
                pk = f"P{[2, 3, 6][sl]}"
                kb.op("pe", lambda e, ks=ks, idb=idb, tb=tb: e.transpose(
                    tb[:, 0:64], ks, idb),
                    reads=(f"R4:{j}", "cid"), pre=(pk,), writes=(pk,))

            def lin_mask(h):
                hp = h % 2
                at, atk = PS[4 + hp], f"P{4 + hp}"
                atm = A["ATm"][:, hp * 128:hp * 128 + 128]
                kb.op("dve", lambda e, at=at, atm=atm: e.scalar_tensor_tensor(
                    atm, at[:, 0:128], 1.0, A["triu"], ALU.mult, ALU.mult),
                    reads=(atk, "triu"), pre=(f"ATm{hp}",), writes=(f"ATm{hp}",))
                sl = h % 3
                kt = A["kT"][:, sl * 64:sl * 64 + 64]
                pk = f"P{[2, 3, 6][sl]}"
                kb.op("act", lambda e, kt=kt, sl=sl: e.activation(
                    kt, PS[[2, 3, 6][sl]][:, 0:64], AF.Copy),
                    reads=(pk,), pre=(f"kT{sl}",), writes=(f"kT{sl}",))

            def lin_nd(h):
                j, base, hp = h // 2, (h % 2) * 64, h % 2
                qs = A["R3"][base:base + 64, j * T + c * 128: j * T + (c + 1) * 128]
                atm = A["ATm"][:, hp * 128:hp * 128 + 128]
                nd, ndk = PS[hp], f"P{hp}"
                sz = A["SZ"][base:base + 64, j * 65:(j + 1) * 65]
                vex = vexr[:, c + 1, h * 65:h * 65 + 65]
                kb.op("pe", lambda e, qs=qs, sz=sz, nd=nd: e.matmul(
                    nd[:, 0:65], qs, sz, start=True, stop=False),
                    reads=(f"R3:{j}", f"SZ:{h}"), pre=(ndk,), sig=False)
                kb.op("pe", lambda e, atm=atm, vex=vex, nd=nd: e.matmul(
                    nd[:, 0:65], atm, vex, start=False, stop=True),
                    reads=(f"ATm{hp}", f"vex:{c + 1}", "R5ones"), writes=(ndk,))
                sl = h % 3
                kt = A["kT"][:, sl * 64:sl * 64 + 64]
                szp = PS[7]
                tp = None if base == 0 else (0, 64)
                kb.op("pe", lambda e, kt=kt, vex=vex, szp=szp, base=base, tp=tp:
                      e.matmul(szp[base:base + 64, 0:65], kt, vex,
                               start=True, stop=True, tile_position=tp),
                      reads=(f"kT{sl}", f"vex:{c + 1}", "R5ones"),
                      pre=("P7",), writes=("P7",))

            def lin_out(h):
                j, base, hp = h // 2, (h % 2) * 64, h % 2
                nd, ndk = PS[hp], f"P{hp}"
                dent, rden = A["dens"][:, 4:5], A["dens"][:, 5:6]
                kb.op("dve", lambda e, nd=nd, dent=dent: e.tensor_scalar(
                    dent, nd[:, 64:65], 1e-6, None, ALU.add),
                    reads=(ndk,), writes=("dent2",))
                kb.op("dve", lambda e, dent=dent, rden=rden: e.reciprocal(
                    rden, dent), reads=("dent2",), writes=("rden2",))
                dst = A["R10"][:, c * D + h * 64: c * D + (h + 1) * 64]
                rkey = f"R10:{2 * c + h // 8}"
                kb.op("dve", lambda e, nd=nd, dst=dst, rden=rden: e.tensor_scalar(
                    dst, nd[:, 0:64], rden, None, ALU.mult),
                    reads=(ndk,), sreads=("rden2",),
                    pre=(rkey,) if h % 8 == 0 else (), writes=(rkey,))
                sz = A["SZ"][base:base + 64, j * 65:(j + 1) * 65]
                szp = PS[7]
                kb.op("dve", lambda e, sz=sz, szp=szp, base=base: e.tensor_add(
                    sz, sz, szp[base:base + 64, 0:65]),
                    reads=("P7", f"SZ:{h}"), writes=(f"SZ:{h}",))

            lin_at(0)
            lin_at(1)
            for h in range(H):
                lin_mask(h)
                if h + 2 < H:
                    lin_at(h + 2)
                lin_nd(h)
                lin_out(h)
        tap(f"o{g}", A["R10"], 4096, rkeys("R10"))

        transpose_slab(rkeys("R1"), A["R1"], "R3")

        def evac_wp(nchunk, bank, bkey):
            dst = A["R9"][:, nchunk * T:(nchunk + 1) * T]
            kb.op("act", lambda e, d=dst, nchunk=nchunk: e.activation(
                d, bank[:, 0:512], AF.Identity,
                bias=A["bp_a"][:, nchunk:nchunk + 1]),
                reads=(bkey, "bp_a"), writes=(f"R9:{nchunk}",))
        modeA3([(wv["Wp_a_hi"][:, :, j * 512:(j + 1) * 512],
                 wv["Wp_a_lo"][:, :, j * 512:(j + 1) * 512])
                for j in range(2)], "R3", 8, evac_wp)

        transpose_slab(rkeys("R10"), A["R10"], "R4")

        def evac_wos(nchunk, bank, bkey):
            dst = A["R9"][:, nchunk * T:(nchunk + 1) * T]
            kb.op("dve", lambda e, d=dst, nchunk=nchunk: e.scalar_tensor_tensor(
                d, bank[:, 0:512], A["bo_s"][:, nchunk:nchunk + 1], d,
                ALU.add, ALU.add),
                reads=(bkey, "bo_s", f"R9:{nchunk}"), writes=(f"R9:{nchunk}",))
        modeA3([(wv["Wo_s_hi"][:, :, j * 512:(j + 1) * 512],
                 wv["Wo_s_lo"][:, :, j * 512:(j + 1) * 512])
                for j in range(2)], "R4", 8, evac_wos)

        for j in range(KC):
            a32 = A["R9"][:, j * T:(j + 1) * T]
            hid, lod = phi("R3", j), plo("R3", j)
            kb.op("act", lambda e, a32=a32, hid=hid: e.activation(
                hid, a32, AF.Copy),
                reads=(f"R9:{j}",), pre=(f"R3:{j}",), writes=(f"R3:{j}",))
            kb.op("dve", lambda e, a32=a32, hid=hid, lod=lod:
                  e.scalar_tensor_tensor(lod, a32, 1.0, hid,
                                         ALU.mult, ALU.subtract),
                  reads=(f"R9:{j}", f"R3:{j}"), writes=(f"R3:{j}",))

        def evac_wo(ts, nt, bank, bkey):
            dst = A["R10"][:, ts * D + nt * 512: ts * D + (nt + 1) * 512]
            kb.op("act", lambda e, d=dst, bank=bank: e.activation(
                d, bank, AF.Copy),
                reads=(bkey,), pre=(f"R10:{2 * ts + nt}",),
                writes=(f"R10:{2 * ts + nt}",))
        modeB3([(wv["Wo_hi"][:, :, nt * 512:(nt + 1) * 512],
                 wv["Wo_lo"][:, :, nt * 512:(nt + 1) * 512])
                for nt in range(2)], "R3", evac_wo, bias_row=2,
               resid=[(A["R0"], ("R0",))])
        tap(f"x1_{g}", A["R10"], 4096, rkeys("R10"))

        ln_stats(rkeys("R10"), A["R10"])
        ln_norm(rkeys("R10"), A["R10"], "R1", A["R1"])
        tap(f"h{g}", A["R1"], 4096, rkeys("R1"))
        transpose_slab(rkeys("R1"), A["R1"], "R2", affine=("gmt", "bmt"))

        gel_reg = ["R3", "R4", "R9", "R1"]
        gel_slab = [A["R3"], A["R4"], A["R9"], A["R1"]]

        def evac_w1(nchunk, bank, bkey):
            regn = gel_reg[nchunk // 8]
            kk = nchunk % 8
            gk = "g32a" if nchunk % 2 == 0 else "g32b"
            g32 = A["sqdump"][:, 0:512] if nchunk % 2 == 0 \
                else A["sqdump"][:, 512:1024]
            dk = (f"{regn}:{kk}",)
            hid, lod = phi(regn, kk), plo(regn, kk)
            kb.op("act", lambda e, g32=g32, nchunk=nchunk: e.activation(
                g32, bank[:, 0:512], AF.Gelu,
                bias=A["b1mt"][:, nchunk:nchunk + 1]),
                reads=(bkey, "b1mt"), pre=(gk,) + scr_pre(gk), writes=(gk,))
            kb.op("act", lambda e, g32=g32, hid=hid: e.activation(
                hid, g32, AF.Copy),
                reads=(gk,), pre=dk, writes=dk)
            kb.op("dve", lambda e, g32=g32, hid=hid, lod=lod:
                  e.scalar_tensor_tensor(lod, g32, 1.0, hid,
                                         ALU.mult, ALU.subtract),
                  reads=(gk,) + dk, writes=dk)
        modeA3([(wv["W1_hi"][:, :, j * 512:(j + 1) * 512],
                 wv["W1_lo"][:, :, j * 512:(j + 1) * 512])
                for j in range(8)], "R2", 32, evac_w1)
        tap(f"gela{g}", A["R3"], 4096, rkeys("R3"))
        tap(f"gelb{g}", A["R4"], 4096, rkeys("R4"))
        tap(f"gelc{g}", A["R9"], 4096, rkeys("R9"))
        tap(f"geld{g}", A["R1"], 4096, rkeys("R1"))

        for nt in range(2):
            banks = [(PS[i], f"P{i}") for i in range(4)]
            for kg in range(4):
                bh, bl, rk = load_piece3(
                    wv["W2_hi"][kg][:, :, nt * 512:(nt + 1) * 512],
                    wv["W2_lo"][kg][:, :, nt * 512:(nt + 1) * 512])
                for k in range(KC):
                    kk = kg * 8 + k
                    regn = gel_reg[kk // 8]
                    for ts in range(NCH):
                        bank, bkey = banks[ts]
                        xh = phi(regn, kk % 8)[:, ts * 128:(ts + 1) * 128]
                        xl = plo(regn, kk % 8)[:, ts * 128:(ts + 1) * 128]
                        mh, ml = bh[:, k, :], bl[:, k, :]
                        ik = (rk, f"{regn}:{kk % 8}")
                        first = (kg == 0 and k == 0)
                        kb.op("pe", lambda e, xh=xh, mh=mh, bank=bank,
                              first=first: e.matmul(bank, xh, mh,
                                                    start=first, stop=False),
                              reads=ik, pre=(bkey,) if first else (),
                              sig=False)
                        kb.op("pe", lambda e, xh=xh, ml=ml, bank=bank:
                              e.matmul(bank, xh, ml, start=False, stop=False),
                              reads=ik, sig=False)
                        kb.op("pe", lambda e, xl=xl, mh=mh, bank=bank:
                              e.matmul(bank, xl, mh, start=False, stop=False),
                              reads=ik,
                              sig=(k == KC - 1 and ts == NCH - 1))
            for ts in range(NCH):
                bank, bkey = banks[ts]
                kb.op("pe", lambda e, bank=bank, nt=nt: e.matmul(
                    bank, A["allones"][0:1, 0:128],
                    A["brows"][0:1, 1024 + nt * 512:1024 + (nt + 1) * 512],
                    start=False, stop=False),
                    reads=("allones", "brows3"), sig=False)
                mvx1 = A["R10"][:, ts * D + nt * 512: ts * D + (nt + 1) * 512]
                kb.op("pe", lambda e, bank=bank, mv=mvx1: e.matmul(
                    bank, A["cid"][:, 0:128], mv, start=False, stop=False),
                    reads=("cid", f"R10:{2 * ts + nt}"), sig=False)
                mvx = A["R0"][:, ts * D + nt * 512: ts * D + (nt + 1) * 512]
                kb.op("pe", lambda e, bank=bank, mv=mvx: e.matmul(
                    bank, A["cid"][:, 0:128], mv, start=False, stop=True),
                    reads=("cid", "R0"), writes=(bkey,))
                dst = A["R11"][:, ts * D + nt * 512: ts * D + (nt + 1) * 512]
                kb.op("act", lambda e, d=dst, bank=bank: e.activation(
                    d, bank, AF.Copy),
                    reads=(bkey,), pre=(f"R11:{2 * ts + nt}",),
                    writes=(f"R11:{2 * ts + nt}",))
        tap(f"t{g}", A["R11"], 4096, rkeys("R11"))
        if g + 1 < NG:
            x_load(g + 1)

        ln_stats(rkeys("R11"), A["R11"])
        ln_norm(rkeys("R11"), A["R11"], "R1", A["R1"])
        transpose_slab(rkeys("R1"), A["R1"], "R2")

        def evac_re(ts, nt, bank, bkey):
            dst = A["R10"][:, ts * D + nt * 512: ts * D + (nt + 1) * 512]
            kb.op("act", lambda e, d=dst, bank=bank: e.activation(
                d, bank, AF.Square),
                reads=(bkey,), pre=(f"R10:{2 * ts + nt}",),
                writes=(f"R10:{2 * ts + nt}",))
        def evac_im(ts, nt, bank, bkey):
            dst = A["R9"][:, ts * D + nt * 512: ts * D + (nt + 1) * 512]
            kb.op("act", lambda e, d=dst, bank=bank: e.activation(
                d, bank, AF.Square),
                reads=(bkey,), pre=(f"R9:{2 * ts + nt}",),
                writes=(f"R9:{2 * ts + nt}",))
        modeB3([(wv["csc_hi"][:, :, nt * 512:(nt + 1) * 512],
                 wv["csc_lo"][:, :, nt * 512:(nt + 1) * 512])
                for nt in range(2)], "R2", evac_re,
               wextra=("csc_hi", "csc_lo"))
        modeB3([(wv["ssc_hi"][:, :, nt * 512:(nt + 1) * 512],
                 wv["ssc_lo"][:, :, nt * 512:(nt + 1) * 512])
                for nt in range(2)], "R2", evac_im,
               wextra=("ssc_hi", "ssc_lo"))
        kb.op("dve", lambda e: e.tensor_add(A["R3"][:, 0:4096], A["R10"],
                                            A["R9"]),
              reads=rkeys("R10") + rkeys("R9"), pre=rkeys("R3"),
              writes=rkeys("R3"))
        kb.op("act", lambda e: e.activation(A["R10"], A["R3"][:, 0:4096],
                                            AF.Sqrt),
              reads=rkeys("R3"), pre=rkeys("R10"), writes=rkeys("R10"))
        kb.op("dve", lambda e: e.tensor_add(A["R9"], A["R1"], A["R10"]),
              reads=rkeys("R10") + rkeys("R1"), pre=rkeys("R9"),
              writes=rkeys("R9"))
        ln_stats(rkeys("R9"), A["R9"])
        ln_norm(rkeys("R9"), A["R9"], "R11", A["R11"])
        src = A["R11"].rearrange("p (c d) -> p c d", d=D)
        kb.op("gp", lambda e, g=g: e.dma_start(out=outr[g], in_=src),
              reads=rkeys("R11"))

    x_load(0)
    for g in range(NG):
        emit_group(g)


    with ExitStack() as sst:
        sems = {n: sst.enter_context(nc.semaphore(f"sem_{n}")) for n in KB.ALL}

        def make_body(eng_name):
            oplist = kb.ops[eng_name]
            mysem = sems[eng_name]
            mymul = 16 if eng_name in KB.DMAQ else 1

            def body(e):
                for fn, waits, selfwaits, sig, sigchan in oplist:
                    for p, v in waits:
                        e.wait_ge(sems[p], v * (16 if p in KB.DMAQ else 1))
                    for v in selfwaits:
                        e.wait_ge(mysem, v * mymul)
                    inst = fn(e)
                    if sig:
                        inst.then_inc(sems[sigchan],
                                      16 if sigchan in KB.DMAQ else 1)
                if eng_name == "gp":
                    for ch in KB.CHANS:
                        if kb.cnt[ch]:
                            e.wait_ge(sems[ch], kb.cnt[ch] * 16)
            return body

        with nc.Block() as blk:
            blk.tensor(make_body("pe"))
            blk.vector(make_body("dve"))
            blk.scalar(make_body("act"))
            blk.sync(make_body("sp"))
            blk.gpsimd(make_body("gp"))
    st.close()
    return nc


def make_const_inputs():
    cid = np.zeros((128, 192), np.float32)
    cid[:, 0:128] = np.eye(128, dtype=np.float32)
    cid[64:128, 128:192] = np.eye(64, dtype=np.float32)
    triu = np.triu(np.ones((128, 128), np.float32))
    r = np.arange(128)[:, None]
    s = np.arange(256)[None, :]
    band = (s >= r + 1) & (s <= r + 128)
    first = band & (s >= 128)
    mb = np.where(band, 0.0, -1e30).astype(np.float32)
    mf = np.where(first, 0.0, -1e30).astype(np.float32)
    return cid, triu, mb, mf


def shard_inputs(inputs, b, L):
    cid, triu, mb, mf = make_const_inputs()
    g = lambda n: np.ascontiguousarray(np.asarray(inputs[n], dtype=np.float32))
    bq_a, bq_s = g("L0_attn_bqkv_a"), g("L0_attn_bqkv_s")
    brows = np.zeros((4, 1024), np.float32)
    brows[0] = bq_a[2048:3072]
    brows[1] = bq_s[2048:3072]
    brows[2] = g("L0_attn_bo")
    brows[3] = g("L0_attn_b2m")
    return {
        "x": np.ascontiguousarray(np.asarray(inputs["L0_attn_x"],
                                             dtype=np.float32)[b, :L]),
        "Wqkv_a": g("L0_attn_Wqkv_a"), "Wp_a": g("L0_attn_Wp_a"),
        "Wqkv_s": g("L0_attn_Wqkv_s"), "Wo_s": g("L0_attn_Wo_s"),
        "Wo": g("L0_attn_Wo"), "W1": g("L0_attn_W1"), "W2": g("L0_attn_W2"),
        "th": g("L0_ffn_theta"), "thn": g("L0_ffn_theta_noise"),
        "ph": g("L0_ffn_phi"), "phn": g("L0_ffn_phi_noise"),
        "cid": cid, "triu": triu, "mb": mb, "mf": mf,
        "cid16": cid[:, 0:128].astype(ml_dtypes.bfloat16),
        "mb16": mb.astype(ml_dtypes.bfloat16),
        "mf16": mf.astype(ml_dtypes.bfloat16),
        "bqk_a": np.ascontiguousarray(bq_a[0:2048].reshape(16, 128).T),
        "bqk_s": np.ascontiguousarray(bq_s[0:2048].reshape(16, 128).T),
        "bp_a": np.ascontiguousarray(g("L0_attn_bp_a").reshape(8, 128).T),
        "bo_s": np.ascontiguousarray(g("L0_attn_bo_s").reshape(8, 128).T),
        "b1mt": np.ascontiguousarray(g("L0_attn_b1m").reshape(32, 128).T),
        "g1t": np.ascontiguousarray(g("L0_attn_g1").reshape(8, 128).T),
        "b1t": np.ascontiguousarray(g("L0_attn_b1").reshape(8, 128).T),
        "gmt": np.ascontiguousarray(g("L0_attn_gm").reshape(8, 128).T),
        "bmt": np.ascontiguousarray(g("L0_attn_bm").reshape(8, 128).T),
        "brows": brows,
    }


def kernel(**inputs):
    L = 4096
    nc = build_nc(L)
    in_maps = [shard_inputs(inputs, b, L) for b in range(8)]
    res = run_bass_kernel_spmd(nc, in_maps, core_ids=list(range(8)))
    out = np.stack([np.asarray(r["out"]).reshape(L, D) for r in res.results])
    return out.astype(np.float32)


# version 06
# speedup vs baseline: 1.2132x; 1.0627x over previous
import ml_dtypes
import numpy as np
from contextlib import ExitStack

import concourse.bass as bass
import concourse.mybir as mybir
from concourse.bass_utils import run_bass_kernel_spmd

F32 = mybir.dt.float32
BF16 = mybir.dt.bfloat16
AF = mybir.ActivationFunctionType
ALU = mybir.AluOpType
AX = mybir.AxisListType

D, H, E = 1024, 16, 64
T, NCH, KC = 512, 4, 8
EPS = 1e-5

_AR = {}
_off = 0
def _reg(name, cols):
    global _off
    _AR[name] = (_off, cols)
    _off += (cols + 15) // 16 * 16
for _n, _c in [
    ("R0", 4096), ("R1", 4096), ("R2", 4096), ("R3", 4096), ("R4", 4096),
    ("R5", 5200), ("R9", 4096), ("R10", 4096), ("R11", 4096),
    ("ring0", 4096), ("ring1", 4096),
    ("kprev", 1024), ("pT", 512), ("ATm", 384), ("kT", 128),
    ("SZ", 520), ("sqdump", 1024), ("stats", 32), ("stats2", 32),
    ("dens", 16),
    ("brows", 2048), ("allones", 128), ("cid", 192), ("triu", 128),
    ("m16", 320),
    ("bqk_a", 16), ("bqk_s", 16), ("bp_a", 8), ("bo_s", 8), ("b1mt", 32),
    ("g1t", 8), ("b1t", 8), ("gmt", 8), ("bmt", 8),
]:
    _reg(_n, _c)
ARENA_COLS = _off
assert ARENA_COLS <= 53200, ARENA_COLS


def rkeys(region, n=8):
    return tuple(f"{region}:{i}" for i in range(n))


class KB:
    ENGS = ("pe", "dve", "act", "sp", "gp")
    CHANS = ("sp", "gp", "r0", "r1", "ws0", "ws1", "ws2", "ws3")
    DMAQ = CHANS
    ALL = ENGS + ("r0", "r1", "ws0", "ws1", "ws2", "ws3")

    def __init__(self):
        self.ops = {n: [] for n in self.ENGS}
        self.cnt = {n: 0 for n in self.ALL}
        self.waited = {c: {p: 0 for p in self.ALL} for c in self.ENGS}
        self.reg = {}
        self.pend = {n: [] for n in self.ENGS}

    def _need(self, eng, waits, selfwaits, p, v):
        if v is None:
            raise RuntimeError("dependency on an unsignalled op")
        if p == eng:
            if eng == "pe":
                return
            if v > self.waited[eng][eng]:
                selfwaits.append(v)
                self.waited[eng][eng] = v
            return
        if p in self.CHANS:
            v = self.cnt[p]
        if v > self.waited[eng][p]:
            waits.append((p, v))
            self.waited[eng][p] = v

    def op(self, eng, fn, reads=(), writes=(), sreads=(), pre=(), sig=True,
           chan=None):
        sigchan = chan or eng
        waits, selfwaits = [], []
        if sigchan in self.CHANS and self.cnt[sigchan] > 0:
            self._need(eng, waits, selfwaits, sigchan, self.cnt[sigchan])
        for k in tuple(reads) + tuple(sreads):
            r = self.reg.get(k)
            if r and r["w"] is not None:
                self._need(eng, waits, selfwaits, *r["w"])
        for k in tuple(writes) + tuple(pre):
            r = self.reg.get(k)
            if r:
                if r["w"] is not None:
                    self._need(eng, waits, selfwaits, *r["w"])
                for p, v in r["r"].items():
                    self._need(eng, waits, selfwaits, p, v)
        if sig:
            self.cnt[sigchan] += 1
            val = self.cnt[sigchan]
            for k in self.pend[eng]:
                rr = self.reg.setdefault(k, {"w": None, "r": {}})
                rr["r"][eng] = val if sigchan == eng else self.cnt[eng]
            self.pend[eng].clear()
            for k in tuple(reads) + tuple(sreads):
                rr = self.reg.setdefault(k, {"w": None, "r": {}})
                rr["r"][sigchan] = val
        else:
            val = None
            for k in tuple(reads) + tuple(sreads):
                self.pend[eng].append(k)
        for k in writes:
            if not sig:
                raise RuntimeError("writes require sig=True")
            self.reg[k] = {"w": (sigchan, val), "r": {}}
        self.ops[eng].append((fn, waits, selfwaits, sig, sigchan))


def build_nc(L, taps=()):
    NG = L // T
    nc = bass.Bass("TRN2", target_bir_lowering=False, debug=False)

    dr = {}
    dr["x"] = nc.dram_tensor("x", [L, D], F32, kind="ExternalInput").ap()
    for n, shp in [
        ("Wqkv_a", [D, 3 * D]), ("Wp_a", [D, D]), ("Wqkv_s", [D, 3 * D]),
        ("Wo_s", [D, D]), ("Wo", [D, D]), ("W1", [D, 4 * D]), ("W2", [4 * D, D]),
        ("th", [D, D]), ("thn", [D, D]), ("ph", [D, D]), ("phn", [D, D]),
        ("cid", [128, 192]), ("triu", [128, 128]), ("mb", [128, 256]),
        ("mf", [128, 256]), ("bqk_a", [128, 16]), ("bqk_s", [128, 16]),
        ("bp_a", [128, 8]), ("bo_s", [128, 8]), ("b1mt", [128, 32]),
        ("g1t", [128, 8]), ("b1t", [128, 8]), ("gmt", [128, 8]), ("bmt", [128, 8]),
        ("brows", [4, 1024]),
    ]:
        dr[n] = nc.dram_tensor(n, shp, F32, kind="ExternalInput").ap()
    for n, shp in [("cid16", [128, 128]), ("mb16", [128, 256]),
                   ("mf16", [128, 256])]:
        dr[n] = nc.dram_tensor(n, shp, BF16, kind="ExternalInput").ap()
    dr["out"] = nc.dram_tensor("out", [L, D], F32, kind="ExternalOutput").ap()
    for w, shp in [("Wqkv_a", [D, 3 * D]), ("Wqkv_s", [D, 3 * D]),
                   ("Wp_a", [D, D]), ("Wo_s", [D, D]), ("Wo", [D, D]),
                   ("W1", [D, 4 * D]), ("W2", [4 * D, D]),
                   ("csc", [D, D]), ("ssc", [D, D])]:
        dr[w + "_hi"] = nc.dram_tensor(w + "_hi", shp, BF16, kind="Internal").ap()
        dr[w + "_lo"] = nc.dram_tensor(w + "_lo", shp, BF16, kind="Internal").ap()
    tap_dr = {}
    for tname, tcols in taps:
        tap_dr[tname] = nc.dram_tensor("tap_" + tname, [128, tcols], F32,
                                       kind="ExternalOutput").ap()

    st = ExitStack()
    arena = st.enter_context(nc.sbuf_tensor("arena", [128, ARENA_COLS], F32)).ap()
    A = {n: arena[:, o:o + c] for n, (o, c) in _AR.items()}
    A["eluscr"] = A["sqdump"][:, 0:512]
    _m16 = A["m16"].bitcast(BF16)
    A["cid16"] = _m16[:, 0:128]
    A["mb16"] = _m16[:, 128:384]
    A["mf16"] = _m16[:, 384:640]
    A["pexp"] = A["sqdump"][:, 0:512]
    SCRK = ("sqd", "elusc", "pexp0", "pexp1", "aff", "g32a", "g32b")
    def scr_pre(own):
        return tuple(k for k in SCRK if k != own)
    PS = [st.enter_context(nc.psum_tensor(f"pb{i}", [128, 512], F32)).ap()
          for i in range(8)]

    kb = KB()

    xr = dr["x"].rearrange("(g c p) d -> g p c d", c=NCH, p=128)
    outr = dr["out"].rearrange("(g c p) d -> g p c d", c=NCH, p=128)
    wv = {}
    for n in ("Wqkv_a", "Wqkv_s", "Wp_a", "Wo_s", "Wo", "W1"):
        wv[n] = dr[n].rearrange("(kc p) n -> p kc n", p=128)
    wv["W2"] = dr["W2"].rearrange("(kg kc p) n -> kg p kc n", kg=4, p=128)
    for w in ("Wqkv_a", "Wqkv_s", "Wp_a", "Wo_s", "Wo", "W1", "csc", "ssc"):
        for h in ("_hi", "_lo"):
            wv[w + h] = dr[w + h].rearrange("(kc p) n -> p kc n", p=128)
    for h in ("_hi", "_lo"):
        wv["W2" + h] = dr["W2" + h].rearrange("(kg kc p) n -> kg p kc n",
                                              kg=4, p=128)

    def phi(region, k):
        bc = A[region].bitcast(BF16)
        return bc[:, k * 512:(k + 1) * 512]

    def plo(region, k):
        bc = A[region].bitcast(BF16)
        return bc[:, 4096 + k * 512: 4096 + (k + 1) * 512]

    ring = [A["ring0"], A["ring1"]]
    ring_i = [0]

    def load_piece(src_ap, ncols=512, reads=()):
        i = ring_i[0] % 2
        ring_i[0] += 1
        buf = ring[i]
        key = f"ring{i}"
        dst = buf.rearrange("p (kc n) -> p kc n", n=ncols)
        kb.op("sp", lambda e, d=dst, s=src_ap: e.dma_start(out=d, in_=s),
              reads=reads, writes=(key,), chan=f"r{i}")
        return buf, key

    def load_piece3(src_hi, src_lo, reads=()):
        i = ring_i[0] % 2
        ring_i[0] += 1
        bc = ring[i].bitcast(BF16)
        key = f"ring{i}"
        bh = bc[:, 0:4096].rearrange("p (kc n) -> p kc n", n=512)
        bl = bc[:, 4096:8192].rearrange("p (kc n) -> p kc n", n=512)
        kb.op("sp", lambda e, s=src_hi: e.dma_start(out=bh, in_=s),
              reads=reads, writes=(key,), chan=f"r{i}")
        kb.op("sp", lambda e, s=src_lo: e.dma_start(out=bl, in_=s),
              reads=reads, writes=(key,), chan=f"r{i}")
        return bh, bl, key

    pj = [0]

    def next_bank():
        i = pj[0] % 2
        pj[0] += 1
        return PS[i], f"P{i}"

    _wsi = [0]
    for n in ("cid", "triu", "cid16", "mb16", "mf16",
              "bqk_a", "bqk_s", "bp_a", "bo_s",
              "b1mt", "g1t", "b1t", "gmt", "bmt"):
        kb.op("sp", lambda e, d=A[n], s=dr[n]: e.dma_start(out=d, in_=s),
              writes=(n,))
    _BR = {0: (0, 0), 1: (32, 0), 2: (64, 0), 3: (0, 1024)}
    for r in range(4):
        bp, bc = _BR[r]
        kb.op("sp", lambda e, d=A["brows"][bp:bp + 1, bc:bc + 1024],
              s=dr["brows"][r:r + 1, :]: e.dma_start(out=d, in_=s),
              writes=(f"brows{r}",))
    kb.op("dve", lambda e: e.memset(A["allones"], 1.0), writes=("allones",))
    kb.op("dve", lambda e: e.memset(A["SZ"], 0.0),
          writes=tuple(f"SZ:{h}" for h in range(H)))
    kb.op("dve", lambda e: e.memset(A["kprev"], 0.0), writes=("kprev",))
    kb.op("dve", lambda e: e.memset(A["R5"], 0.0),
          writes=tuple(f"vex:{s}" for s in range(5)))
    vext_ones = A["R5"].rearrange("p (g e) -> p g e", e=65)[:, :, 64:65]
    kb.op("dve", lambda e: e.memset(vext_ones, 1.0),
          pre=tuple(f"vex:{s_}" for s_ in range(5)), writes=("R5ones",))
    kb.op("dve", lambda e: e.tensor_scalar(A["bqk_a"][:, 0:8], A["bqk_a"][:, 0:8],
                                           0.125, None, ALU.mult),
          reads=("bqk_a",), writes=("bqk_a",))

    spanA = arena[:, _AR["R0"][0]:_AR["R0"][0] + 8192]
    spanB = arena[:, _AR["R2"][0]:_AR["R2"][0] + 8192]
    kA = ("R0",) + rkeys("R1")
    kB = rkeys("R2") + rkeys("R3")
    for src, nsrc, scr, iscos in (("th", "thn", "csc", True),
                                  ("ph", "phn", "ssc", False)):
        srcA = dr[src].rearrange("(kc p) n -> p kc n", p=128)
        srcB = dr[nsrc].rearrange("(kc p) n -> p kc n", p=128)
        dstA = spanA.rearrange("p (kc n) -> p kc n", n=1024)
        dstB = spanB.rearrange("p (kc n) -> p kc n", n=1024)
        kb.op("sp", lambda e, d=dstA, s=srcA: e.dma_start(out=d, in_=s), writes=kA)
        kb.op("sp", lambda e, d=dstB, s=srcB: e.dma_start(out=d, in_=s), writes=kB)
        kb.op("dve", lambda e: e.tensor_add(spanA, spanA, spanB),
              reads=kA + kB, writes=kA)
        if iscos:
            kb.op("act", lambda e: e.activation(spanB, spanA, AF.Sin, scale=0.5),
                  reads=kA, writes=kB)
            kb.op("act", lambda e: e.activation(spanA, spanB, AF.Square),
                  reads=kB, writes=kA)
            kb.op("dve", lambda e: e.tensor_scalar(spanA, spanA, -2.0, 1.0,
                                                   ALU.mult, ALU.add),
                  reads=kA, writes=kA)
        else:
            kb.op("act", lambda e: e.activation(spanA, spanA, AF.Sin, scale=1.0),
                  reads=kA, writes=kA)
        bcB = spanB.bitcast(BF16)
        hiv, lov = bcB[:, 0:8192], bcB[:, 8192:16384]
        kb.op("act", lambda e: e.activation(hiv, spanA, AF.Copy),
              reads=kA, writes=kB)
        kb.op("dve", lambda e: e.scalar_tensor_tensor(
            lov, spanA, 1.0, hiv, ALU.mult, ALU.subtract),
            reads=kA + kB, writes=kB)
        for h, v in (("_hi", hiv), ("_lo", lov)):
            dstv = wv[scr + h]
            srcv = v.rearrange("p (kc n) -> p kc n", n=1024)
            kb.op("gp", lambda e, d=dstv, s=srcv: e.dma_start(out=d, in_=s),
                  reads=kB, writes=(scr + h,), chan=f"ws{_wsi[0] % 4}")
            _wsi[0] += 1

    _splits = []
    for w, npieces in (("Wqkv_a", 6), ("Wqkv_s", 6), ("Wp_a", 2), ("Wo_s", 2),
                       ("Wo", 2), ("W1", 8)):
        for j in range(npieces):
            _splits.append((wv[w][:, :, j * 512:(j + 1) * 512],
                            wv[w + "_hi"][:, :, j * 512:(j + 1) * 512],
                            wv[w + "_lo"][:, :, j * 512:(j + 1) * 512]))
    for kg in range(4):
        for nt in range(2):
            _splits.append((wv["W2"][kg][:, :, nt * 512:(nt + 1) * 512],
                            wv["W2_hi"][kg][:, :, nt * 512:(nt + 1) * 512],
                            wv["W2_lo"][kg][:, :, nt * 512:(nt + 1) * 512]))
    spl_bc = A["R9"].bitcast(BF16)
    spl_hi = spl_bc[:, 0:4096].rearrange("p (kc n) -> p kc n", n=512)
    spl_lo = spl_bc[:, 4096:8192].rearrange("p (kc n) -> p kc n", n=512)
    for src, dhi, dlo in _splits:
        buf, rk = load_piece(src)
        b3 = buf.rearrange("p (kc n) -> p kc n", n=512)
        kb.op("act", lambda e, b3=b3: e.activation(spl_hi, b3, AF.Copy),
              reads=(rk,), pre=rkeys("R9"), writes=rkeys("R9"))
        kb.op("dve", lambda e, b3=b3: e.scalar_tensor_tensor(
            spl_lo, b3, 1.0, spl_hi, ALU.mult, ALU.subtract),
            reads=(rk,) + rkeys("R9"), writes=rkeys("R9"))
        kb.op("gp", lambda e, d=dhi: e.dma_start(out=d, in_=spl_hi),
              reads=rkeys("R9"), chan=f"ws{_wsi[0] % 4}")
        _wsi[0] += 1
        kb.op("gp", lambda e, d=dlo: e.dma_start(out=d, in_=spl_lo),
              reads=rkeys("R9"), chan=f"ws{_wsi[0] % 4}")
        _wsi[0] += 1

    def x_load(g):
        dst = A["R0"].rearrange("p (c d) -> p c d", d=D)
        kb.op("gp", lambda e, d=dst, s=xr[g]: e.dma_start(out=d, in_=s),
              writes=("R0",))

    def ln_stats(src_keys, src_slab, sb="stats"):
        stt_ = A[sb]
        sums, ssq = stt_[:, 0:4], stt_[:, 4:8]
        st_m, st_r = stt_[:, 8:12], stt_[:, 12:16]
        scr1, scr2 = stt_[:, 16:20], stt_[:, 20:24]
        p = sb
        if sb == "stats":
            dump, dkeys = A["sqdump"], ("sqd",) + scr_pre("sqd")
        else:
            dump, dkeys = A["R4"][:, 0:1024], ("R4:0", "R4:1")
        for c in range(NCH):
            xc = src_slab[:, c * D:(c + 1) * D]
            kb.op("act", lambda e, xc=xc, c=c, dump=dump: e.activation(
                dump, xc, AF.Square, accum_out=ssq[:, c:c + 1]),
                reads=src_keys,
                pre=((f"{p}_ssq",) if c == 0 else ()) + dkeys,
                writes=((f"{p}_ssq",) if c == 3 else ()) + dkeys, sig=True)
        kb.op("dve", lambda e: e.reduce_sum(
            sums, src_slab.rearrange("p (c d) -> p c d", d=D), axis=AX.X),
            reads=src_keys, pre=(f"{p}_sum",), writes=(f"{p}_sum",))
        kb.op("dve", lambda e: e.tensor_scalar(st_m, sums, 1.0 / D, None, ALU.mult),
              reads=(f"{p}_sum",), writes=(f"{p}_m",))
        kb.op("dve", lambda e: e.tensor_scalar(scr1, ssq, 1.0 / D, EPS,
                                               ALU.mult, ALU.add),
              reads=(f"{p}_ssq",), writes=(f"{p}_s1",))
        kb.op("act", lambda e: e.activation(scr2, st_m, AF.Square),
              reads=(f"{p}_m",), writes=(f"{p}_s2",))
        kb.op("dve", lambda e: e.tensor_sub(scr1, scr1, scr2),
              reads=(f"{p}_s1", f"{p}_s2"), writes=(f"{p}_s1",))
        kb.op("act", lambda e: e.activation(scr2, scr1, AF.Ln),
              reads=(f"{p}_s1",), writes=(f"{p}_s2",))
        kb.op("act", lambda e: e.activation(st_r, scr2, AF.Exp, scale=-0.5),
              reads=(f"{p}_s2",), writes=(f"{p}_r",))

    def ln_norm(src_keys, src_slab, dst_region, dst_slab, sb="stats"):
        stt_ = A[sb]
        st_m, st_r = stt_[:, 8:12], stt_[:, 12:16]
        p = sb
        for c in range(NCH):
            xc = src_slab[:, c * D:(c + 1) * D]
            oc = dst_slab[:, c * D:(c + 1) * D]
            wk = (f"{dst_region}:{2 * c}", f"{dst_region}:{2 * c + 1}") \
                if dst_region != "R0" else ("R0",)
            kb.op("dve", lambda e, xc=xc, oc=oc, c=c, st_m=st_m, st_r=st_r:
                  e.tensor_scalar(oc, xc, st_m[:, c:c + 1], st_r[:, c:c + 1],
                                  ALU.subtract, ALU.mult),
                  reads=src_keys, sreads=(f"{p}_m", f"{p}_r"), writes=wk)

    def transpose_slab(src_keys, src_slab, dst_region, affine=None):
        for k in range(KC):
            bi = [6, 2, 3][k % 3]
            bank, bk = PS[bi], f"P{bi}"
            for c in range(NCH):
                src = src_slab[:, c * D + k * 128: c * D + (k + 1) * 128]
                kb.op("pe", lambda e, s=src, c=c, bank=bank: e.transpose(
                    bank[:, c * 128:(c + 1) * 128], s, A["cid"][:, 0:128]),
                    reads=src_keys + ("cid",),
                    pre=(bk,) if c == 0 else (),
                    writes=(bk,) if c == 3 else (), sig=(c == 3))
            dk = (f"{dst_region}:{k}",)
            hid, lod = phi(dst_region, k), plo(dst_region, k)
            if affine is None:
                kb.op("act", lambda e, hid=hid, bank=bank: e.activation(
                    hid, bank[:, 0:512], AF.Copy),
                    reads=(bk,), pre=dk, writes=dk)
                kb.op("dve", lambda e, hid=hid, lod=lod, bank=bank:
                      e.scalar_tensor_tensor(
                          lod, bank[:, 0:512], 1.0, hid, ALU.mult, ALU.subtract),
                      reads=(bk,) + dk, writes=dk)
            else:
                gt, bt = affine
                ak = "aff" if k % 2 == 0 else "g32a"
                aff = A["sqdump"][:, 0:512] if k % 2 == 0 \
                    else A["sqdump"][:, 512:1024]
                kb.op("dve", lambda e, k=k, gt=gt, bt=bt, aff=aff, bank=bank:
                      e.tensor_scalar(
                          aff, bank[:, 0:512], A[gt][:, k:k + 1],
                          A[bt][:, k:k + 1], ALU.mult, ALU.add),
                      reads=(bk, gt, bt), pre=(ak,) + scr_pre(ak),
                      writes=(ak,))
                kb.op("act", lambda e, hid=hid, aff=aff: e.activation(
                    hid, aff, AF.Copy),
                    reads=(ak,), pre=dk, writes=dk)
                kb.op("dve", lambda e, hid=hid, lod=lod, aff=aff:
                      e.scalar_tensor_tensor(
                          lod, aff, 1.0, hid, ALU.mult, ALU.subtract),
                      reads=(ak,) + dk, writes=dk)

    def modeA(w_piece_srcs, in_region, in_slab, n_chunks, evac):
        for j, src in enumerate(w_piece_srcs):
            buf, rk = load_piece(src)
            for nsub in range(4):
                nchunk = 4 * j + nsub
                if nchunk >= n_chunks:
                    break
                bank, bkey = next_bank()
                for k in range(KC):
                    stat = buf[:, k * T + nsub * 128: k * T + (nsub + 1) * 128]
                    mov = in_slab[:, k * T:(k + 1) * T]
                    kb.op("pe", lambda e, st_=stat, mv=mov, k=k, bank=bank:
                          e.matmul(bank, st_, mv, start=(k == 0),
                                   stop=(k == KC - 1)),
                          reads=(rk, f"{in_region}:{k}"),
                          pre=(bkey,) if k == 0 else (),
                          writes=(bkey,) if k == KC - 1 else (),
                          sig=(k == KC - 1))
                evac(nchunk, bank, bkey)

    def modeB(w_piece_srcs, in_region, in_slab, evac, bias_row=None,
              resid=(), wextra=()):
        for nt, src in enumerate(w_piece_srcs):
            buf, rk = load_piece(src, reads=wextra)
            banks = [(PS[i], f"P{i}") for i in range(4)]
            tail = ([("bias", bias_row)] if bias_row is not None else []) + \
                   [("resid", rv) for rv in resid]
            for k in range(KC):
                for ts in range(NCH):
                    bank, bkey = banks[ts]
                    stat = in_slab[:, k * T + ts * 128: k * T + (ts + 1) * 128]
                    mov = buf[:, k * 512:(k + 1) * 512]
                    last = (k == KC - 1) and not tail
                    kb.op("pe", lambda e, st_=stat, mv=mov, bank=bank, k=k,
                          last=last: e.matmul(bank, st_, mv, start=(k == 0),
                                              stop=last),
                          reads=(rk, f"{in_region}:{k}"),
                          pre=(bkey,) if k == 0 else (),
                          writes=(bkey,) if last else (), sig=last)
            for ts in range(NCH):
                bank, bkey = banks[ts]
                for i, (kind, val) in enumerate(tail):
                    last = (i == len(tail) - 1)
                    if kind == "bias":
                        r = val
                        bp, bc = _BR[r]
                        kb.op("pe", lambda e, bank=bank, bp=bp, bc=bc, nt=nt,
                              last=last: e.matmul(
                                  bank, A["allones"][bp:bp + 1, 0:128],
                                  A["brows"][bp:bp + 1,
                                             bc + nt * 512:bc + (nt + 1) * 512],
                                  start=False, stop=last),
                              reads=("allones", f"brows{r}"),
                              writes=(bkey,) if last else (), sig=last)
                    else:
                        slab, skeys = val
                        mv = slab[:, ts * D + nt * 512: ts * D + (nt + 1) * 512]
                        kb.op("pe", lambda e, bank=bank, mv=mv, last=last:
                              e.matmul(bank, A["cid"][:, 0:128], mv,
                                       start=False, stop=last),
                              reads=("cid",) + tuple(skeys),
                              writes=(bkey,) if last else (), sig=last)
                evac(ts, nt, bank, bkey)

    def modeA3(w_srcs, in_region, n_chunks, evac):
        for j, (shi, slo) in enumerate(w_srcs):
            bh, bl, rk = load_piece3(shi, slo)
            for nsub in range(4):
                nchunk = 4 * j + nsub
                if nchunk >= n_chunks:
                    break
                bank, bkey = next_bank()
                for k in range(KC):
                    wh = bh[:, k, nsub * 128:(nsub + 1) * 128]
                    wl = bl[:, k, nsub * 128:(nsub + 1) * 128]
                    xh, xl = phi(in_region, k), plo(in_region, k)
                    ik = (rk, f"{in_region}:{k}")
                    kb.op("pe", lambda e, wh=wh, xh=xh, bank=bank, k=k:
                          e.matmul(bank, wh, xh, start=(k == 0), stop=False),
                          reads=ik, pre=(bkey,) if k == 0 else (), sig=False)
                    kb.op("pe", lambda e, wh=wh, xl=xl, bank=bank:
                          e.matmul(bank, wh, xl, start=False, stop=False),
                          reads=ik, sig=False)
                    kb.op("pe", lambda e, wl=wl, xh=xh, bank=bank, k=k:
                          e.matmul(bank, wl, xh, start=False,
                                   stop=(k == KC - 1)),
                          reads=ik, writes=(bkey,) if k == KC - 1 else (),
                          sig=(k == KC - 1))
                evac(nchunk, bank, bkey)

    def modeB3(w_srcs, in_region, evac, bias_row=None, resid=(), wextra=()):
        for nt, (shi, slo) in enumerate(w_srcs):
            bh, bl, rk = load_piece3(shi, slo, reads=wextra)
            banks = [(PS[i], f"P{i}") for i in range(4)]
            tail = ([("bias", bias_row)] if bias_row is not None else []) + \
                   [("resid", rv) for rv in resid]
            for k in range(KC):
                for ts in range(NCH):
                    bank, bkey = banks[ts]
                    xh = phi(in_region, k)[:, ts * 128:(ts + 1) * 128]
                    xl = plo(in_region, k)[:, ts * 128:(ts + 1) * 128]
                    mh, ml = bh[:, k, :], bl[:, k, :]
                    ik = (rk, f"{in_region}:{k}")
                    kb.op("pe", lambda e, xh=xh, mh=mh, bank=bank, k=k:
                          e.matmul(bank, xh, mh, start=(k == 0), stop=False),
                          reads=ik, pre=(bkey,) if k == 0 else (), sig=False)
                    kb.op("pe", lambda e, xh=xh, ml=ml, bank=bank:
                          e.matmul(bank, xh, ml, start=False, stop=False),
                          reads=ik, sig=False)
                    bank_last = (k == KC - 1 and not tail)
                    kb.op("pe", lambda e, xl=xl, mh=mh, bank=bank,
                          bank_last=bank_last: e.matmul(
                              bank, xl, mh, start=False, stop=bank_last),
                          reads=ik,
                          writes=(bkey,) if bank_last else (),
                          sig=(bank_last or (k == KC - 1 and ts == NCH - 1)))
            for ts in range(NCH):
                bank, bkey = banks[ts]
                for i, (kind, val) in enumerate(tail):
                    lastt = (i == len(tail) - 1)
                    if kind == "bias":
                        r = val
                        bp, bc = _BR[r]
                        kb.op("pe", lambda e, bank=bank, bp=bp, bc=bc, nt=nt,
                              lastt=lastt: e.matmul(
                                  bank, A["allones"][bp:bp + 1, 0:128],
                                  A["brows"][bp:bp + 1,
                                             bc + nt * 512:bc + (nt + 1) * 512],
                                  start=False, stop=lastt),
                              reads=("allones", f"brows{r}"),
                              writes=(bkey,) if lastt else (), sig=lastt)
                    else:
                        slab, skeys = val
                        mv = slab[:, ts * D + nt * 512: ts * D + (nt + 1) * 512]
                        kb.op("pe", lambda e, bank=bank, mv=mv, lastt=lastt:
                              e.matmul(bank, A["cid"][:, 0:128], mv,
                                       start=False, stop=lastt),
                              reads=("cid",) + tuple(skeys),
                              writes=(bkey,) if lastt else (), sig=lastt)
                evac(ts, nt, bank, bkey)

    def tap(name, slab, cols, reads):
        if name in tap_dr:
            kb.op("gp", lambda e: e.dma_start(out=tap_dr[name],
                                              in_=slab[:, 0:cols]),
                  reads=reads)

    vexr = A["R5"].rearrange("p (s q) -> p s q", s=5)

    def emit_group(g):
        is_first = (g == 0)

        ln_stats(("R0",), A["R0"], sb="stats2")
        ln_norm(("R0",), A["R0"], "R10", A["R10"], sb="stats2")
        tap(f"xn{g}", A["R10"], 4096, rkeys("R10"))
        transpose_slab(rkeys("R10"), A["R10"], "R2", affine=("g1t", "b1t"))

        def evac_qk(nchunk, bank, bkey):
            if nchunk < 8:
                dst, key, scale = (A["R3"][:, nchunk * T:(nchunk + 1) * T],
                                   f"R3:{nchunk}", 0.125)
            else:
                dst, key, scale = (A["R4"][:, (nchunk - 8) * T:(nchunk - 7) * T],
                                   f"R4:{nchunk - 8}", 1.0)
            kb.op("act", lambda e, d=dst, nchunk=nchunk, scale=scale:
                  e.activation(d, bank[:, 0:512], AF.Identity,
                               bias=A["bqk_a"][:, nchunk:nchunk + 1], scale=scale),
                  reads=(bkey, "bqk_a"), writes=(key,))
        modeA3([(wv["Wqkv_a_hi"][:, :, j * 512:(j + 1) * 512],
                 wv["Wqkv_a_lo"][:, :, j * 512:(j + 1) * 512])
                for j in range(4)], "R2", 16, evac_qk)

        def evac_v(ts, nt, bank, bkey):
            dst = A["R5"].rearrange("p (s h e) -> p s h e", s=5, e=65)[
                :, ts + 1, 8 * nt:8 * (nt + 1), 0:64]
            src = bank.rearrange("p (h e) -> p h e", e=64)[:, 0:8, :]
            kb.op("dve", lambda e, d=dst, s=src: e.tensor_copy(d, s),
                  reads=(bkey,), pre=(f"vex:{ts + 1}",),
                  writes=(f"vex:{ts + 1}",))
        modeB3([(wv["Wqkv_a_hi"][:, :, 2048 + nt * 512: 2048 + (nt + 1) * 512],
                 wv["Wqkv_a_lo"][:, :, 2048 + nt * 512: 2048 + (nt + 1) * 512])
                for nt in range(2)], "R2", evac_v, bias_row=0)

        for c in range(NCH):
            def swa_scores(h):
                j, base, hp = h // 2, (h % 2) * 64, h % 2
                sc, sck = PS[4 + hp], f"P{4 + hp}"
                q = A["R3"][base:base + 64, j * T + c * 128: j * T + (c + 1) * 128]
                if c == 0:
                    kp = A["kprev"][base:base + 64, j * 128:(j + 1) * 128]
                    kcur = A["R4"][base:base + 64, j * T: j * T + 128]
                    kb.op("pe", lambda e, q=q, kp=kp, sc=sc: e.matmul(
                        sc[:, 0:128], q, kp, start=True, stop=False),
                        reads=(f"R3:{j}", "kprev"), pre=(sck,), sig=False)
                    kb.op("pe", lambda e, q=q, kc_=kcur, sc=sc: e.matmul(
                        sc[:, 128:256], q, kc_, start=False, stop=False),
                        reads=(f"R3:{j}", f"R4:{j}"), sig=False)
                else:
                    kc_ = A["R4"][base:base + 64,
                                  j * T + (c - 1) * 128: j * T + (c + 1) * 128]
                    kb.op("pe", lambda e, q=q, kc_=kc_, sc=sc: e.matmul(
                        sc[:, 0:256], q, kc_, start=True, stop=False),
                        reads=(f"R3:{j}", f"R4:{j}"), pre=(sck,), sig=False)
                mkey = "mf16" if (is_first and c == 0) else "mb16"
                kb.op("pe", lambda e, m=A[mkey], sc=sc: e.matmul(
                    sc[:, 0:256], A["cid16"], m, start=False, stop=True),
                    reads=("cid16", mkey), writes=(sck,))

            def swa_exp(h):
                hp = h % 2
                sc, sck = PS[4 + hp], f"P{4 + hp}"
                nm = A["dens"][:, 0:1]
                kb.op("dve", lambda e, sc=sc, nm=nm: e.reduce_max(
                    nm, sc[:, 0:256], axis=AX.X, negate=True),
                    reads=(sck,), writes=("nm",))
                px = A["pexp"][:, hp * 256:hp * 256 + 256]
                kb.op("act", lambda e, sc=sc, px=px, nm=nm: e.activation(
                    px, sc[:, 0:256], AF.Exp, bias=nm, scale=1.0),
                    reads=(sck, "nm"), pre=(f"pexp{hp}",) + scr_pre(f"pexp{hp}"),
                    writes=(f"pexp{hp}",))

            def swa_trans(h):
                hp = h % 2
                px = A["pexp"][:, hp * 256:hp * 256 + 256]
                tb = PS[6]
                kb.op("pe", lambda e, px=px, tb=tb: e.transpose(
                    tb[:, 0:128], px[:, 0:128], A["cid"][:, 0:128]),
                    reads=(f"pexp{hp}", "cid"), pre=("P6",), sig=False)
                kb.op("pe", lambda e, px=px, tb=tb: e.transpose(
                    tb[:, 128:256], px[:, 128:256], A["cid"][:, 0:128]),
                    reads=(f"pexp{hp}", "cid"), writes=("P6",))
                pt = A["pT"][:, hp * 256:hp * 256 + 256]
                kb.op("act", lambda e, tb=tb, pt=pt: e.activation(
                    pt, tb[:, 0:256], AF.Copy),
                    reads=("P6",), pre=(f"pT{hp}",), writes=(f"pT{hp}",))

            def swa_av(h):
                hp = h % 2
                pt = A["pT"][:, hp * 256:hp * 256 + 256]
                av = PS[7]
                vprev = vexr[:, c, h * 65:h * 65 + 65]
                vcur = vexr[:, c + 1, h * 65:h * 65 + 65]
                kb.op("pe", lambda e, pt=pt, v=vprev, av=av: e.matmul(
                    av[:, 0:65], pt[:, 0:128], v, start=True, stop=False),
                    reads=(f"pT{hp}", f"vex:{c}", "R5ones"), pre=("P7",),
                    sig=False)
                kb.op("pe", lambda e, pt=pt, v=vcur, av=av: e.matmul(
                    av[:, 0:65], pt[:, 128:256], v, start=False, stop=True),
                    reads=(f"pT{hp}", f"vex:{c + 1}", "R5ones"), writes=("P7",))
                dent, rden = A["dens"][:, 1:2], A["dens"][:, 2:3]
                kb.op("dve", lambda e, av=av, dent=dent: e.tensor_copy(
                    dent, av[:, 64:65]), reads=("P7",), writes=("dent",))
                kb.op("dve", lambda e, dent=dent, rden=rden: e.reciprocal(
                    rden, dent), reads=("dent",), writes=("rden",))
                dst = A["R1"][:, c * D + h * 64: c * D + (h + 1) * 64]
                rkey = f"R1:{2 * c + h // 8}"
                kb.op("dve", lambda e, av=av, dst=dst, rden=rden: e.tensor_scalar(
                    dst, av[:, 0:64], rden, None, ALU.mult),
                    reads=("P7",), sreads=("rden",),
                    pre=(rkey,) if h % 8 == 0 else (), writes=(rkey,))

            swa_scores(0)
            swa_scores(1)
            for h in range(H):
                swa_exp(h)
                swa_trans(h)
                if h + 2 < H:
                    swa_scores(h + 2)
                swa_av(h)
        tap(f"swa{g}", A["R1"], 4096, rkeys("R1"))

        kpdst = A["kprev"].rearrange("p (j q) -> p j q", q=128)
        kpsrc = A["R4"].rearrange("p (j t) -> p j t", t=T)[:, :, 384:512]
        kb.op("dve", lambda e: e.tensor_copy(kpdst, kpsrc),
              reads=rkeys("R4"), writes=("kprev",))
        kb.op("dve", lambda e: e.tensor_copy(A["R5"][:, 0:1040],
                                             A["R5"][:, 4160:5200]),
              reads=("vex:4", "R5ones"), pre=("vex:0",), writes=("vex:0",))

        def evac_qk_s(nchunk, bank, bkey):
            isq = nchunk < 8
            jj = nchunk if isq else nchunk - 8
            slab = A["R3"] if isq else A["R4"]
            dst = slab[:, jj * T:(jj + 1) * T]
            key = f"R3:{jj}" if isq else f"R4:{jj}"
            scale = 0.125 if isq else 1.0
            scr = A["eluscr"]
            kb.op("dve", lambda e, d=dst, nchunk=nchunk, scale=scale:
                  e.tensor_scalar(d, bank[:, 0:512],
                                  A["bqk_s"][:, nchunk:nchunk + 1], scale,
                                  ALU.add, ALU.mult),
                  reads=(bkey, "bqk_s"), pre=(key,), writes=(key,))
            kb.op("act", lambda e, d=dst: e.activation(scr, d, AF.Exp),
                  reads=(key,), pre=("elusc",) + scr_pre("elusc"),
                  writes=("elusc",))
            kb.op("dve", lambda e: e.tensor_scalar(scr, scr, 1.0, 1e-4,
                                                   ALU.min, ALU.add),
                  reads=("elusc",), writes=("elusc",))
            kb.op("dve", lambda e, d=dst: e.scalar_tensor_tensor(
                d, d, 0.0, scr, ALU.max, ALU.add),
                reads=(key, "elusc"), writes=(key,))
        modeA3([(wv["Wqkv_s_hi"][:, :, j * 512:(j + 1) * 512],
                 wv["Wqkv_s_lo"][:, :, j * 512:(j + 1) * 512])
                for j in range(4)], "R2", 16, evac_qk_s)
        modeB3([(wv["Wqkv_s_hi"][:, :, 2048 + nt * 512: 2048 + (nt + 1) * 512],
                 wv["Wqkv_s_lo"][:, :, 2048 + nt * 512: 2048 + (nt + 1) * 512])
                for nt in range(2)], "R2", evac_v, bias_row=1)
        tap(f"qps{g}", A["R3"], 4096, rkeys("R3"))
        tap(f"kps{g}", A["R4"], 4096, rkeys("R4"))

        for c in range(NCH):
            def lin_at(h):
                j, base, hp = h // 2, (h % 2) * 64, h % 2
                a3 = h % 3
                qs = A["R3"][base:base + 64, j * T + c * 128: j * T + (c + 1) * 128]
                ks = A["R4"][base:base + 64, j * T + c * 128: j * T + (c + 1) * 128]
                at, atk = PS[4 + a3], f"P{4 + a3}"
                kb.op("pe", lambda e, ks=ks, qs=qs, at=at: e.matmul(
                    at[:, 0:128], ks, qs, start=True, stop=True),
                    reads=(f"R3:{j}", f"R4:{j}"), pre=(atk,), writes=(atk,))

            def lin_ktt(h):
                j, base, hp = h // 2, (h % 2) * 64, h % 2
                ks = A["R4"][base:base + 64, j * T + c * 128: j * T + (c + 1) * 128]
                idb = A["cid"][0:64, 0:64] if base == 0 else \
                    A["cid"][64:128, 128:192]
                tb = PS[2 + hp]
                pk = f"P{2 + hp}"
                kb.op("pe", lambda e, ks=ks, idb=idb, tb=tb: e.transpose(
                    tb[:, 0:64], ks, idb),
                    reads=(f"R4:{j}", "cid"), pre=(pk,), writes=(pk,))

            def lin_mask(h):
                hp, a3 = h % 2, h % 3
                at, atk = PS[4 + a3], f"P{4 + a3}"
                atm = A["ATm"][:, a3 * 128:a3 * 128 + 128]
                kb.op("dve", lambda e, at=at, atm=atm: e.scalar_tensor_tensor(
                    atm, at[:, 0:128], 1.0, A["triu"], ALU.mult, ALU.mult),
                    reads=(atk, "triu"), pre=(f"ATm{a3}",), writes=(f"ATm{a3}",))
                kt = A["kT"][:, hp * 64:hp * 64 + 64]
                pk = f"P{2 + hp}"
                kb.op("act", lambda e, kt=kt, hp=hp: e.activation(
                    kt, PS[2 + hp][:, 0:64], AF.Copy),
                    reads=(pk,), pre=(f"kT{hp}",), writes=(f"kT{hp}",))

            def lin_nd(h):
                j, base, hp = h // 2, (h % 2) * 64, h % 2
                a3 = h % 3
                qs = A["R3"][base:base + 64, j * T + c * 128: j * T + (c + 1) * 128]
                atm = A["ATm"][:, a3 * 128:a3 * 128 + 128]
                nd, ndk = PS[hp], f"P{hp}"
                sz = A["SZ"][base:base + 64, j * 65:(j + 1) * 65]
                vex = vexr[:, c + 1, h * 65:h * 65 + 65]
                kb.op("pe", lambda e, qs=qs, sz=sz, nd=nd: e.matmul(
                    nd[:, 0:65], qs, sz, start=True, stop=False),
                    reads=(f"R3:{j}", f"SZ:{h}"), pre=(ndk,), sig=False)
                kb.op("pe", lambda e, atm=atm, vex=vex, nd=nd: e.matmul(
                    nd[:, 0:65], atm, vex, start=False, stop=True),
                    reads=(f"ATm{a3}", f"vex:{c + 1}", "R5ones"), writes=(ndk,))
                kt = A["kT"][:, hp * 64:hp * 64 + 64]
                szp = PS[7]
                tp = None if base == 0 else (0, 64)
                kb.op("pe", lambda e, kt=kt, vex=vex, szp=szp, base=base, tp=tp:
                      e.matmul(szp[base:base + 64, 0:65], kt, vex,
                               start=True, stop=True, tile_position=tp),
                      reads=(f"kT{hp}", f"vex:{c + 1}", "R5ones"),
                      pre=("P7",), writes=("P7",))

            def lin_out(h):
                j, base, hp = h // 2, (h % 2) * 64, h % 2
                nd, ndk = PS[hp], f"P{hp}"
                dent, rden = A["dens"][:, 4:5], A["dens"][:, 5:6]
                kb.op("dve", lambda e, nd=nd, dent=dent: e.tensor_scalar(
                    dent, nd[:, 64:65], 1e-6, None, ALU.add),
                    reads=(ndk,), writes=("dent2",))
                kb.op("dve", lambda e, dent=dent, rden=rden: e.reciprocal(
                    rden, dent), reads=("dent2",), writes=("rden2",))
                dst = A["R10"][:, c * D + h * 64: c * D + (h + 1) * 64]
                rkey = f"R10:{2 * c + h // 8}"
                kb.op("dve", lambda e, nd=nd, dst=dst, rden=rden: e.tensor_scalar(
                    dst, nd[:, 0:64], rden, None, ALU.mult),
                    reads=(ndk,), sreads=("rden2",),
                    pre=(rkey,) if h % 8 == 0 else (), writes=(rkey,))
                sz = A["SZ"][base:base + 64, j * 65:(j + 1) * 65]
                szp = PS[7]
                kb.op("dve", lambda e, sz=sz, szp=szp, base=base: e.tensor_add(
                    sz, sz, szp[base:base + 64, 0:65]),
                    reads=("P7", f"SZ:{h}"), writes=(f"SZ:{h}",))

            lin_at(0)
            lin_at(1)
            lin_at(2)
            lin_ktt(0)
            lin_ktt(1)
            for h in range(H):
                lin_mask(h)
                if h + 3 < H:
                    lin_at(h + 3)
                if h + 2 < H:
                    lin_ktt(h + 2)
                lin_nd(h)
                lin_out(h)
        tap(f"o{g}", A["R10"], 4096, rkeys("R10"))

        transpose_slab(rkeys("R1"), A["R1"], "R3")

        def evac_wp(nchunk, bank, bkey):
            dst = A["R9"][:, nchunk * T:(nchunk + 1) * T]
            kb.op("act", lambda e, d=dst, nchunk=nchunk: e.activation(
                d, bank[:, 0:512], AF.Identity,
                bias=A["bp_a"][:, nchunk:nchunk + 1]),
                reads=(bkey, "bp_a"), writes=(f"R9:{nchunk}",))
        modeA3([(wv["Wp_a_hi"][:, :, j * 512:(j + 1) * 512],
                 wv["Wp_a_lo"][:, :, j * 512:(j + 1) * 512])
                for j in range(2)], "R3", 8, evac_wp)

        transpose_slab(rkeys("R10"), A["R10"], "R4")

        def evac_wos(nchunk, bank, bkey):
            dst = A["R9"][:, nchunk * T:(nchunk + 1) * T]
            kb.op("dve", lambda e, d=dst, nchunk=nchunk: e.scalar_tensor_tensor(
                d, bank[:, 0:512], A["bo_s"][:, nchunk:nchunk + 1], d,
                ALU.add, ALU.add),
                reads=(bkey, "bo_s", f"R9:{nchunk}"), writes=(f"R9:{nchunk}",))
        modeA3([(wv["Wo_s_hi"][:, :, j * 512:(j + 1) * 512],
                 wv["Wo_s_lo"][:, :, j * 512:(j + 1) * 512])
                for j in range(2)], "R4", 8, evac_wos)

        for j in range(KC):
            a32 = A["R9"][:, j * T:(j + 1) * T]
            hid, lod = phi("R3", j), plo("R3", j)
            kb.op("act", lambda e, a32=a32, hid=hid: e.activation(
                hid, a32, AF.Copy),
                reads=(f"R9:{j}",), pre=(f"R3:{j}",), writes=(f"R3:{j}",))
            kb.op("dve", lambda e, a32=a32, hid=hid, lod=lod:
                  e.scalar_tensor_tensor(lod, a32, 1.0, hid,
                                         ALU.mult, ALU.subtract),
                  reads=(f"R9:{j}", f"R3:{j}"), writes=(f"R3:{j}",))

        def evac_wo(ts, nt, bank, bkey):
            dst = A["R10"][:, ts * D + nt * 512: ts * D + (nt + 1) * 512]
            kb.op("act", lambda e, d=dst, bank=bank: e.activation(
                d, bank, AF.Copy),
                reads=(bkey,), pre=(f"R10:{2 * ts + nt}",),
                writes=(f"R10:{2 * ts + nt}",))
        modeB3([(wv["Wo_hi"][:, :, nt * 512:(nt + 1) * 512],
                 wv["Wo_lo"][:, :, nt * 512:(nt + 1) * 512])
                for nt in range(2)], "R3", evac_wo, bias_row=2,
               resid=[(A["R0"], ("R0",))])
        tap(f"x1_{g}", A["R10"], 4096, rkeys("R10"))

        ln_stats(rkeys("R10"), A["R10"])
        ln_norm(rkeys("R10"), A["R10"], "R1", A["R1"])
        tap(f"h{g}", A["R1"], 4096, rkeys("R1"))
        transpose_slab(rkeys("R1"), A["R1"], "R2", affine=("gmt", "bmt"))

        gel_reg = ["R3", "R4", "R9", "R1"]
        gel_slab = [A["R3"], A["R4"], A["R9"], A["R1"]]

        def evac_w1(nchunk, bank, bkey):
            regn = gel_reg[nchunk // 8]
            kk = nchunk % 8
            gk = "g32a" if nchunk % 2 == 0 else "g32b"
            g32 = A["sqdump"][:, 0:512] if nchunk % 2 == 0 \
                else A["sqdump"][:, 512:1024]
            dk = (f"{regn}:{kk}",)
            hid, lod = phi(regn, kk), plo(regn, kk)
            kb.op("act", lambda e, g32=g32, nchunk=nchunk: e.activation(
                g32, bank[:, 0:512], AF.Gelu,
                bias=A["b1mt"][:, nchunk:nchunk + 1]),
                reads=(bkey, "b1mt"), pre=(gk,) + scr_pre(gk), writes=(gk,))
            kb.op("act", lambda e, g32=g32, hid=hid: e.activation(
                hid, g32, AF.Copy),
                reads=(gk,), pre=dk, writes=dk)
            kb.op("dve", lambda e, g32=g32, hid=hid, lod=lod:
                  e.scalar_tensor_tensor(lod, g32, 1.0, hid,
                                         ALU.mult, ALU.subtract),
                  reads=(gk,) + dk, writes=dk)
        modeA3([(wv["W1_hi"][:, :, j * 512:(j + 1) * 512],
                 wv["W1_lo"][:, :, j * 512:(j + 1) * 512])
                for j in range(8)], "R2", 32, evac_w1)
        tap(f"gela{g}", A["R3"], 4096, rkeys("R3"))
        tap(f"gelb{g}", A["R4"], 4096, rkeys("R4"))
        tap(f"gelc{g}", A["R9"], 4096, rkeys("R9"))
        tap(f"geld{g}", A["R1"], 4096, rkeys("R1"))

        for nt in range(2):
            banks = [(PS[i], f"P{i}") for i in range(4)]
            for kg in range(4):
                bh, bl, rk = load_piece3(
                    wv["W2_hi"][kg][:, :, nt * 512:(nt + 1) * 512],
                    wv["W2_lo"][kg][:, :, nt * 512:(nt + 1) * 512])
                for k in range(KC):
                    kk = kg * 8 + k
                    regn = gel_reg[kk // 8]
                    for ts in range(NCH):
                        bank, bkey = banks[ts]
                        xh = phi(regn, kk % 8)[:, ts * 128:(ts + 1) * 128]
                        xl = plo(regn, kk % 8)[:, ts * 128:(ts + 1) * 128]
                        mh, ml = bh[:, k, :], bl[:, k, :]
                        ik = (rk, f"{regn}:{kk % 8}")
                        first = (kg == 0 and k == 0)
                        kb.op("pe", lambda e, xh=xh, mh=mh, bank=bank,
                              first=first: e.matmul(bank, xh, mh,
                                                    start=first, stop=False),
                              reads=ik, pre=(bkey,) if first else (),
                              sig=False)
                        kb.op("pe", lambda e, xh=xh, ml=ml, bank=bank:
                              e.matmul(bank, xh, ml, start=False, stop=False),
                              reads=ik, sig=False)
                        kb.op("pe", lambda e, xl=xl, mh=mh, bank=bank:
                              e.matmul(bank, xl, mh, start=False, stop=False),
                              reads=ik,
                              sig=(k == KC - 1 and ts == NCH - 1))
            for ts in range(NCH):
                bank, bkey = banks[ts]
                kb.op("pe", lambda e, bank=bank, nt=nt: e.matmul(
                    bank, A["allones"][0:1, 0:128],
                    A["brows"][0:1, 1024 + nt * 512:1024 + (nt + 1) * 512],
                    start=False, stop=False),
                    reads=("allones", "brows3"), sig=False)
                mvx1 = A["R10"][:, ts * D + nt * 512: ts * D + (nt + 1) * 512]
                kb.op("pe", lambda e, bank=bank, mv=mvx1: e.matmul(
                    bank, A["cid"][:, 0:128], mv, start=False, stop=False),
                    reads=("cid", f"R10:{2 * ts + nt}"), sig=False)
                mvx = A["R0"][:, ts * D + nt * 512: ts * D + (nt + 1) * 512]
                kb.op("pe", lambda e, bank=bank, mv=mvx: e.matmul(
                    bank, A["cid"][:, 0:128], mv, start=False, stop=True),
                    reads=("cid", "R0"), writes=(bkey,))
                dst = A["R11"][:, ts * D + nt * 512: ts * D + (nt + 1) * 512]
                kb.op("act", lambda e, d=dst, bank=bank: e.activation(
                    d, bank, AF.Copy),
                    reads=(bkey,), pre=(f"R11:{2 * ts + nt}",),
                    writes=(f"R11:{2 * ts + nt}",))
        tap(f"t{g}", A["R11"], 4096, rkeys("R11"))
        if g + 1 < NG:
            x_load(g + 1)

        ln_stats(rkeys("R11"), A["R11"])
        ln_norm(rkeys("R11"), A["R11"], "R1", A["R1"])
        transpose_slab(rkeys("R1"), A["R1"], "R2")

        def evac_re(ts, nt, bank, bkey):
            dst = A["R10"][:, ts * D + nt * 512: ts * D + (nt + 1) * 512]
            kb.op("act", lambda e, d=dst, bank=bank: e.activation(
                d, bank, AF.Square),
                reads=(bkey,), pre=(f"R10:{2 * ts + nt}",),
                writes=(f"R10:{2 * ts + nt}",))
        def evac_im(ts, nt, bank, bkey):
            dst = A["R9"][:, ts * D + nt * 512: ts * D + (nt + 1) * 512]
            kb.op("act", lambda e, d=dst, bank=bank: e.activation(
                d, bank, AF.Square),
                reads=(bkey,), pre=(f"R9:{2 * ts + nt}",),
                writes=(f"R9:{2 * ts + nt}",))
        modeB3([(wv["csc_hi"][:, :, nt * 512:(nt + 1) * 512],
                 wv["csc_lo"][:, :, nt * 512:(nt + 1) * 512])
                for nt in range(2)], "R2", evac_re,
               wextra=("csc_hi", "csc_lo"))
        modeB3([(wv["ssc_hi"][:, :, nt * 512:(nt + 1) * 512],
                 wv["ssc_lo"][:, :, nt * 512:(nt + 1) * 512])
                for nt in range(2)], "R2", evac_im,
               wextra=("ssc_hi", "ssc_lo"))
        kb.op("dve", lambda e: e.tensor_add(A["R3"][:, 0:4096], A["R10"],
                                            A["R9"]),
              reads=rkeys("R10") + rkeys("R9"), pre=rkeys("R3"),
              writes=rkeys("R3"))
        kb.op("act", lambda e: e.activation(A["R10"], A["R3"][:, 0:4096],
                                            AF.Sqrt),
              reads=rkeys("R3"), pre=rkeys("R10"), writes=rkeys("R10"))
        kb.op("dve", lambda e: e.tensor_add(A["R9"], A["R1"], A["R10"]),
              reads=rkeys("R10") + rkeys("R1"), pre=rkeys("R9"),
              writes=rkeys("R9"))
        ln_stats(rkeys("R9"), A["R9"])
        ln_norm(rkeys("R9"), A["R9"], "R11", A["R11"])
        src = A["R11"].rearrange("p (c d) -> p c d", d=D)
        kb.op("gp", lambda e, g=g: e.dma_start(out=outr[g], in_=src),
              reads=rkeys("R11"))

    x_load(0)
    for g in range(NG):
        emit_group(g)


    with ExitStack() as sst:
        sems = {n: sst.enter_context(nc.semaphore(f"sem_{n}")) for n in KB.ALL}

        def make_body(eng_name):
            oplist = kb.ops[eng_name]
            mysem = sems[eng_name]
            mymul = 16 if eng_name in KB.DMAQ else 1

            def body(e):
                for fn, waits, selfwaits, sig, sigchan in oplist:
                    for p, v in waits:
                        e.wait_ge(sems[p], v * (16 if p in KB.DMAQ else 1))
                    for v in selfwaits:
                        e.wait_ge(mysem, v * mymul)
                    inst = fn(e)
                    if sig:
                        inst.then_inc(sems[sigchan],
                                      16 if sigchan in KB.DMAQ else 1)
                if eng_name == "gp":
                    for ch in KB.CHANS:
                        if kb.cnt[ch]:
                            e.wait_ge(sems[ch], kb.cnt[ch] * 16)
            return body

        with nc.Block() as blk:
            blk.tensor(make_body("pe"))
            blk.vector(make_body("dve"))
            blk.scalar(make_body("act"))
            blk.sync(make_body("sp"))
            blk.gpsimd(make_body("gp"))
    st.close()
    return nc


def make_const_inputs():
    cid = np.zeros((128, 192), np.float32)
    cid[:, 0:128] = np.eye(128, dtype=np.float32)
    cid[64:128, 128:192] = np.eye(64, dtype=np.float32)
    triu = np.triu(np.ones((128, 128), np.float32))
    r = np.arange(128)[:, None]
    s = np.arange(256)[None, :]
    band = (s >= r + 1) & (s <= r + 128)
    first = band & (s >= 128)
    mb = np.where(band, 0.0, -1e30).astype(np.float32)
    mf = np.where(first, 0.0, -1e30).astype(np.float32)
    return cid, triu, mb, mf


def shard_inputs(inputs, b, L):
    cid, triu, mb, mf = make_const_inputs()
    g = lambda n: np.ascontiguousarray(np.asarray(inputs[n], dtype=np.float32))
    bq_a, bq_s = g("L0_attn_bqkv_a"), g("L0_attn_bqkv_s")
    brows = np.zeros((4, 1024), np.float32)
    brows[0] = bq_a[2048:3072]
    brows[1] = bq_s[2048:3072]
    brows[2] = g("L0_attn_bo")
    brows[3] = g("L0_attn_b2m")
    return {
        "x": np.ascontiguousarray(np.asarray(inputs["L0_attn_x"],
                                             dtype=np.float32)[b, :L]),
        "Wqkv_a": g("L0_attn_Wqkv_a"), "Wp_a": g("L0_attn_Wp_a"),
        "Wqkv_s": g("L0_attn_Wqkv_s"), "Wo_s": g("L0_attn_Wo_s"),
        "Wo": g("L0_attn_Wo"), "W1": g("L0_attn_W1"), "W2": g("L0_attn_W2"),
        "th": g("L0_ffn_theta"), "thn": g("L0_ffn_theta_noise"),
        "ph": g("L0_ffn_phi"), "phn": g("L0_ffn_phi_noise"),
        "cid": cid, "triu": triu, "mb": mb, "mf": mf,
        "cid16": cid[:, 0:128].astype(ml_dtypes.bfloat16),
        "mb16": mb.astype(ml_dtypes.bfloat16),
        "mf16": mf.astype(ml_dtypes.bfloat16),
        "bqk_a": np.ascontiguousarray(bq_a[0:2048].reshape(16, 128).T),
        "bqk_s": np.ascontiguousarray(bq_s[0:2048].reshape(16, 128).T),
        "bp_a": np.ascontiguousarray(g("L0_attn_bp_a").reshape(8, 128).T),
        "bo_s": np.ascontiguousarray(g("L0_attn_bo_s").reshape(8, 128).T),
        "b1mt": np.ascontiguousarray(g("L0_attn_b1m").reshape(32, 128).T),
        "g1t": np.ascontiguousarray(g("L0_attn_g1").reshape(8, 128).T),
        "b1t": np.ascontiguousarray(g("L0_attn_b1").reshape(8, 128).T),
        "gmt": np.ascontiguousarray(g("L0_attn_gm").reshape(8, 128).T),
        "bmt": np.ascontiguousarray(g("L0_attn_bm").reshape(8, 128).T),
        "brows": brows,
    }


def kernel(**inputs):
    L = 4096
    nc = build_nc(L)
    in_maps = [shard_inputs(inputs, b, L) for b in range(8)]
    res = run_bass_kernel_spmd(nc, in_maps, core_ids=list(range(8)))
    out = np.stack([np.asarray(r["out"]).reshape(L, D) for r in res.results])
    return out.astype(np.float32)
